# version 06
# baseline (speedup 1.0000x reference)
import numpy as np
import ml_dtypes
import concourse.bass as bass
import concourse.mybir as mybir
from concourse.bass_utils import run_bass_kernel_spmd

F32 = mybir.dt.float32
BF16 = mybir.dt.bfloat16
I16 = mybir.dt.int16
I32 = mybir.dt.int32
ALU = mybir.AluOpType
AF = mybir.ActivationFunctionType
AX = mybir.AxisListType
BF16NP = ml_dtypes.bfloat16

D = 2048
L = 2048
B = 4
NCORE = 8
OWN = 1024
PRE = 2048
NFC = 16
KT_LISTS = [[0, 1, 2, 3, 8, 9, 10, 11], list(range(16))]
MASKED = [[0, 1, 2, 3, 8, 9, 10, 11], [4, 5, 6, 7, 12, 13, 14, 15]]
MLP = 16384
NBK = 32
UCH = 4
GRP = 2
EXPC = 14.0
ROPE_BASE = 10000.0

ENGINES = ("sync", "gpsimd", "vector", "scalar", "tensor")


class Builder:
    def __init__(self):
        self.ops = []
        self.cnt = dict.fromkeys(ENGINES, 0)
        self.stores = {}

    def _store(self, sp):
        return self.stores.setdefault(sp, {"w": [], "r": []})

    def op(self, eng, fn, r=(), w=(), dma=False):
        inc = 16 if dma else 1
        deps = {}

        def dep(e, c):
            if e == eng and eng in ("tensor", "sync"):
                return
            if deps.get(e, 0) < c:
                deps[e] = c

        for (sp, lo, hi) in r:
            st = self._store(sp)
            for (wlo, whi, we, wc) in st["w"]:
                if wlo < hi and lo < whi:
                    dep(we, wc)
            if sp == "ps" and eng in ("scalar", "vector"):
                other = "vector" if eng == "scalar" else "scalar"
                for (rlo, rhi, dd) in st["r"]:
                    if rlo < hi and lo < rhi and other in dd:
                        dep(other, dd[other])
        for (sp, lo, hi) in w:
            st = self._store(sp)
            for (wlo, whi, we, wc) in st["w"]:
                if wlo < hi and lo < whi:
                    dep(we, wc)
            for (rlo, rhi, dd) in st["r"]:
                if rlo < hi and lo < rhi:
                    for e, c in dd.items():
                        dep(e, c)

        self.cnt[eng] += inc
        c_after = self.cnt[eng]

        for (sp, lo, hi) in w:
            st = self._store(sp)
            neww = []
            for (wlo, whi, we, wc) in st["w"]:
                if wlo < hi and lo < whi:
                    if wlo < lo:
                        neww.append((wlo, lo, we, wc))
                    if hi < whi:
                        neww.append((hi, whi, we, wc))
                else:
                    neww.append((wlo, whi, we, wc))
            neww.append((lo, hi, eng, c_after))
            st["w"] = neww
            newr = []
            for (rlo, rhi, dd) in st["r"]:
                if rlo < hi and lo < rhi:
                    if rlo < lo:
                        newr.append((rlo, lo, dict(dd)))
                    if hi < rhi:
                        newr.append((hi, rhi, dict(dd)))
                else:
                    newr.append((rlo, rhi, dd))
            st["r"] = newr
        for (sp, lo, hi) in r:
            st = self._store(sp)
            for ent in st["r"]:
                if ent[0] == lo and ent[1] == hi:
                    ent[2][eng] = c_after
                    break
            else:
                st["r"].append((lo, hi, {eng: c_after}))

        self.ops.append((eng, fn, deps, inc))
        return c_after

    def check(self):
        qs = {e: [o for o in self.ops if o[0] == e] for e in ENGINES}
        idx = dict.fromkeys(ENGINES, 0)
        done = dict.fromkeys(ENGINES, 0)
        progress = True
        while progress:
            progress = False
            for e in ENGINES:
                while idx[e] < len(qs[e]):
                    _, _, deps, inc = qs[e][idx[e]]
                    if all(done[de] >= dc for de, dc in deps.items()):
                        done[e] += inc
                        idx[e] += 1
                        progress = True
                    else:
                        break
        stuck = {e: (idx[e], len(qs[e])) for e in ENGINES if idx[e] < len(qs[e])}
        if stuck:
            for e, (i, n) in stuck.items():
                print("STUCK", e, "at", i, "of", n, "deps", qs[e][i][2])
            raise RuntimeError(f"build-time deadlock: {stuck}")
        return done

    def emit(self, nc, sems, tail):
        hw = {e: dict.fromkeys(ENGINES, 0) for e in ENGINES}
        with nc.Block() as block:
            for eng in ENGINES:
                def emit_one(h, eng=eng):
                    for (e, fn, deps, inc) in self.ops:
                        if e != eng:
                            continue
                        for de in ENGINES:
                            dc = deps.get(de, 0)
                            if dc > hw[eng][de]:
                                h.wait_ge(sems[de], dc)
                                hw[eng][de] = dc
                        inst = fn(h)
                        inst.then_inc(sems[eng], inc)
                    if eng in tail:
                        h.wait_ge(sems[eng], self.cnt[eng])
                getattr(block, eng)(emit_one)


def build_program(limit=99, dbg=None, dbg_s=False, sub=None):
    nc = bass.Bass("TRN2", target_bir_lowering=False, debug=False)

    def din(name, shape, dtype):
        return nc.dram_tensor(name, shape, dtype, kind="ExternalInput").ap()

    x_pref = din("x_pref", [PRE, D], F32)
    wq = din("wq", [D, D], F32)
    wk = din("wk", [D, 512], F32)
    wv = din("wv", [D, 512], F32)
    wo = din("wo", [D, D], F32)
    gpk = din("gpk", [1024, MLP], I16)
    upk = din("upk", [1024, MLP], I16)
    dpk = din("dpk", [MLP // 2, D], I16)
    gsc = din("gsc", [8, 64 * 2048], F32)
    usc = din("usc", [8, 64 * 2048], F32)
    dsc = din("dsc", [8, 64 * 2048], F32)
    cosk = din("cosk", [128, PRE], F32)
    sink = din("sink", [128, PRE], F32)
    cosq = din("cosq", [128, OWN], F32)
    sinq = din("sinq", [128, OWN], F32)
    masks_in = din("masks", [16 * 128, 512], BF16)
    ident_in = din("ident", [128, 128], BF16)
    swap_in = din("swapm", [128, 128], BF16)

    out = nc.dram_tensor("out", [OWN, D], F32, kind="ExternalOutput").ap()
    s_dram = nc.dram_tensor("s_spill", [OWN, D], BF16).ap()

    GA_EL = 52416
    ga_cm = nc.sbuf_tensor("GA", [128, GA_EL], F32)
    ga = ga_cm.__enter__().ap()
    gab = ga.bitcast(BF16)
    gai = ga.bitcast(I16)
    LIMIT = GA_EL * 4

    def fsl(off, nel, r0=0, r1=128):
        assert off % 4 == 0 and off + 4 * nel <= LIMIT, (off, nel)
        return ga[r0:r1, off // 4: off // 4 + nel]

    def bsl(off, nel, r0=0, r1=128):
        assert off % 4 == 0 and off + 2 * nel <= LIMIT, (off, nel)
        return gab[r0:r1, off // 2: off // 2 + nel]

    def isl(off, nel, r0=0, r1=128):
        assert off % 4 == 0 and off + 2 * nel <= LIMIT, (off, nel)
        return gai[r0:r1, off // 2: off // 2 + nel]

    O_XNT_OWN = 0
    O_CONST = 32768
    O_P = 33280
    P_SZ = LIMIT - O_P
    assert O_P + P_SZ <= LIMIT

    XNT_OWN = bsl(O_XNT_OWN, NFC * 1024)
    c_expb = fsl(O_CONST, 1)
    c_exps = fsl(O_CONST + 16, 1)
    c_eps = fsl(O_CONST + 32, 1)
    c_one64 = fsl(O_CONST + 48, 64, 0, 1)
    c_smask = fsl(O_CONST + 304, 1).bitcast(I32)
    c_seven = fsl(O_CONST + 320, 1).bitcast(I32)
    c_half = fsl(O_CONST + 336, 1)
    c_neg2 = fsl(O_CONST + 352, 1)
    c_neg6 = fsl(O_CONST + 368, 1)
    c_one1 = fsl(O_CONST + 384, 1)

    _o = [O_P]

    def take(n):
        r = _o[0]
        _o[0] += n
        assert _o[0] <= O_P + P_SZ, (_o[0], O_P + P_SZ)
        return r

    O_XNT_PEER = take(32768)
    O_COSK = take(8192)
    O_SINK = take(8192)
    O_COSQ = take(4096)
    O_SINQ = take(4096)
    O_MASK = take(8192)
    O_KT = take(16384)
    O_VO = take(16640)
    O_QT = take(16384)
    O_AOT = take(16384)
    O_WBUF = take(8192)
    O_XLN = take(8192)
    O_XNB = take(4096)
    O_RTMP = take(6144)
    O_PB = take(2048)
    O_REC = take(4096)
    O_STAT = take(128)
    O_SEV = take(1024)
    O_IDSW = take(1024)
    O_WBUF2 = take(8192)
    P123_END = _o[0]
    assert P123_END <= O_P + P_SZ

    XNT_PEER = bsl(O_XNT_PEER, NFC * 1024)
    COSK = fsl(O_COSK, 2048)
    SINK = fsl(O_SINK, 2048)
    COSQ = fsl(O_COSQ, 1024)
    SINQ = fsl(O_SINQ, 1024)
    MASKS_SB = bsl(O_MASK, 8 * 512)
    KT = bsl(O_KT, 4 * 2048)
    VO = bsl(O_VO, 16 * 8 * 65)
    QT = bsl(O_QT, NFC * 512)
    AOT = bsl(O_AOT, NFC * 512)
    WBUFS = [bsl(O_WBUF, 8 * 512), bsl(O_WBUF2, 8 * 512)]
    O_WBUFS = [O_WBUF, O_WBUF2]
    WBUF = WBUFS[0]
    WVBUF = bsl(O_QT, NFC * 512)
    XLNS = [fsl(O_XLN, 2048), fsl(O_WBUF2, 2048)]
    O_XLNS = [O_XLN, O_WBUF2]
    XLN = XLNS[0]
    XNBS = [bsl(O_XNB, 2048), bsl(O_RTMP, 2048)]
    O_XNBS = [O_XNB, O_RTMP]
    XNB = XNBS[0]
    KRAW = fsl(O_RTMP, 512)
    RT0 = fsl(O_RTMP + 2048, 512)
    RT1 = fsl(O_RTMP + 4096, 512)
    PB = [bsl(O_PB + i * 1024, 512) for i in range(2)] + \
         [bsl(O_XLN + i * 1024, 512) for i in range(2)]
    O_PBS = [O_PB, O_PB + 1024, O_XLN, O_XLN + 1024]
    RECROWS = [fsl(O_REC, 512, 0, 1), fsl(O_RTMP, 512, 0, 1)]
    RECBCS = [fsl(O_REC + 2048, 512), fsl(O_RTMP + 2048, 512)]
    O_RECS = [O_REC, O_RTMP]
    ST_SUM = fsl(O_STAT, 1)
    ST_SQ = fsl(O_STAT + 16, 1)
    ST_MEAN = fsl(O_STAT + 32, 1)
    ST_VAR = fsl(O_STAT + 48, 1)
    ST_SD = fsl(O_STAT + 64, 1)
    ST_RSTD = fsl(O_STAT + 80, 1)
    SEV = bsl(O_SEV, 512)
    IDENT_SB = bsl(O_IDSW, 128)
    SWAP_SB = bsl(O_IDSW + 256, 128)

    _o[0] = O_P
    O_WG = take(16384)
    O_WU = take(16384)
    O_WD = take(32768)
    O_INT = take(16384)
    O_PK = take(8192)
    O_SC = take(8192)
    O_DQ = take(12288)
    O_FACC = take(65536)
    P4_END = _o[0]
    assert P4_END <= O_P + P_SZ

    WG = bsl(O_WG, NFC * 512)
    WU = bsl(O_WU, NFC * 512)
    WD = [bsl(O_WD + g * 16384, UCH * 2048) for g in range(GRP)]
    INTER = [bsl(O_INT + g * 8192, UCH * 1024) for g in range(GRP)]
    PK = [isl(O_PK + i * 4096, 2048) for i in range(2)]
    SC = [bsl(O_SC + i * 4096, 2048) for i in range(2)]
    DQ_M = isl(O_DQ, 2048)
    DQ_T1 = isl(O_DQ + 4096, 2048)
    DQ_T2 = isl(O_DQ + 8192, 2048)

    def facc(tc):
        return fsl(O_FACC + tc * 8192, 2048)

    nc.dbg_offsets = dict(
        XNT_OWN=O_XNT_OWN, XNT_PEER=O_XNT_PEER, KT=O_KT, VO=O_VO, QT=O_QT,
        AOT=O_AOT, FACC=O_FACC, COSK=O_COSK, SINK=O_SINK, MASK=O_MASK)

    SIL = fsl(O_WG, 2048)
    XOUT = fsl(O_WG + 8192, 2048)
    SBF5 = bsl(O_WU, 2048)
    XIN5 = fsl(O_WU + 4096, 2048)

    ps_cm = nc.psum_tensor("PS", [128, 4096], F32)
    psf = ps_cm.__enter__().ap()
    psb = psf.bitcast(BF16)

    def bank(i, n=1):
        return psf[:, 512 * i: 512 * (i + n)]

    def bankb128(i):
        return psb[:, 1024 * i: 1024 * i + 128]

    def R(off, nbytes):
        return ("sb", off, off + nbytes)

    def RP(i, n=1):
        return ("ps", 2048 * i, 2048 * (i + n))

    bld = Builder()
    op = bld.op

    def dma_sp(dst, src, r=(), w=()):
        return op("sync", lambda h, d=dst, s=src: h.dma_start(out=d, in_=s),
                  r=r, w=w, dma=True)

    def dma_gp(dst, src, r=(), w=()):
        return op("gpsimd", lambda h, d=dst, s=src: h.dma_start(out=d, in_=s),
                  r=r, w=w, dma=True)

    op("vector", lambda h: h.memset(c_expb, -EXPC), w=[R(O_CONST, 16)])
    op("vector", lambda h: h.memset(c_exps, 0.125), w=[R(O_CONST + 16, 16)])
    op("vector", lambda h: h.memset(c_eps, 1e-5), w=[R(O_CONST + 32, 16)])
    op("vector", lambda h: h.memset(c_one64, 1.0), w=[R(O_CONST + 48, 256)])
    op("vector", lambda h: h.memset(c_smask, -32768), w=[R(O_CONST + 304, 16)])
    op("vector", lambda h: h.memset(c_seven, 7), w=[R(O_CONST + 320, 16)])
    op("vector", lambda h: h.memset(c_half, 0.5), w=[R(O_CONST + 336, 16)])
    op("vector", lambda h: h.memset(c_neg2, -2.0), w=[R(O_CONST + 352, 16)])
    op("vector", lambda h: h.memset(c_neg6, -6.0), w=[R(O_CONST + 368, 16)])
    op("vector", lambda h: h.memset(c_one1, 1.0), w=[R(O_CONST + 384, 16)])
    op("vector", lambda h: h.memset(
        gab[0:128, O_VO // 2 + 64: O_VO // 2 + 16 * 8 * 65: 65], 1.0),
       w=[R(O_VO, 16 * 8 * 65 * 2)])

    dma_sp(IDENT_SB, ident_in, w=[R(O_IDSW, 256)])
    dma_sp(SWAP_SB, swap_in, w=[R(O_IDSW + 256, 256)])
    dma_gp(COSK, cosk, w=[R(O_COSK, 8192)])
    dma_gp(SINK, sink, w=[R(O_SINK, 8192)])
    dma_gp(COSQ, cosq, w=[R(O_COSQ, 4096)])
    dma_gp(SINQ, sinq, w=[R(O_SINQ, 4096)])

    def _truncate():
        for i, (off, nbytes) in enumerate(dbg or []):
            dt = nc.dram_tensor("dbg%d" % i, [128, nbytes // 4], F32,
                                kind="ExternalOutput").ap()
            dma_sp(dt, fsl(off, nbytes // 4), r=[R(off, nbytes)])
        if dbg_s:
            ds = nc.dram_tensor("dbgs", [OWN, D], BF16,
                                kind="ExternalOutput").ap()
            dma_gp(ds, s_dram, r=[("sdram", 0, 1)])
        dma_gp(out, x_pref[1024:2048, :])
        bld.check()
        _emit(nc, bld)
        return nc

    if limit <= 0:
        return _truncate()

    INV_D = 1.0 / D
    for ti in range(PRE // 128):
        bi = ti % 2
        xln, r_xln = XLNS[bi], R(O_XLNS[bi], 8192)
        xnb, r_xnb = XNBS[bi], R(O_XNBS[bi], 4096)
        dma_sp(xln, x_pref[ti * 128:(ti + 1) * 128, :], w=[r_xln])
        op("vector", lambda h, x_=xln: h.reduce_sum(ST_SUM, x_, axis=AX.X),
           r=[r_xln], w=[R(O_STAT, 16)])
        op("scalar", lambda h, x_=xln, n_=xnb: h.activation(
            n_, x_, AF.Square, accum_out=ST_SQ),
           r=[r_xln], w=[r_xnb, R(O_STAT + 16, 16)])
        op("vector", lambda h: h.tensor_scalar(ST_MEAN, ST_SUM, INV_D, None, ALU.mult),
           r=[R(O_STAT, 16)], w=[R(O_STAT + 32, 16)])
        op("vector", lambda h: h.tensor_scalar(ST_VAR, ST_SQ, INV_D, None, ALU.mult),
           r=[R(O_STAT + 16, 16)], w=[R(O_STAT + 48, 16)])
        op("vector", lambda h: h.tensor_tensor(ST_SD, ST_MEAN, ST_MEAN, ALU.mult),
           r=[R(O_STAT + 32, 16)], w=[R(O_STAT + 64, 16)])
        op("vector", lambda h: h.tensor_sub(ST_VAR, ST_VAR, ST_SD),
           r=[R(O_STAT + 48, 32)], w=[R(O_STAT + 48, 16)])
        op("vector", lambda h: h.tensor_tensor(ST_VAR, ST_VAR, c_eps, ALU.add),
           r=[R(O_STAT + 48, 16), R(O_CONST + 32, 16)], w=[R(O_STAT + 48, 16)])
        op("scalar", lambda h: h.activation(ST_SD, ST_VAR, AF.Sqrt),
           r=[R(O_STAT + 48, 16)], w=[R(O_STAT + 64, 16)])
        op("vector", lambda h: h.reciprocal(ST_RSTD, ST_SD),
           r=[R(O_STAT + 64, 16)], w=[R(O_STAT + 80, 16)])
        op("vector", lambda h, x_=xln, n_=xnb: h.tensor_scalar(
            n_, x_, ST_MEAN, ST_RSTD, ALU.subtract, ALU.mult),
           r=[r_xln, R(O_STAT + 32, 16), R(O_STAT + 80, 16)], w=[r_xnb])
        dst_off = O_XNT_PEER if ti < 8 else O_XNT_OWN
        dst = XNT_PEER if ti < 8 else XNT_OWN
        col = (ti % 8) * 128
        for fc in range(NFC):
            if fc < 8:
                src = xnb.rearrange("p (j two) -> p two j", two=2)[:, 0, 128 * fc: 128 * fc + 128]
            else:
                src = xnb.rearrange("p (j two) -> p two j", two=2)[:, 1, 128 * (fc - 8): 128 * (fc - 8) + 128]
            pdst = bankb128(fc % 2)
            op("tensor", lambda h, s=src, p=pdst: h.transpose(p, s, IDENT_SB),
               r=[r_xnb, R(O_IDSW, 256)], w=[RP(fc % 2)])
            op("vector",
               lambda h, p=pdst, dd=dst, c=col, f=fc: h.tensor_copy(
                   dd[:, 1024 * f + c: 1024 * f + c + 128], p),
               r=[RP(fc % 2)],
               w=[R(dst_off + (1024 * fc + col) * 2, 256)])

    if limit <= 1:
        return _truncate()

    def xnt_rhs(fc, t0, width):
        if t0 + width <= 1024:
            return XNT_PEER[:, 1024 * fc + t0: 1024 * fc + t0 + width]
        assert t0 >= 1024
        return XNT_OWN[:, 1024 * fc + (t0 - 1024): 1024 * fc + (t0 - 1024) + width]

    def xnt_r(fc, t0, width):
        if t0 + width <= 1024:
            return R(O_XNT_PEER + (1024 * fc + t0) * 2, width * 2)
        return R(O_XNT_OWN + (1024 * fc + (t0 - 1024)) * 2, width * 2)

    R_XNT_ALL = [R(O_XNT_PEER, 32768), R(O_XNT_OWN, 32768)]

    R_WBUF = R(O_WBUF, 8192)
    R_XNB = R(O_XNB, 4096)
    R_KRAW = R(O_RTMP, 2048)
    R_RT0 = R(O_RTMP + 2048, 2048)
    R_RT1 = R(O_RTMP + 4096, 2048)

    def rope_store(pbank_i, cos_t, cos_off, sin_t, sin_off, tsz, col0, width,
                   dst, dst_r):
        pk = bank(pbank_i)[:, 0:width]
        if sub != "krope2b":
            op("scalar", lambda h: h.activation(KRAW[:, 0:width], pk, AF.Copy),
               r=[RP(pbank_i)], w=[R_KRAW])
        if sub == "krope1":
            op("vector", lambda h: h.tensor_copy(dst, KRAW[:, 0:width]),
               r=[R_KRAW], w=[dst_r])
            return
        op("vector", lambda h: h.tensor_copy(XNB[:, 0:width], pk),
           r=[RP(pbank_i)], w=[R_XNB])
        swm = IDENT_SB if sub == "krope2c" else SWAP_SB
        obank = 3 if sub == "krope2d" else 7
        swb = bank(obank)[:, 0:width]
        op("tensor", lambda h: h.matmul(swb, swm, XNB[:, 0:width],
                                        start=True, stop=True),
           r=[R_XNB, R(O_IDSW, 512)], w=[RP(obank)])
        if sub in ("krope2", "krope2b", "krope2c", "krope2d"):
            op("vector", lambda h: h.tensor_copy(dst, swb),
               r=[RP(obank)], w=[dst_r])
            return
        op("vector", lambda h: h.tensor_tensor(
            RT0[:, 0:width], KRAW[:, 0:width],
            cos_t[:, col0:col0 + width], ALU.mult),
           r=[R_KRAW, R(cos_off, tsz)], w=[R_RT0])
        op("vector", lambda h: h.tensor_tensor(
            RT1[:, 0:width], swb, sin_t[:, col0:col0 + width], ALU.mult),
           r=[RP(7), R(sin_off, tsz)], w=[R_RT1])
        op("vector", lambda h: h.tensor_add(dst, RT0[:, 0:width], RT1[:, 0:width]),
           r=[R_RT0, R_RT1], w=[dst_r])

    ndc, nt4 = (1, 1) if (sub or "").endswith("a") else (4, 4)
    if (sub or "").endswith("a"):
        sub = sub[:-1]
    if sub != "vonly":
        for dc in range(ndc):
            wb = dc % 2
            dma_gp(WBUFS[wb][:, 0:16 * 128],
                   wk[:, dc * 128:(dc + 1) * 128].rearrange("(c p) d -> p c d", p=128),
                   w=[R(O_WBUFS[wb], 8192)])
            for t4 in range(nt4):
                pb = 2 + (dc * 4 + t4) % 2
                for fc in range(NFC):
                    op("tensor", lambda h, f=fc, tt=t4 * 512, b=pb, w_=wb: h.matmul(
                        bank(b)[:, 0:512], WBUFS[w_][:, f * 128:(f + 1) * 128],
                        xnt_rhs(f, tt, 512),
                        start=(f == 0), stop=(f == NFC - 1)),
                       r=[R(O_WBUFS[wb], 8192), xnt_r(fc, t4 * 512, 512)],
                       w=[RP(pb)])
                if sub == "kproj":
                    op("vector", lambda h, dc_=dc, t_=t4, b=pb: h.tensor_copy(
                        KT[:, dc_ * 2048 + t_ * 512: dc_ * 2048 + t_ * 512 + 512],
                        bank(b)[:, 0:512]),
                       r=[RP(pb)],
                       w=[R(O_KT + (dc * 2048 + t4 * 512) * 2, 1024)])
                else:
                    rope_store(pb, COSK, O_COSK, SINK, O_SINK, 8192, t4 * 512, 512,
                               KT[:, dc * 2048 + t4 * 512: dc * 2048 + t4 * 512 + 512],
                               R(O_KT + (dc * 2048 + t4 * 512) * 2, 1024))

    R_WVBUF = R(O_QT, 16384)
    if sub not in ("kproj", "krope"):
        dma_gp(WVBUF, wv.rearrange("(c p) d -> p c d", p=128), w=[R_WVBUF])
        for tcv in range(16):
            vb = 2 + tcv % 2
            for fc in range(NFC):
                op("tensor", lambda h, f=fc, t=tcv, b=vb: h.matmul(
                    bank(b)[:, 0:512], xnt_rhs(f, t * 128, 128),
                    WVBUF[:, f * 512:(f + 1) * 512],
                    start=(f == 0), stop=(f == NFC - 1)),
                   r=[R_WVBUF, xnt_r(fc, tcv * 128, 128)], w=[RP(vb)])
            for g in range(8):
                op("scalar", lambda h, t=tcv, gg=g, b=vb: h.activation(
                    VO[:, (t * 8 + gg) * 65: (t * 8 + gg) * 65 + 64],
                    bank(b)[:, gg * 64: gg * 64 + 64], AF.Copy),
                   r=[RP(vb)], w=[R(O_VO + (tcv * 8 + g) * 65 * 2, 128)])

    if limit <= 2:
        return _truncate()

    R_QT = R(O_QT, 16384)
    R_AOT = R(O_AOT, 16384)
    R_MASK = R(O_MASK, 8192)
    R_REC = R(O_REC, 2048)
    R_RECBC = R(O_REC + 2048, 2048)
    R_SEV = R(O_SEV, 1024)

    for slot in range(2):
        q0 = slot * 512
        nkt = len(KT_LISTS[slot])
        mask_of = {kt: mi for mi, kt in enumerate(MASKED[slot])}
        dma_sp(MASKS_SB,
               masks_in.rearrange("(m p) n -> p m n", p=128)
               [:, slot * 8:(slot + 1) * 8, :],
               w=[R_MASK])
        for s16 in range(NFC):
            wb = s16 % 2
            pb = 2 + s16 % 2
            dma_gp(WBUFS[wb][:, 0:16 * 128],
                   wq[:, s16 * 128:(s16 + 1) * 128].rearrange("(c p) d -> p c d", p=128),
                   w=[R(O_WBUFS[wb], 8192)])
            for fc in range(NFC):
                op("tensor", lambda h, f=fc, qq=q0, b=pb, w_=wb: h.matmul(
                    bank(b)[:, 0:512], WBUFS[w_][:, f * 128:(f + 1) * 128],
                    XNT_OWN[:, 1024 * f + qq: 1024 * f + qq + 512],
                    start=(f == 0), stop=(f == NFC - 1)),
                   r=[R(O_WBUFS[wb], 8192), xnt_r(fc, 1024 + q0, 512)],
                   w=[RP(pb)])
            rope_store(pb, COSQ, O_COSQ, SINQ, O_SINQ, 4096, q0, 512,
                       QT[:, s16 * 512:(s16 + 1) * 512],
                       R(O_QT + s16 * 512 * 2, 1024))

        for s16 in range(NFC):
            for odd in range(2):
                H = (s16 // 4) * 8 + s16 % 4 + 4 * odd
                g = H // 4
                rb = 64 * (g % 2)
                assert rb == 64 * odd
                avb = 4 + odd
                qsl = QT[rb: rb + 64, s16 * 512:(s16 + 1) * 512]
                r_qsl = R(O_QT + s16 * 1024, 1024)
                for kt_i, kt in enumerate(KT_LISTS[slot]):
                    sb = 2 + (kt_i % 2)
                    kof = (g // 2) * 2048 + kt * 128
                    ksl = KT[rb: rb + 64, kof: kof + 128]
                    op("tensor", lambda h, p=sb, k=ksl, q=qsl: h.matmul(
                        bank(p)[:, 0:512], k, q, start=True, stop=True),
                       r=[R(O_KT + kof * 2, 256), r_qsl], w=[RP(sb)])
                    pbi = (2 * kt_i + odd) % 4
                    pbuf = PB[pbi]
                    r_pb = R(O_PBS[pbi], 1024)
                    op("scalar", lambda h, d=pbuf, p=sb: h.activation(
                        d, bank(p)[:, 0:512], AF.Exp, bias=c_expb, scale=c_exps),
                       r=[RP(sb), R(O_CONST, 32)], w=[r_pb])
                    if kt in mask_of:
                        mi = mask_of[kt]
                        op("gpsimd", lambda h, d=pbuf, m=mi: h.tensor_tensor(
                            d, d, MASKS_SB[:, m * 512:(m + 1) * 512], ALU.mult),
                           r=[r_pb, R_MASK], w=[r_pb])
                    st = (kt_i == 0)
                    sp = (kt_i == nkt - 1)
                    vbase = (kt * 8 + g) * 65
                    if odd == 0:
                        op("tensor", lambda h, st_=st, sp_=sp, vb=vbase, d=pbuf,
                           b=avb: h.matmul(bank(b)[0:65, 0:512],
                                           VO[:, vb: vb + 65], d,
                                           start=st_, stop=sp_),
                           r=[R(O_VO + vbase * 2, 130), r_pb], w=[RP(avb)])
                    else:
                        op("tensor", lambda h, st_=st, sp_=sp, vb=vbase, d=pbuf,
                           b=avb: h.matmul(bank(b)[64:128, 0:512],
                                           VO[:, vb: vb + 64], d,
                                           start=st_, stop=sp_),
                           r=[R(O_VO + vbase * 2, 130), r_pb], w=[RP(avb)])
                        op("tensor", lambda h, st_=st, sp_=sp, vb=vbase, d=pbuf,
                           b=avb: h.matmul(bank(b)[0:1, 0:512],
                                           VO[:, vb + 64: vb + 65], d,
                                           start=st_, stop=sp_),
                           r=[R(O_VO + vbase * 2, 130), r_pb], w=[RP(avb)])
                bcb = 6 if odd == 0 else 1
                recrow, recbc = RECROWS[odd], RECBCS[odd]
                r_rec = R(O_RECS[odd], 2048)
                r_recbc = R(O_RECS[odd] + 2048, 2048)
                sum_row = bank(avb)[64:65, 0:512] if odd == 0 else bank(avb)[0:1, 0:512]
                av_rows = bank(avb)[0:64, 0:512] if odd == 0 else bank(avb)[64:128, 0:512]
                op("vector", lambda h, sr=sum_row, rr=recrow: h.reciprocal(rr, sr),
                   r=[RP(avb)], w=[r_rec])
                op("tensor", lambda h, r0=rb, b=bcb, rr=recrow: h.matmul(
                    bank(b)[r0:r0 + 64, 0:512], c_one64, rr,
                    start=True, stop=True),
                   r=[r_rec, R(O_CONST + 48, 256)], w=[RP(bcb)])
                op("scalar", lambda h, r0=rb, b=bcb, rc=recbc: h.activation(
                    rc[r0:r0 + 64, :], bank(b)[r0:r0 + 64, 0:512], AF.Copy),
                   r=[RP(bcb)], w=[r_recbc])
                op("vector", lambda h, av=av_rows, r0=rb, s=s16, rc=recbc:
                   h.tensor_tensor(
                       AOT[r0:r0 + 64, s * 512:(s + 1) * 512], av,
                       rc[r0:r0 + 64, :], ALU.mult),
                   r=[RP(avb), r_recbc], w=[R(O_AOT + s16 * 1024, 1024)])

        for fp in range(4):
            for half in range(2):
                wb = (fp * 2 + half) % 2
                dma_gp(WBUFS[wb][:, 0:8 * 512],
                       wo[1024 * half: 1024 * half + 1024, fp * 512:(fp + 1) * 512]
                       .rearrange("(c p) d -> p c d", p=128),
                       w=[R(O_WBUFS[wb], 8192)])
                for tch in range(4):
                    for dc8 in range(8):
                        aof = (half * 8 + dc8) * 512 + tch * 128
                        op("tensor", lambda h, d=dc8, hf=half, t=tch, w_=wb,
                           ao=aof: h.matmul(
                               bank(t)[:, 0:512],
                               AOT[:, ao: ao + 128],
                               WBUFS[w_][:, d * 512:(d + 1) * 512],
                               start=(hf == 0 and d == 0), stop=(hf == 1 and d == 7)),
                           r=[R(O_AOT + aof * 2, 256), R(O_WBUFS[wb], 8192)],
                           w=[RP(tch)])
            for tch in range(4):
                op("scalar", lambda h, t=tch: h.activation(
                    SEV, bank(t)[:, 0:512], AF.Copy),
                   r=[RP(tch)], w=[R_SEV])
                dma_gp(s_dram[q0 + tch * 128: q0 + tch * 128 + 128,
                              fp * 512:(fp + 1) * 512],
                       SEV,
                       r=[R_SEV], w=[("sdram", 0, 1)])

    if limit <= 3:
        return _truncate()

    R_PK = [R(O_PK + i * 4096, 4096) for i in range(2)]
    R_SC = [R(O_SC + i * 4096, 4096) for i in range(2)]
    R_DQM = R(O_DQ, 4096)
    R_DQ1 = R(O_DQ + 4096, 4096)
    R_DQ2 = R(O_DQ + 8192, 4096)
    R_WG = R(O_WG, 16384)
    R_WU = R(O_WU, 16384)

    def load_packed_gu(src_pk, src_sc, kb, slot, hi):
        c0 = 8192 * hi + 256 * kb
        dma_sp(PK[slot].rearrange("p (r j) -> p r j", j=256),
               src_pk.rearrange("(r p) u -> p r u", p=128)[:, :, c0:c0 + 256],
               w=[R_PK[slot]])
        dma_gp(SC[slot],
               src_sc.rearrange("e (c n) -> e c n", n=2048)[:, 32 * hi + kb]
               .unsqueeze(1).broadcast_to([8, 16, 2048]),
               w=[R_SC[slot]])

    def load_packed_d(kb, rt, slot):
        dma_sp(PK[slot],
               dpk.rearrange("(r p) f -> p r f", p=128)[:, 2 * kb + rt, :],
               w=[R_PK[slot]])
        dma_gp(SC[slot],
               dsc.rearrange("e (c n) -> e c n", n=2048)[:, 2 * kb + rt]
               .unsqueeze(1).broadcast_to([8, 16, 2048]),
               w=[R_SC[slot]])

    DQM = [gai[:, (O_DQ + q * 4096) // 2: (O_DQ + q * 4096) // 2 + 1024]
           for q in range(2)]
    DQU = [gab[:, (O_DQ + q * 4096 + 2048) // 2:
               (O_DQ + q * 4096 + 2048) // 2 + 1024] for q in range(2)]
    rDQM = [R(O_DQ + q * 4096, 2048) for q in range(2)]
    rDQU = [R(O_DQ + q * 4096 + 2048, 2048) for q in range(2)]
    dq_par = [0]

    def dq_chain(pk_sl, r_pk, nib, dstb, dsti, scv, r_sc, w_dst):
        q = dq_par[0]
        dq_par[0] ^= 1
        M, U, rM, rU = DQM[q], DQU[q], rDQM[q], rDQU[q]
        if nib == 0:
            op("vector", lambda h: h.tensor_single_scalar(
                M, pk_sl, 7, ALU.bitwise_and), r=[r_pk], w=[rM])
        else:
            op("vector", lambda h: h.tensor_scalar(
                M, pk_sl, 4, 7, ALU.logical_shift_right, ALU.bitwise_and),
               r=[r_pk], w=[rM])
        op("scalar", lambda h, d=dstb: h.activation(
            d, M, AF.Relu, bias=c_neg2, scale=c_half),
           r=[rM, R(O_CONST + 336, 48)], w=w_dst)
        op("scalar", lambda h: h.activation(
            U, M, AF.Relu, bias=c_neg6, scale=c_one1),
           r=[rM, R(O_CONST + 336, 48)], w=[rU])
        op("vector", lambda h, d=dstb, u=U: h.tensor_tensor(
            d, d, u, ALU.add), r=[rU] + w_dst, w=w_dst)
        op("vector", lambda h, d=dstb: h.scalar_tensor_tensor(
            d, M, 0.5, d, ALU.mult, ALU.add), r=[rM] + w_dst, w=w_dst)
        op("vector", lambda h, sh=12 - 4 * nib: h.tensor_scalar(
            M, pk_sl, sh, c_smask, ALU.logical_shift_left, ALU.bitwise_and),
           r=[r_pk, R(O_CONST + 304, 16)], w=[rM])
        op("vector", lambda h, d=dsti, m=M: h.tensor_tensor(
            d, d, m, ALU.bitwise_or), r=[rM] + w_dst, w=w_dst)
        op("vector", lambda h, d=dstb, s=scv: h.tensor_tensor(
            d, d, s, ALU.mult), r=w_dst + [r_sc], w=w_dst)

    def dequant(dst_lo, dst_hi, r_dst, slot, grouped=False):
        for nib, dsti in ((0, dst_lo), (1, dst_hi)):
            for half in range(2):
                pk_sl = PK[slot][:, half * 1024: half * 1024 + 1024]
                r_pk = R(O_PK + slot * 4096 + half * 2048, 2048)
                r_sc = R(O_SC + slot * 4096 + half * 2048, 2048)
                if grouped:
                    wbase, hload = r_dst
                    dstb = dsti[:, 4 * half: 4 * half + 4, :]
                    base16 = wbase // 2
                    d16full = gai[:, base16: base16 + 8192] \
                        .rearrange("p (fc half j) -> p fc half j",
                                   fc=16, half=2, j=256)
                    dsti16 = d16full[:, nib * 8 + 4 * half: nib * 8 + 4 * half + 4,
                                     hload, :]
                    scv = SC[slot][:, half * 1024: half * 1024 + 1024] \
                        .rearrange("p (r j) -> p r j", j=256)
                    fcr = range(nib * 8 + 4 * half, nib * 8 + 4 * half + 4)
                    w_dst = [R(wbase + (fc * 512 + hload * 256) * 2, 512)
                             for fc in fcr]
                else:
                    off = r_dst[nib]
                    dstb = dsti[:, half * 1024: half * 1024 + 1024]
                    dsti16 = gai[:, (off + half * 2048) // 2:
                                 (off + half * 2048) // 2 + 1024]
                    scv = SC[slot][:, half * 1024: half * 1024 + 1024]
                    w_dst = [R(off + half * 2048, 2048)]
                dq_chain(pk_sl, r_pk, nib, dstb, dsti16, scv, r_sc, w_dst)

    def w4(W):
        return W.rearrange("p (fc half j) -> p fc half j", fc=16, half=2)

    seq = 0
    ffn_sub = sub if (sub or "").startswith("ffn") else None
    nbk_run = {None: NBK, "ffn1": 1, "ffn2": 1, "ffn3": 2}[ffn_sub]
    do_mm = ffn_sub != "ffn1"
    for kb in range(nbk_run):
        gi = kb % GRP
        for hi in range(2):
            load_packed_gu(gpk, gsc, kb, seq % 2, hi)
            wv_ = w4(WG)
            dequant(wv_[:, 0:8, hi, :], wv_[:, 8:16, hi, :], (O_WG, hi),
                    seq % 2, grouped=True)
            seq += 1
        for u in (range(UCH) if do_mm else []):
            for t2 in range(2):
                bk_i = (2 * u + t2) % 2
                for fc in range(NFC):
                    op("tensor", lambda h, p=bk_i, f=fc, uu=u, tt=t2: h.matmul(
                        bank(p)[:, 0:512],
                        WG[:, f * 512 + uu * 128: f * 512 + uu * 128 + 128],
                        XNT_OWN[:, 1024 * f + tt * 512: 1024 * f + tt * 512 + 512],
                        start=(f == 0), stop=(f == NFC - 1)),
                       r=[R_WG, R(O_XNT_OWN, 32768)], w=[RP(bk_i)])
                op("scalar", lambda h, p=bk_i, uu=u, tt=t2, g_=gi: h.activation(
                    INTER[g_][:, uu * 1024 + tt * 512: uu * 1024 + tt * 512 + 512],
                    bank(p)[:, 0:512], AF.Gelu_apprx_tanh),
                   r=[RP(bk_i)],
                   w=[R(O_INT + gi * 8192 + (u * 1024 + t2 * 512) * 2, 1024)])
        for hi in range(2):
            load_packed_gu(upk, usc, kb, seq % 2, hi)
            wv_ = w4(WU)
            dequant(wv_[:, 0:8, hi, :], wv_[:, 8:16, hi, :], (O_WU, hi),
                    seq % 2, grouped=True)
            seq += 1
        for u in (range(UCH) if do_mm else []):
            for t2 in range(2):
                bk_i = 2 + (2 * u + t2) % 2
                for fc in range(NFC):
                    op("tensor", lambda h, p=bk_i, f=fc, uu=u, tt=t2: h.matmul(
                        bank(p)[:, 0:512],
                        WU[:, f * 512 + uu * 128: f * 512 + uu * 128 + 128],
                        XNT_OWN[:, 1024 * f + tt * 512: 1024 * f + tt * 512 + 512],
                        start=(f == 0), stop=(f == NFC - 1)),
                       r=[R_WU, R(O_XNT_OWN, 32768)], w=[RP(bk_i)])
                op("vector", lambda h, p=bk_i, uu=u, tt=t2, g_=gi: h.tensor_tensor(
                    INTER[g_][:, uu * 1024 + tt * 512: uu * 1024 + tt * 512 + 512],
                    INTER[g_][:, uu * 1024 + tt * 512: uu * 1024 + tt * 512 + 512],
                    bank(p)[:, 0:512], ALU.mult),
                   r=[RP(bk_i),
                      R(O_INT + gi * 8192 + (u * 1024 + t2 * 512) * 2, 1024)],
                   w=[R(O_INT + gi * 8192 + (u * 1024 + t2 * 512) * 2, 1024)])
        for rt in range(2):
            load_packed_d(kb, rt, seq % 2)
            dequant(WD[gi][:, rt * 2048: rt * 2048 + 2048],
                    WD[gi][:, (2 + rt) * 2048: (2 + rt) * 2048 + 2048],
                    (O_WD + gi * 16384 + rt * 4096,
                     O_WD + gi * 16384 + (2 + rt) * 4096), seq % 2)
            seq += 1
        if gi == GRP - 1 and do_mm and ffn_sub != "ffn2":
            for tc in range(8):
                bset = 4 if tc % 2 == 0 else 0
                for g2 in range(GRP):
                    for u in range(UCH):
                        for fpp in range(4):
                            st = (g2 == 0 and u == 0)
                            spv = (g2 == GRP - 1 and u == UCH - 1)
                            op("tensor", lambda h, b=bset, g_=g2, uu=u, f=fpp,
                               t=tc, st_=st, sp_=spv: h.matmul(
                                   bank(b + f)[:, 0:512],
                                   INTER[g_][:, uu * 1024 + t * 128:
                                             uu * 1024 + t * 128 + 128],
                                   WD[g_][:, uu * 2048 + f * 512:
                                          uu * 2048 + f * 512 + 512],
                                   start=st_, stop=sp_),
                               r=[R(O_INT + g2 * 8192, 8192),
                                  R(O_WD + g2 * 16384, 16384)],
                               w=[RP(bset + fpp)])
                if kb == GRP - 1:
                    op("vector", lambda h, t=tc, b=bset: h.tensor_copy(
                        facc(t), bank(b, 4)),
                       r=[RP(bset, 4)], w=[R(O_FACC + tc * 8192, 8192)])
                else:
                    op("vector", lambda h, t=tc, b=bset: h.tensor_tensor(
                        facc(t), facc(t), bank(b, 4), ALU.add),
                       r=[RP(bset, 4), R(O_FACC + tc * 8192, 8192)],
                       w=[R(O_FACC + tc * 8192, 8192)])

    if limit <= 4:
        return _truncate()

    R_SIL = R(O_WG, 8192)
    R_XOUT = R(O_WG + 8192, 8192)
    R_SBF5 = R(O_WU, 4096)
    R_XIN5 = R(O_WU + 4096, 8192)
    for tc in range(8):
        dma_sp(SBF5, s_dram[tc * 128:(tc + 1) * 128, :],
               r=[("sdram", 0, 1)], w=[R_SBF5])
        dma_sp(XIN5, x_pref[1024 + tc * 128: 1024 + (tc + 1) * 128, :],
               w=[R_XIN5])
        op("scalar", lambda h, t=tc: h.activation(SIL, facc(t), AF.Silu),
           r=[R(O_FACC + tc * 8192, 8192)], w=[R_SIL])
        op("vector", lambda h: h.tensor_tensor(SIL, SIL, SBF5, ALU.mult),
           r=[R_SIL, R_SBF5], w=[R_SIL])
        op("vector", lambda h: h.tensor_tensor(XOUT, SIL, XIN5, ALU.add),
           r=[R_SIL, R_XIN5], w=[R_XOUT])
        dma_gp(out[tc * 128:(tc + 1) * 128, :], XOUT,
               r=[R_XOUT], w=[("outd", tc, tc + 1)])

    bld.check()
    _emit(nc, bld)
    return nc


def _emit(nc, bld):
    sem_cms = {}
    sems = {}
    for n in ENGINES:
        sem_cms[n] = nc.semaphore(n)
        sems[n] = sem_cms[n].__enter__()
    bld.emit(nc, sems, tail=("gpsimd", "sync"))
    for cm in sem_cms.values():
        cm.__exit__(None, None, None)


_PERMF = np.concatenate([np.arange(0, D, 2), np.arange(1, D, 2)])
_PERMU = np.concatenate([np.arange(0, MLP, 2), np.arange(1, MLP, 2)])


def _own_rows(core):
    b, p = core // 2, core % 2
    if p == 0:
        own = np.concatenate([np.arange(0, 512), np.arange(1536, 2048)])
    else:
        own = np.arange(512, 1536)
    return b, own


def _head_pair(s):
    he = (s // 4) * 8 + s % 4
    return he, he + 4


def _rope_tables(pos):
    inv = 1.0 / ROPE_BASE ** (np.arange(0, 64, 2, dtype=np.float64) / 64.0)
    d = np.arange(128)
    ang = pos[None, :].astype(np.float64) * inv[d % 32][:, None]
    cos = np.cos(ang).astype(np.float32)
    sin = np.sin(ang).astype(np.float32)
    sign = np.where((d % 64) < 32, -1.0, 1.0).astype(np.float32)
    return np.ascontiguousarray(cos), np.ascontiguousarray(sin * sign[:, None])


def _build_masks(core):
    p = core % 2
    mk = np.zeros((16, 128, 512), dtype=BF16NP)
    diag = (np.arange(512)[:, None] <= np.arange(512)[None, :])
    for slot in range(2):
        for mi, kt in enumerate(MASKED[slot]):
            m = slot * 8 + mi
            if mi < 4:
                if slot == 0:
                    mk[m] = 1.0 if p == 1 else 0.0
                else:
                    mk[m] = 1.0 if p == 0 else 0.0
            else:
                j = mi - 4
                mk[m] = diag[j * 128:(j + 1) * 128, :].astype(BF16NP)
    return np.ascontiguousarray(mk.reshape(16 * 128, 512))


def prepare_in_maps(L0_attn_x, L0_attn_wq, L0_attn_wk, L0_attn_wv, L0_attn_wo,
                    L0_ffn_gate_W_packed, L0_ffn_gate_scales,
                    L0_ffn_up_W_packed, L0_ffn_up_scales,
                    L0_ffn_down_W_packed, L0_ffn_down_scales):
    X = np.ascontiguousarray(np.asarray(L0_attn_x, dtype=np.float32))
    qcolmap = np.concatenate([
        np.concatenate([np.arange(he * 64, he * 64 + 64),
                        np.arange(ho * 64, ho * 64 + 64)])
        for he, ho in (_head_pair(s) for s in range(16))])
    wq = np.ascontiguousarray(
        np.asarray(L0_attn_wq, dtype=np.float32)[_PERMF][:, qcolmap])
    wk = np.ascontiguousarray(np.asarray(L0_attn_wk, dtype=np.float32)[_PERMF])
    wv = np.ascontiguousarray(np.asarray(L0_attn_wv, dtype=np.float32)[_PERMF])
    womap = np.concatenate([
        np.concatenate([np.arange(he * 64, he * 64 + 64),
                        np.arange(ho * 64, ho * 64 + 64)])
        for he, ho in (_head_pair(s) for s in range(16))])
    wo = np.ascontiguousarray(np.asarray(L0_attn_wo, dtype=np.float32)[womap])

    gp = np.asarray(L0_ffn_gate_W_packed, dtype=np.int32)[_PERMU]
    up = np.asarray(L0_ffn_up_W_packed, dtype=np.int32)[_PERMU]
    dn = np.asarray(L0_ffn_down_W_packed, dtype=np.int32)
    gpk = np.ascontiguousarray(gp.view(np.int16)[:, 0::2].T)
    upk = np.ascontiguousarray(up.view(np.int16)[:, 0::2].T)
    dpk = np.ascontiguousarray(dn.view(np.int16)[:, 0::2].T)

    def _retile_gu(sc):
        t = np.asarray(sc, np.float32)[_PERMU].T.reshape(8, 8, 64, 256)
        return np.ascontiguousarray(t.transpose(1, 2, 0, 3).reshape(8, 64 * 2048))

    def _retile_d(sc):
        t = np.asarray(sc, np.float32).T.reshape(64, 8, 2048)
        return np.ascontiguousarray(t.transpose(1, 0, 2).reshape(8, 64 * 2048))

    gsc = _retile_gu(L0_ffn_gate_scales)
    usc = _retile_gu(L0_ffn_up_scales)
    dsc = _retile_d(L0_ffn_down_scales)

    ident = np.ascontiguousarray(np.eye(128, dtype=BF16NP))
    sw = np.zeros((128, 128), dtype=BF16NP)
    for i in range(128):
        sw[(i // 64) * 64 + (i % 64 + 32) % 64, i] = 1.0
    sw = np.ascontiguousarray(sw)

    in_maps = []
    for core in range(NCORE):
        b, own = _own_rows(core)
        other = np.setdiff1d(np.arange(L), own)
        order = np.concatenate([other, own])
        cosk, sink = _rope_tables(order)
        cosq, sinq = _rope_tables(own)
        in_maps.append({
            "x_pref": np.ascontiguousarray(X[b][order]),
            "wq": wq, "wk": wk, "wv": wv, "wo": wo,
            "gpk": gpk, "upk": upk, "dpk": dpk,
            "gsc": gsc, "usc": usc, "dsc": dsc,
            "cosk": cosk, "sink": sink, "cosq": cosq, "sinq": sinq,
            "masks": _build_masks(core),
            "ident": ident, "swapm": sw,
        })
    return in_maps


def kernel(**inputs):
    in_maps = prepare_in_maps(**inputs)
    nc = build_program()
    res = run_bass_kernel_spmd(nc, in_maps, core_ids=list(range(NCORE)))

    OUT = np.empty((B, L, D), dtype=np.float32)
    for core in range(NCORE):
        b, own = _own_rows(core)
        OUT[b][own] = res.results[core]["out"]
    return OUT


# version 07 (final)
# speedup vs baseline: 1.3332x; 1.3332x over previous
import numpy as np
import ml_dtypes
import concourse.bass as bass
import concourse.mybir as mybir
from concourse.bass_utils import run_bass_kernel_spmd

F32 = mybir.dt.float32
BF16 = mybir.dt.bfloat16
I16 = mybir.dt.int16
I32 = mybir.dt.int32
ALU = mybir.AluOpType
AF = mybir.ActivationFunctionType
AX = mybir.AxisListType
BF16NP = ml_dtypes.bfloat16

D = 2048
L = 2048
B = 4
NCORE = 8
OWN = 1024
PRE = 2048
NFC = 16
KT_LISTS = [[0, 1, 2, 3, 8, 9, 10, 11], list(range(16))]
MASKED = [[0, 1, 2, 3, 8, 9, 10, 11], [4, 5, 6, 7, 12, 13, 14, 15]]
MLP = 16384
NBK = 32
UCH = 4
GRP = 2
EXPC = 14.0
ROPE_BASE = 10000.0

ENGINES = ("sync", "gpsimd", "vector", "scalar", "tensor")


class Builder:
    def __init__(self):
        self.ops = []
        self.cnt = dict.fromkeys(ENGINES, 0)
        self.stores = {}

    def _store(self, sp):
        return self.stores.setdefault(sp, {"w": [], "r": []})

    def op(self, eng, fn, r=(), w=(), dma=False):
        inc = 16 if dma else 1
        deps = {}

        def dep(e, c):
            if e == eng and eng in ("tensor", "sync"):
                return
            if deps.get(e, 0) < c:
                deps[e] = c

        for (sp, lo, hi) in r:
            st = self._store(sp)
            for (wlo, whi, we, wc) in st["w"]:
                if wlo < hi and lo < whi:
                    dep(we, wc)
            if sp == "ps" and eng in ("scalar", "vector"):
                other = "vector" if eng == "scalar" else "scalar"
                for (rlo, rhi, dd) in st["r"]:
                    if rlo < hi and lo < rhi and other in dd:
                        dep(other, dd[other])
        for (sp, lo, hi) in w:
            st = self._store(sp)
            for (wlo, whi, we, wc) in st["w"]:
                if wlo < hi and lo < whi:
                    dep(we, wc)
            for (rlo, rhi, dd) in st["r"]:
                if rlo < hi and lo < rhi:
                    for e, c in dd.items():
                        dep(e, c)

        self.cnt[eng] += inc
        c_after = self.cnt[eng]

        for (sp, lo, hi) in w:
            st = self._store(sp)
            neww = []
            for (wlo, whi, we, wc) in st["w"]:
                if wlo < hi and lo < whi:
                    if wlo < lo:
                        neww.append((wlo, lo, we, wc))
                    if hi < whi:
                        neww.append((hi, whi, we, wc))
                else:
                    neww.append((wlo, whi, we, wc))
            neww.append((lo, hi, eng, c_after))
            st["w"] = neww
            newr = []
            for (rlo, rhi, dd) in st["r"]:
                if rlo < hi and lo < rhi:
                    if rlo < lo:
                        newr.append((rlo, lo, dict(dd)))
                    if hi < rhi:
                        newr.append((hi, rhi, dict(dd)))
                else:
                    newr.append((rlo, rhi, dd))
            st["r"] = newr
        for (sp, lo, hi) in r:
            st = self._store(sp)
            for ent in st["r"]:
                if ent[0] == lo and ent[1] == hi:
                    ent[2][eng] = c_after
                    break
            else:
                st["r"].append((lo, hi, {eng: c_after}))

        self.ops.append((eng, fn, deps, inc))
        return c_after

    def check(self):
        qs = {e: [o for o in self.ops if o[0] == e] for e in ENGINES}
        idx = dict.fromkeys(ENGINES, 0)
        done = dict.fromkeys(ENGINES, 0)
        progress = True
        while progress:
            progress = False
            for e in ENGINES:
                while idx[e] < len(qs[e]):
                    _, _, deps, inc = qs[e][idx[e]]
                    if all(done[de] >= dc for de, dc in deps.items()):
                        done[e] += inc
                        idx[e] += 1
                        progress = True
                    else:
                        break
        stuck = {e: (idx[e], len(qs[e])) for e in ENGINES if idx[e] < len(qs[e])}
        if stuck:
            for e, (i, n) in stuck.items():
                print("STUCK", e, "at", i, "of", n, "deps", qs[e][i][2])
            raise RuntimeError(f"build-time deadlock: {stuck}")
        return done

    def emit(self, nc, sems, tail):
        hw = {e: dict.fromkeys(ENGINES, 0) for e in ENGINES}
        with nc.Block() as block:
            for eng in ENGINES:
                def emit_one(h, eng=eng):
                    for (e, fn, deps, inc) in self.ops:
                        if e != eng:
                            continue
                        for de in ENGINES:
                            dc = deps.get(de, 0)
                            if dc > hw[eng][de]:
                                h.wait_ge(sems[de], dc)
                                hw[eng][de] = dc
                        inst = fn(h)
                        inst.then_inc(sems[eng], inc)
                    if eng in tail:
                        h.wait_ge(sems[eng], self.cnt[eng])
                getattr(block, eng)(emit_one)


def build_program(limit=99, dbg=None, dbg_s=False, sub=None):
    nc = bass.Bass("TRN2", target_bir_lowering=False, debug=False)

    def din(name, shape, dtype):
        return nc.dram_tensor(name, shape, dtype, kind="ExternalInput").ap()

    x_pref = din("x_pref", [PRE, D], F32)
    wq = din("wq", [D, D], F32)
    wk = din("wk", [D, 512], F32)
    wv = din("wv", [D, 512], F32)
    wo = din("wo", [D, D], F32)
    gpk = din("gpk", [1024, MLP], I16)
    upk = din("upk", [1024, MLP], I16)
    dpk = din("dpk", [MLP // 2, D], I16)
    gsc = din("gsc", [8, 64 * 2048], F32)
    usc = din("usc", [8, 64 * 2048], F32)
    dsc = din("dsc", [8, 64 * 2048], F32)
    cosk = din("cosk", [128, PRE], F32)
    sink = din("sink", [128, PRE], F32)
    cosq = din("cosq", [128, OWN], F32)
    sinq = din("sinq", [128, OWN], F32)
    masks_in = din("masks", [16 * 128, 512], BF16)
    ident_in = din("ident", [128, 128], BF16)
    swap_in = din("swapm", [128, 128], BF16)

    out = nc.dram_tensor("out", [OWN, D], F32, kind="ExternalOutput").ap()
    s_dram = nc.dram_tensor("s_spill", [OWN, D], BF16).ap()

    GA_EL = 52416
    ga_cm = nc.sbuf_tensor("GA", [128, GA_EL], F32)
    ga = ga_cm.__enter__().ap()
    gab = ga.bitcast(BF16)
    gai = ga.bitcast(I16)
    LIMIT = GA_EL * 4

    def fsl(off, nel, r0=0, r1=128):
        assert off % 4 == 0 and off + 4 * nel <= LIMIT, (off, nel)
        return ga[r0:r1, off // 4: off // 4 + nel]

    def bsl(off, nel, r0=0, r1=128):
        assert off % 4 == 0 and off + 2 * nel <= LIMIT, (off, nel)
        return gab[r0:r1, off // 2: off // 2 + nel]

    def isl(off, nel, r0=0, r1=128):
        assert off % 4 == 0 and off + 2 * nel <= LIMIT, (off, nel)
        return gai[r0:r1, off // 2: off // 2 + nel]

    O_XNT_OWN = 0
    O_CONST = 32768
    O_P = 33280
    P_SZ = LIMIT - O_P
    assert O_P + P_SZ <= LIMIT

    XNT_OWN = bsl(O_XNT_OWN, NFC * 1024)
    c_expb = fsl(O_CONST, 1)
    c_exps = fsl(O_CONST + 16, 1)
    c_eps = fsl(O_CONST + 32, 1)
    c_one64 = fsl(O_CONST + 48, 64, 0, 1)
    c_smask = fsl(O_CONST + 304, 1).bitcast(I32)
    c_seven = fsl(O_CONST + 320, 1).bitcast(I32)
    c_half = fsl(O_CONST + 336, 1)
    c_neg2 = fsl(O_CONST + 352, 1)
    c_neg6 = fsl(O_CONST + 368, 1)
    c_one1 = fsl(O_CONST + 384, 1)

    _o = [O_P]

    def take(n):
        r = _o[0]
        _o[0] += n
        assert _o[0] <= O_P + P_SZ, (_o[0], O_P + P_SZ)
        return r

    O_XNT_PEER = take(32768)
    O_COSK = take(8192)
    O_SINK = take(8192)
    O_COSQ = take(4096)
    O_SINQ = take(4096)
    O_MASK = take(8192)
    O_KT = take(16384)
    O_VO = take(16640)
    O_QT = take(16384)
    O_AOT = take(16384)
    O_WBUF = take(8192)
    O_XLN = take(8192)
    O_XNB = take(4096)
    O_RTMP = take(6144)
    O_PB = take(2048)
    O_REC = take(4096)
    O_STAT = take(128)
    O_SEV = take(1024)
    O_IDSW = take(1024)
    O_WBUF2 = take(8192)
    P123_END = _o[0]
    assert P123_END <= O_P + P_SZ

    XNT_PEER = bsl(O_XNT_PEER, NFC * 1024)
    COSK = fsl(O_COSK, 2048)
    SINK = fsl(O_SINK, 2048)
    COSQ = fsl(O_COSQ, 1024)
    SINQ = fsl(O_SINQ, 1024)
    MASKS_SB = bsl(O_MASK, 8 * 512)
    KT = bsl(O_KT, 4 * 2048)
    VO = bsl(O_VO, 16 * 8 * 65)
    QT = bsl(O_QT, NFC * 512)
    AOT = bsl(O_AOT, NFC * 512)
    WBUFS = [bsl(O_WBUF, 8 * 512), bsl(O_WBUF2, 8 * 512)]
    O_WBUFS = [O_WBUF, O_WBUF2]
    WBUF = WBUFS[0]
    WVBUF = bsl(O_QT, NFC * 512)
    XLNS = [fsl(O_XLN, 2048), fsl(O_WBUF2, 2048)]
    O_XLNS = [O_XLN, O_WBUF2]
    XLN = XLNS[0]
    XNBS = [bsl(O_XNB, 2048), bsl(O_RTMP, 2048)]
    O_XNBS = [O_XNB, O_RTMP]
    XNB = XNBS[0]
    KRAW = fsl(O_RTMP, 512)
    RT0 = fsl(O_RTMP + 2048, 512)
    RT1 = fsl(O_RTMP + 4096, 512)
    PB = [bsl(O_PB + i * 1024, 512) for i in range(2)] + \
         [bsl(O_XLN + i * 1024, 512) for i in range(2)]
    O_PBS = [O_PB, O_PB + 1024, O_XLN, O_XLN + 1024]
    RECROWS = [fsl(O_REC, 512, 0, 1), fsl(O_RTMP, 512, 0, 1)]
    RECBCS = [fsl(O_REC + 2048, 512), fsl(O_RTMP + 2048, 512)]
    O_RECS = [O_REC, O_RTMP]
    ST_SUM = fsl(O_STAT, 1)
    ST_SQ = fsl(O_STAT + 16, 1)
    ST_MEAN = fsl(O_STAT + 32, 1)
    ST_VAR = fsl(O_STAT + 48, 1)
    ST_SD = fsl(O_STAT + 64, 1)
    ST_RSTD = fsl(O_STAT + 80, 1)
    SEV = bsl(O_SEV, 512)
    IDENT_SB = bsl(O_IDSW, 128)
    SWAP_SB = bsl(O_IDSW + 256, 128)

    _o[0] = O_P
    O_WG = take(16384)
    O_WU = take(16384)
    O_WD = take(32768)
    O_INT = take(16384)
    O_PK = take(8192)
    O_SC = take(8192)
    O_DQ = take(12288)
    O_FACC = take(65536)
    P4_END = _o[0]
    assert P4_END <= O_P + P_SZ

    WG = bsl(O_WG, NFC * 512)
    WU = bsl(O_WU, NFC * 512)
    WD = [bsl(O_WD + g * 16384, UCH * 2048) for g in range(GRP)]
    INTER = [bsl(O_INT + g * 8192, UCH * 1024) for g in range(GRP)]
    PK = [isl(O_PK + i * 4096, 2048) for i in range(2)]
    SC = [bsl(O_SC + i * 4096, 2048) for i in range(2)]
    DQ_M = isl(O_DQ, 2048)
    DQ_T1 = isl(O_DQ + 4096, 2048)
    DQ_T2 = isl(O_DQ + 8192, 2048)

    def facc(tc):
        return fsl(O_FACC + tc * 8192, 2048)

    nc.dbg_offsets = dict(
        XNT_OWN=O_XNT_OWN, XNT_PEER=O_XNT_PEER, KT=O_KT, VO=O_VO, QT=O_QT,
        AOT=O_AOT, FACC=O_FACC, COSK=O_COSK, SINK=O_SINK, MASK=O_MASK)

    SIL = fsl(O_WG, 2048)
    XOUT = fsl(O_WG + 8192, 2048)
    SBF5 = bsl(O_WU, 2048)
    XIN5 = fsl(O_WU + 4096, 2048)

    ps_cm = nc.psum_tensor("PS", [128, 4096], F32)
    psf = ps_cm.__enter__().ap()
    psb = psf.bitcast(BF16)

    def bank(i, n=1):
        return psf[:, 512 * i: 512 * (i + n)]

    def bankb128(i):
        return psb[:, 1024 * i: 1024 * i + 128]

    def R(off, nbytes):
        return ("sb", off, off + nbytes)

    def RP(i, n=1):
        return ("ps", 2048 * i, 2048 * (i + n))

    bld = Builder()
    op = bld.op

    def dma_sp(dst, src, r=(), w=()):
        return op("sync", lambda h, d=dst, s=src: h.dma_start(out=d, in_=s),
                  r=r, w=w, dma=True)

    def dma_gp(dst, src, r=(), w=()):
        return op("gpsimd", lambda h, d=dst, s=src: h.dma_start(out=d, in_=s),
                  r=r, w=w, dma=True)

    op("vector", lambda h: h.memset(c_expb, -EXPC), w=[R(O_CONST, 16)])
    op("vector", lambda h: h.memset(c_exps, 0.125), w=[R(O_CONST + 16, 16)])
    op("vector", lambda h: h.memset(c_eps, 1e-5), w=[R(O_CONST + 32, 16)])
    op("vector", lambda h: h.memset(c_one64, 1.0), w=[R(O_CONST + 48, 256)])
    op("vector", lambda h: h.memset(c_smask, -32768), w=[R(O_CONST + 304, 16)])
    op("vector", lambda h: h.memset(c_seven, 7), w=[R(O_CONST + 320, 16)])
    op("vector", lambda h: h.memset(c_half, 0.5), w=[R(O_CONST + 336, 16)])
    op("vector", lambda h: h.memset(c_neg2, -2.0), w=[R(O_CONST + 352, 16)])
    op("vector", lambda h: h.memset(c_neg6, -6.0), w=[R(O_CONST + 368, 16)])
    op("vector", lambda h: h.memset(c_one1, 1.0), w=[R(O_CONST + 384, 16)])
    op("vector", lambda h: h.memset(
        gab[0:128, O_VO // 2 + 64: O_VO // 2 + 16 * 8 * 65: 65], 1.0),
       w=[R(O_VO, 16 * 8 * 65 * 2)])

    dma_sp(IDENT_SB, ident_in, w=[R(O_IDSW, 256)])
    dma_sp(SWAP_SB, swap_in, w=[R(O_IDSW + 256, 256)])
    dma_gp(COSK, cosk, w=[R(O_COSK, 8192)])
    dma_gp(SINK, sink, w=[R(O_SINK, 8192)])
    dma_gp(COSQ, cosq, w=[R(O_COSQ, 4096)])
    dma_gp(SINQ, sinq, w=[R(O_SINQ, 4096)])

    def _truncate():
        for i, (off, nbytes) in enumerate(dbg or []):
            dt = nc.dram_tensor("dbg%d" % i, [128, nbytes // 4], F32,
                                kind="ExternalOutput").ap()
            dma_sp(dt, fsl(off, nbytes // 4), r=[R(off, nbytes)])
        if dbg_s:
            ds = nc.dram_tensor("dbgs", [OWN, D], BF16,
                                kind="ExternalOutput").ap()
            dma_gp(ds, s_dram, r=[("sdram", 0, 1)])
        dma_gp(out, x_pref[1024:2048, :])
        bld.check()
        _emit(nc, bld)
        return nc

    if limit <= 0:
        return _truncate()

    INV_D = 1.0 / D
    for ti in range(PRE // 128):
        bi = ti % 2
        xln, r_xln = XLNS[bi], R(O_XLNS[bi], 8192)
        xnb, r_xnb = XNBS[bi], R(O_XNBS[bi], 4096)
        dma_sp(xln, x_pref[ti * 128:(ti + 1) * 128, :], w=[r_xln])
        op("vector", lambda h, x_=xln: h.reduce_sum(ST_SUM, x_, axis=AX.X),
           r=[r_xln], w=[R(O_STAT, 16)])
        op("scalar", lambda h, x_=xln, n_=xnb: h.activation(
            n_, x_, AF.Square, accum_out=ST_SQ),
           r=[r_xln], w=[r_xnb, R(O_STAT + 16, 16)])
        op("vector", lambda h: h.tensor_scalar(ST_MEAN, ST_SUM, INV_D, None, ALU.mult),
           r=[R(O_STAT, 16)], w=[R(O_STAT + 32, 16)])
        op("vector", lambda h: h.tensor_scalar(ST_VAR, ST_SQ, INV_D, None, ALU.mult),
           r=[R(O_STAT + 16, 16)], w=[R(O_STAT + 48, 16)])
        op("vector", lambda h: h.tensor_tensor(ST_SD, ST_MEAN, ST_MEAN, ALU.mult),
           r=[R(O_STAT + 32, 16)], w=[R(O_STAT + 64, 16)])
        op("vector", lambda h: h.tensor_sub(ST_VAR, ST_VAR, ST_SD),
           r=[R(O_STAT + 48, 32)], w=[R(O_STAT + 48, 16)])
        op("vector", lambda h: h.tensor_tensor(ST_VAR, ST_VAR, c_eps, ALU.add),
           r=[R(O_STAT + 48, 16), R(O_CONST + 32, 16)], w=[R(O_STAT + 48, 16)])
        op("scalar", lambda h: h.activation(ST_SD, ST_VAR, AF.Sqrt),
           r=[R(O_STAT + 48, 16)], w=[R(O_STAT + 64, 16)])
        op("vector", lambda h: h.reciprocal(ST_RSTD, ST_SD),
           r=[R(O_STAT + 64, 16)], w=[R(O_STAT + 80, 16)])
        op("vector", lambda h, x_=xln, n_=xnb: h.tensor_scalar(
            n_, x_, ST_MEAN, ST_RSTD, ALU.subtract, ALU.mult),
           r=[r_xln, R(O_STAT + 32, 16), R(O_STAT + 80, 16)], w=[r_xnb])
        dst_off = O_XNT_PEER if ti < 8 else O_XNT_OWN
        dst = XNT_PEER if ti < 8 else XNT_OWN
        col = (ti % 8) * 128
        for fc in range(NFC):
            if fc < 8:
                src = xnb.rearrange("p (j two) -> p two j", two=2)[:, 0, 128 * fc: 128 * fc + 128]
            else:
                src = xnb.rearrange("p (j two) -> p two j", two=2)[:, 1, 128 * (fc - 8): 128 * (fc - 8) + 128]
            pdst = bankb128(fc % 2)
            op("tensor", lambda h, s=src, p=pdst: h.transpose(p, s, IDENT_SB),
               r=[r_xnb, R(O_IDSW, 256)], w=[RP(fc % 2)])
            op("vector",
               lambda h, p=pdst, dd=dst, c=col, f=fc: h.tensor_copy(
                   dd[:, 1024 * f + c: 1024 * f + c + 128], p),
               r=[RP(fc % 2)],
               w=[R(dst_off + (1024 * fc + col) * 2, 256)])

    if limit <= 1:
        return _truncate()

    def xnt_rhs(fc, t0, width):
        if t0 + width <= 1024:
            return XNT_PEER[:, 1024 * fc + t0: 1024 * fc + t0 + width]
        assert t0 >= 1024
        return XNT_OWN[:, 1024 * fc + (t0 - 1024): 1024 * fc + (t0 - 1024) + width]

    def xnt_r(fc, t0, width):
        if t0 + width <= 1024:
            return R(O_XNT_PEER + (1024 * fc + t0) * 2, width * 2)
        return R(O_XNT_OWN + (1024 * fc + (t0 - 1024)) * 2, width * 2)

    R_XNT_ALL = [R(O_XNT_PEER, 32768), R(O_XNT_OWN, 32768)]

    R_WBUF = R(O_WBUF, 8192)
    R_XNB = R(O_XNB, 4096)
    R_KRAW = R(O_RTMP, 2048)
    R_RT0 = R(O_RTMP + 2048, 2048)
    R_RT1 = R(O_RTMP + 4096, 2048)

    def rope_store(pbank_i, cos_t, cos_off, sin_t, sin_off, tsz, col0, width,
                   dst, dst_r):
        pk = bank(pbank_i)[:, 0:width]
        if sub != "krope2b":
            op("scalar", lambda h: h.activation(KRAW[:, 0:width], pk, AF.Copy),
               r=[RP(pbank_i)], w=[R_KRAW])
        if sub == "krope1":
            op("vector", lambda h: h.tensor_copy(dst, KRAW[:, 0:width]),
               r=[R_KRAW], w=[dst_r])
            return
        op("vector", lambda h: h.tensor_copy(XNB[:, 0:width], pk),
           r=[RP(pbank_i)], w=[R_XNB])
        swm = IDENT_SB if sub == "krope2c" else SWAP_SB
        obank = 3 if sub == "krope2d" else 7
        swb = bank(obank)[:, 0:width]
        op("tensor", lambda h: h.matmul(swb, swm, XNB[:, 0:width],
                                        start=True, stop=True),
           r=[R_XNB, R(O_IDSW, 512)], w=[RP(obank)])
        if sub in ("krope2", "krope2b", "krope2c", "krope2d"):
            op("vector", lambda h: h.tensor_copy(dst, swb),
               r=[RP(obank)], w=[dst_r])
            return
        op("vector", lambda h: h.tensor_tensor(
            RT0[:, 0:width], KRAW[:, 0:width],
            cos_t[:, col0:col0 + width], ALU.mult),
           r=[R_KRAW, R(cos_off, tsz)], w=[R_RT0])
        op("vector", lambda h: h.tensor_tensor(
            RT1[:, 0:width], swb, sin_t[:, col0:col0 + width], ALU.mult),
           r=[RP(7), R(sin_off, tsz)], w=[R_RT1])
        op("vector", lambda h: h.tensor_add(dst, RT0[:, 0:width], RT1[:, 0:width]),
           r=[R_RT0, R_RT1], w=[dst_r])

    ndc, nt4 = (1, 1) if (sub or "").endswith("a") else (4, 4)
    if (sub or "").endswith("a"):
        sub = sub[:-1]
    if sub != "vonly":
        for dc in range(ndc):
            wb = dc % 2
            dma_gp(WBUFS[wb][:, 0:16 * 128],
                   wk[:, dc * 128:(dc + 1) * 128].rearrange("(c p) d -> p c d", p=128),
                   w=[R(O_WBUFS[wb], 8192)])
            for t4 in range(nt4):
                pb = 2 + (dc * 4 + t4) % 2
                for fc in range(NFC):
                    op("tensor", lambda h, f=fc, tt=t4 * 512, b=pb, w_=wb: h.matmul(
                        bank(b)[:, 0:512], WBUFS[w_][:, f * 128:(f + 1) * 128],
                        xnt_rhs(f, tt, 512),
                        start=(f == 0), stop=(f == NFC - 1)),
                       r=[R(O_WBUFS[wb], 8192), xnt_r(fc, t4 * 512, 512)],
                       w=[RP(pb)])
                if sub == "kproj":
                    op("vector", lambda h, dc_=dc, t_=t4, b=pb: h.tensor_copy(
                        KT[:, dc_ * 2048 + t_ * 512: dc_ * 2048 + t_ * 512 + 512],
                        bank(b)[:, 0:512]),
                       r=[RP(pb)],
                       w=[R(O_KT + (dc * 2048 + t4 * 512) * 2, 1024)])
                else:
                    rope_store(pb, COSK, O_COSK, SINK, O_SINK, 8192, t4 * 512, 512,
                               KT[:, dc * 2048 + t4 * 512: dc * 2048 + t4 * 512 + 512],
                               R(O_KT + (dc * 2048 + t4 * 512) * 2, 1024))

    R_WVBUF = R(O_QT, 16384)
    if sub not in ("kproj", "krope"):
        dma_gp(WVBUF, wv.rearrange("(c p) d -> p c d", p=128), w=[R_WVBUF])
        for tcv in range(16):
            vb = 2 + tcv % 2
            for fc in range(NFC):
                op("tensor", lambda h, f=fc, t=tcv, b=vb: h.matmul(
                    bank(b)[:, 0:512], xnt_rhs(f, t * 128, 128),
                    WVBUF[:, f * 512:(f + 1) * 512],
                    start=(f == 0), stop=(f == NFC - 1)),
                   r=[R_WVBUF, xnt_r(fc, tcv * 128, 128)], w=[RP(vb)])
            for g in range(8):
                op("scalar", lambda h, t=tcv, gg=g, b=vb: h.activation(
                    VO[:, (t * 8 + gg) * 65: (t * 8 + gg) * 65 + 64],
                    bank(b)[:, gg * 64: gg * 64 + 64], AF.Copy),
                   r=[RP(vb)], w=[R(O_VO + (tcv * 8 + g) * 65 * 2, 128)])

    if limit <= 2:
        return _truncate()

    R_QT = R(O_QT, 16384)
    R_AOT = R(O_AOT, 16384)
    R_MASK = R(O_MASK, 8192)
    R_REC = R(O_REC, 2048)
    R_RECBC = R(O_REC + 2048, 2048)
    R_SEV = R(O_SEV, 1024)

    for slot in range(2):
        q0 = slot * 512
        nkt = len(KT_LISTS[slot])
        mask_of = {kt: mi for mi, kt in enumerate(MASKED[slot])}
        dma_sp(MASKS_SB,
               masks_in.rearrange("(m p) n -> p m n", p=128)
               [:, slot * 8:(slot + 1) * 8, :],
               w=[R_MASK])
        for s16 in range(NFC):
            wb = s16 % 2
            pb = 2 + s16 % 2
            dma_gp(WBUFS[wb][:, 0:16 * 128],
                   wq[:, s16 * 128:(s16 + 1) * 128].rearrange("(c p) d -> p c d", p=128),
                   w=[R(O_WBUFS[wb], 8192)])
            for fc in range(NFC):
                op("tensor", lambda h, f=fc, qq=q0, b=pb, w_=wb: h.matmul(
                    bank(b)[:, 0:512], WBUFS[w_][:, f * 128:(f + 1) * 128],
                    XNT_OWN[:, 1024 * f + qq: 1024 * f + qq + 512],
                    start=(f == 0), stop=(f == NFC - 1)),
                   r=[R(O_WBUFS[wb], 8192), xnt_r(fc, 1024 + q0, 512)],
                   w=[RP(pb)])
            rope_store(pb, COSQ, O_COSQ, SINQ, O_SINQ, 4096, q0, 512,
                       QT[:, s16 * 512:(s16 + 1) * 512],
                       R(O_QT + s16 * 512 * 2, 1024))

        for s16 in range(NFC):
            for odd in range(2):
                H = (s16 // 4) * 8 + s16 % 4 + 4 * odd
                g = H // 4
                rb = 64 * (g % 2)
                assert rb == 64 * odd
                avb = 4 + odd
                qsl = QT[rb: rb + 64, s16 * 512:(s16 + 1) * 512]
                r_qsl = R(O_QT + s16 * 1024, 1024)
                for kt_i, kt in enumerate(KT_LISTS[slot]):
                    sb = 2 + (kt_i % 2)
                    kof = (g // 2) * 2048 + kt * 128
                    ksl = KT[rb: rb + 64, kof: kof + 128]
                    op("tensor", lambda h, p=sb, k=ksl, q=qsl: h.matmul(
                        bank(p)[:, 0:512], k, q, start=True, stop=True),
                       r=[R(O_KT + kof * 2, 256), r_qsl], w=[RP(sb)])
                    pbi = (2 * kt_i + odd) % 4
                    pbuf = PB[pbi]
                    r_pb = R(O_PBS[pbi], 1024)
                    op("scalar", lambda h, d=pbuf, p=sb: h.activation(
                        d, bank(p)[:, 0:512], AF.Exp, bias=c_expb, scale=c_exps),
                       r=[RP(sb), R(O_CONST, 32)], w=[r_pb])
                    if kt in mask_of:
                        mi = mask_of[kt]
                        op("gpsimd", lambda h, d=pbuf, m=mi: h.tensor_tensor(
                            d, d, MASKS_SB[:, m * 512:(m + 1) * 512], ALU.mult),
                           r=[r_pb, R_MASK], w=[r_pb])
                    st = (kt_i == 0)
                    sp = (kt_i == nkt - 1)
                    vbase = (kt * 8 + g) * 65
                    if odd == 0:
                        op("tensor", lambda h, st_=st, sp_=sp, vb=vbase, d=pbuf,
                           b=avb: h.matmul(bank(b)[0:65, 0:512],
                                           VO[:, vb: vb + 65], d,
                                           start=st_, stop=sp_),
                           r=[R(O_VO + vbase * 2, 130), r_pb], w=[RP(avb)])
                    else:
                        op("tensor", lambda h, st_=st, sp_=sp, vb=vbase, d=pbuf,
                           b=avb: h.matmul(bank(b)[64:128, 0:512],
                                           VO[:, vb: vb + 64], d,
                                           start=st_, stop=sp_),
                           r=[R(O_VO + vbase * 2, 130), r_pb], w=[RP(avb)])
                        op("tensor", lambda h, st_=st, sp_=sp, vb=vbase, d=pbuf,
                           b=avb: h.matmul(bank(b)[0:1, 0:512],
                                           VO[:, vb + 64: vb + 65], d,
                                           start=st_, stop=sp_),
                           r=[R(O_VO + vbase * 2, 130), r_pb], w=[RP(avb)])
                bcb = 6 if odd == 0 else 1
                recrow, recbc = RECROWS[odd], RECBCS[odd]
                r_rec = R(O_RECS[odd], 2048)
                r_recbc = R(O_RECS[odd] + 2048, 2048)
                sum_row = bank(avb)[64:65, 0:512] if odd == 0 else bank(avb)[0:1, 0:512]
                av_rows = bank(avb)[0:64, 0:512] if odd == 0 else bank(avb)[64:128, 0:512]
                op("vector", lambda h, sr=sum_row, rr=recrow: h.reciprocal(rr, sr),
                   r=[RP(avb)], w=[r_rec])
                op("tensor", lambda h, r0=rb, b=bcb, rr=recrow: h.matmul(
                    bank(b)[r0:r0 + 64, 0:512], c_one64, rr,
                    start=True, stop=True),
                   r=[r_rec, R(O_CONST + 48, 256)], w=[RP(bcb)])
                op("scalar", lambda h, r0=rb, b=bcb, rc=recbc: h.activation(
                    rc[r0:r0 + 64, :], bank(b)[r0:r0 + 64, 0:512], AF.Copy),
                   r=[RP(bcb)], w=[r_recbc])
                op("vector", lambda h, av=av_rows, r0=rb, s=s16, rc=recbc:
                   h.tensor_tensor(
                       AOT[r0:r0 + 64, s * 512:(s + 1) * 512], av,
                       rc[r0:r0 + 64, :], ALU.mult),
                   r=[RP(avb), r_recbc], w=[R(O_AOT + s16 * 1024, 1024)])

        for fp in range(4):
            for half in range(2):
                wb = (fp * 2 + half) % 2
                dma_gp(WBUFS[wb][:, 0:8 * 512],
                       wo[1024 * half: 1024 * half + 1024, fp * 512:(fp + 1) * 512]
                       .rearrange("(c p) d -> p c d", p=128),
                       w=[R(O_WBUFS[wb], 8192)])
                for tch in range(4):
                    for dc8 in range(8):
                        aof = (half * 8 + dc8) * 512 + tch * 128
                        op("tensor", lambda h, d=dc8, hf=half, t=tch, w_=wb,
                           ao=aof: h.matmul(
                               bank(t)[:, 0:512],
                               AOT[:, ao: ao + 128],
                               WBUFS[w_][:, d * 512:(d + 1) * 512],
                               start=(hf == 0 and d == 0), stop=(hf == 1 and d == 7)),
                           r=[R(O_AOT + aof * 2, 256), R(O_WBUFS[wb], 8192)],
                           w=[RP(tch)])
            for tch in range(4):
                op("scalar", lambda h, t=tch: h.activation(
                    SEV, bank(t)[:, 0:512], AF.Copy),
                   r=[RP(tch)], w=[R_SEV])
                dma_gp(s_dram[q0 + tch * 128: q0 + tch * 128 + 128,
                              fp * 512:(fp + 1) * 512],
                       SEV,
                       r=[R_SEV], w=[("sdram", 0, 1)])

    if limit <= 3:
        return _truncate()

    R_PK = [R(O_PK + i * 4096, 4096) for i in range(2)]
    R_SC = [R(O_SC + i * 4096, 4096) for i in range(2)]
    R_DQM = R(O_DQ, 4096)
    R_DQ1 = R(O_DQ + 4096, 4096)
    R_DQ2 = R(O_DQ + 8192, 4096)
    R_WG = R(O_WG, 16384)
    R_WU = R(O_WU, 16384)

    def load_packed_gu(src_pk, src_sc, kb, slot, hi):
        c0 = 8192 * hi + 256 * kb
        dma_sp(PK[slot].rearrange("p (r j) -> p r j", j=256),
               src_pk.rearrange("(r p) u -> p r u", p=128)[:, :, c0:c0 + 256],
               w=[R_PK[slot]])
        dma_gp(SC[slot],
               src_sc.rearrange("e (c n) -> e c n", n=2048)[:, 32 * hi + kb]
               .unsqueeze(1).broadcast_to([8, 16, 2048]),
               w=[R_SC[slot]])

    def load_packed_d(kb, rt, slot):
        dma_sp(PK[slot],
               dpk.rearrange("(r p) f -> p r f", p=128)[:, 2 * kb + rt, :],
               w=[R_PK[slot]])
        dma_gp(SC[slot],
               dsc.rearrange("e (c n) -> e c n", n=2048)[:, 2 * kb + rt]
               .unsqueeze(1).broadcast_to([8, 16, 2048]),
               w=[R_SC[slot]])

    DQ_M = isl(O_DQ, 2048)
    DQ_T1 = isl(O_DQ + 4096, 2048)
    DQ_T2 = isl(O_DQ + 8192, 2048)
    R_DQM = R(O_DQ, 4096)
    R_DQ1 = R(O_DQ + 4096, 4096)
    R_DQ2 = R(O_DQ + 8192, 4096)

    def dequant(dst_lo, dst_hi, r_dst, slot, grouped=False):
        if grouped:
            t1b = DQ_T1.bitcast(BF16).rearrange("p (r j) -> p r j", j=256)
            scv = SC[slot].rearrange("p (r j) -> p r j", j=256)
        else:
            t1b = DQ_T1.bitcast(BF16)
            scv = SC[slot]
        for hi, dsti, w_dst in ((0, dst_lo, r_dst[0]), (1, dst_hi, r_dst[1])):
            if hi == 0:
                op("vector", lambda h: h.tensor_single_scalar(
                    DQ_M, PK[slot], 7, ALU.bitwise_and),
                   r=[R(O_PK + slot * 4096, 4096)], w=[R_DQM])
            else:
                op("vector", lambda h: h.tensor_scalar(
                    DQ_M, PK[slot], 4, 7,
                    ALU.logical_shift_right, ALU.bitwise_and),
                   r=[R(O_PK + slot * 4096, 4096)], w=[R_DQM])
            op("vector", lambda h: h.tensor_scalar(
                DQ_T1, DQ_M, 1, float(0x3F00), ALU.min, ALU.mult),
               r=[R_DQM], w=[R_DQ1])
            op("vector", lambda h: h.tensor_scalar(
                DQ_T2, DQ_M, 1, float(0x80), ALU.max, ALU.mult),
               r=[R_DQM], w=[R_DQ2])
            op("vector", lambda h: h.tensor_add(DQ_T1, DQ_T1, DQ_T2),
               r=[R_DQ1, R_DQ2], w=[R_DQ1])
            op("vector", lambda h: h.tensor_scalar(
                DQ_T2, DQ_M, 2, float(-0x40), ALU.max, ALU.mult),
               r=[R_DQM], w=[R_DQ2])
            op("vector", lambda h: h.tensor_add(DQ_T1, DQ_T1, DQ_T2),
               r=[R_DQ1, R_DQ2], w=[R_DQ1])
            op("vector", lambda h, sh=12 - 4 * hi: h.tensor_scalar(
                DQ_T2, PK[slot], sh, c_smask,
                ALU.logical_shift_left, ALU.bitwise_and),
               r=[R(O_PK + slot * 4096, 4096), R(O_CONST + 304, 16)],
               w=[R_DQ2])
            op("vector", lambda h: h.tensor_tensor(
                DQ_T1, DQ_T1, DQ_T2, ALU.bitwise_or),
               r=[R_DQ1, R_DQ2], w=[R_DQ1])
            op("vector", lambda h, dd=dsti: h.tensor_tensor(dd, t1b, scv, ALU.mult),
               r=[R_DQ1, R(O_SC + slot * 4096, 4096)], w=w_dst)

    def w4(W):
        return W.rearrange("p (fc half j) -> p fc half j", fc=16, half=2)

    seq = 0
    ffn_sub = sub if (sub or "").startswith("ffn") else None
    nbk_run = {None: NBK, "ffn1": 1, "ffn2": 1, "ffn3": 2}[ffn_sub]
    do_mm = ffn_sub != "ffn1"
    for kb in range(nbk_run):
        gi = kb % GRP
        for hi in range(2):
            load_packed_gu(gpk, gsc, kb, seq % 2, hi)
            wv_ = w4(WG)
            dequant(wv_[:, 0:8, hi, :], wv_[:, 8:16, hi, :],
                    ([R(O_WG + (fc * 512 + hi * 256) * 2, 512) for fc in range(8)],
                     [R(O_WG + (fc * 512 + hi * 256) * 2, 512) for fc in range(8, 16)]),
                    seq % 2, grouped=True)
            seq += 1
        for u in (range(UCH) if do_mm else []):
            for t2 in range(2):
                bk_i = (2 * u + t2) % 2
                for fc in range(NFC):
                    op("tensor", lambda h, p=bk_i, f=fc, uu=u, tt=t2: h.matmul(
                        bank(p)[:, 0:512],
                        WG[:, f * 512 + uu * 128: f * 512 + uu * 128 + 128],
                        XNT_OWN[:, 1024 * f + tt * 512: 1024 * f + tt * 512 + 512],
                        start=(f == 0), stop=(f == NFC - 1)),
                       r=[R_WG, R(O_XNT_OWN, 32768)], w=[RP(bk_i)])
                op("scalar", lambda h, p=bk_i, uu=u, tt=t2, g_=gi: h.activation(
                    INTER[g_][:, uu * 1024 + tt * 512: uu * 1024 + tt * 512 + 512],
                    bank(p)[:, 0:512], AF.Gelu_apprx_tanh),
                   r=[RP(bk_i)],
                   w=[R(O_INT + gi * 8192 + (u * 1024 + t2 * 512) * 2, 1024)])
        for hi in range(2):
            load_packed_gu(upk, usc, kb, seq % 2, hi)
            wv_ = w4(WU)
            dequant(wv_[:, 0:8, hi, :], wv_[:, 8:16, hi, :], ([R(O_WU + (fc * 512 + hi * 256) * 2, 512) for fc in range(8)],
                     [R(O_WU + (fc * 512 + hi * 256) * 2, 512) for fc in range(8, 16)]),
                    seq % 2, grouped=True)
            seq += 1
        for u in (range(UCH) if do_mm else []):
            for t2 in range(2):
                bk_i = 2 + (2 * u + t2) % 2
                for fc in range(NFC):
                    op("tensor", lambda h, p=bk_i, f=fc, uu=u, tt=t2: h.matmul(
                        bank(p)[:, 0:512],
                        WU[:, f * 512 + uu * 128: f * 512 + uu * 128 + 128],
                        XNT_OWN[:, 1024 * f + tt * 512: 1024 * f + tt * 512 + 512],
                        start=(f == 0), stop=(f == NFC - 1)),
                       r=[R_WU, R(O_XNT_OWN, 32768)], w=[RP(bk_i)])
                op("vector", lambda h, p=bk_i, uu=u, tt=t2, g_=gi: h.tensor_tensor(
                    INTER[g_][:, uu * 1024 + tt * 512: uu * 1024 + tt * 512 + 512],
                    INTER[g_][:, uu * 1024 + tt * 512: uu * 1024 + tt * 512 + 512],
                    bank(p)[:, 0:512], ALU.mult),
                   r=[RP(bk_i),
                      R(O_INT + gi * 8192 + (u * 1024 + t2 * 512) * 2, 1024)],
                   w=[R(O_INT + gi * 8192 + (u * 1024 + t2 * 512) * 2, 1024)])
        for rt in range(2):
            load_packed_d(kb, rt, seq % 2)
            dequant(WD[gi][:, rt * 2048: rt * 2048 + 2048],
                    WD[gi][:, (2 + rt) * 2048: (2 + rt) * 2048 + 2048],
                    ([R(O_WD + gi * 16384 + rt * 4096, 4096)],
                     [R(O_WD + gi * 16384 + (2 + rt) * 4096, 4096)]), seq % 2)
            seq += 1
        if gi == GRP - 1 and do_mm and ffn_sub != "ffn2":
            for tc in range(8):
                bset = 4 if tc % 2 == 0 else 0
                for g2 in range(GRP):
                    for u in range(UCH):
                        for fpp in range(4):
                            st = (g2 == 0 and u == 0)
                            spv = (g2 == GRP - 1 and u == UCH - 1)
                            op("tensor", lambda h, b=bset, g_=g2, uu=u, f=fpp,
                               t=tc, st_=st, sp_=spv: h.matmul(
                                   bank(b + f)[:, 0:512],
                                   INTER[g_][:, uu * 1024 + t * 128:
                                             uu * 1024 + t * 128 + 128],
                                   WD[g_][:, uu * 2048 + f * 512:
                                          uu * 2048 + f * 512 + 512],
                                   start=st_, stop=sp_),
                               r=[R(O_INT + g2 * 8192, 8192),
                                  R(O_WD + g2 * 16384, 16384)],
                               w=[RP(bset + fpp)])
                if kb == GRP - 1:
                    op("vector", lambda h, t=tc, b=bset: h.tensor_copy(
                        facc(t), bank(b, 4)),
                       r=[RP(bset, 4)], w=[R(O_FACC + tc * 8192, 8192)])
                else:
                    op("vector", lambda h, t=tc, b=bset: h.tensor_tensor(
                        facc(t), facc(t), bank(b, 4), ALU.add),
                       r=[RP(bset, 4), R(O_FACC + tc * 8192, 8192)],
                       w=[R(O_FACC + tc * 8192, 8192)])

    if limit <= 4:
        return _truncate()

    R_SIL = R(O_WG, 8192)
    R_XOUT = R(O_WG + 8192, 8192)
    R_SBF5 = R(O_WU, 4096)
    R_XIN5 = R(O_WU + 4096, 8192)
    for tc in range(8):
        dma_sp(SBF5, s_dram[tc * 128:(tc + 1) * 128, :],
               r=[("sdram", 0, 1)], w=[R_SBF5])
        dma_sp(XIN5, x_pref[1024 + tc * 128: 1024 + (tc + 1) * 128, :],
               w=[R_XIN5])
        op("scalar", lambda h, t=tc: h.activation(SIL, facc(t), AF.Silu),
           r=[R(O_FACC + tc * 8192, 8192)], w=[R_SIL])
        op("vector", lambda h: h.tensor_tensor(SIL, SIL, SBF5, ALU.mult),
           r=[R_SIL, R_SBF5], w=[R_SIL])
        op("vector", lambda h: h.tensor_tensor(XOUT, SIL, XIN5, ALU.add),
           r=[R_SIL, R_XIN5], w=[R_XOUT])
        dma_gp(out[tc * 128:(tc + 1) * 128, :], XOUT,
               r=[R_XOUT], w=[("outd", tc, tc + 1)])

    bld.check()
    _emit(nc, bld)
    return nc


def _emit(nc, bld):
    sem_cms = {}
    sems = {}
    for n in ENGINES:
        sem_cms[n] = nc.semaphore(n)
        sems[n] = sem_cms[n].__enter__()
    bld.emit(nc, sems, tail=("gpsimd", "sync"))
    for cm in sem_cms.values():
        cm.__exit__(None, None, None)


_PERMF = np.concatenate([np.arange(0, D, 2), np.arange(1, D, 2)])
_PERMU = np.concatenate([np.arange(0, MLP, 2), np.arange(1, MLP, 2)])


def _own_rows(core):
    b, p = core // 2, core % 2
    if p == 0:
        own = np.concatenate([np.arange(0, 512), np.arange(1536, 2048)])
    else:
        own = np.arange(512, 1536)
    return b, own


def _head_pair(s):
    he = (s // 4) * 8 + s % 4
    return he, he + 4


def _rope_tables(pos):
    inv = 1.0 / ROPE_BASE ** (np.arange(0, 64, 2, dtype=np.float64) / 64.0)
    d = np.arange(128)
    ang = pos[None, :].astype(np.float64) * inv[d % 32][:, None]
    cos = np.cos(ang).astype(np.float32)
    sin = np.sin(ang).astype(np.float32)
    sign = np.where((d % 64) < 32, -1.0, 1.0).astype(np.float32)
    return np.ascontiguousarray(cos), np.ascontiguousarray(sin * sign[:, None])


def _build_masks(core):
    p = core % 2
    mk = np.zeros((16, 128, 512), dtype=BF16NP)
    diag = (np.arange(512)[:, None] <= np.arange(512)[None, :])
    for slot in range(2):
        for mi, kt in enumerate(MASKED[slot]):
            m = slot * 8 + mi
            if mi < 4:
                if slot == 0:
                    mk[m] = 1.0 if p == 1 else 0.0
                else:
                    mk[m] = 1.0 if p == 0 else 0.0
            else:
                j = mi - 4
                mk[m] = diag[j * 128:(j + 1) * 128, :].astype(BF16NP)
    return np.ascontiguousarray(mk.reshape(16 * 128, 512))


def prepare_in_maps(L0_attn_x, L0_attn_wq, L0_attn_wk, L0_attn_wv, L0_attn_wo,
                    L0_ffn_gate_W_packed, L0_ffn_gate_scales,
                    L0_ffn_up_W_packed, L0_ffn_up_scales,
                    L0_ffn_down_W_packed, L0_ffn_down_scales):
    X = np.ascontiguousarray(np.asarray(L0_attn_x, dtype=np.float32))
    qcolmap = np.concatenate([
        np.concatenate([np.arange(he * 64, he * 64 + 64),
                        np.arange(ho * 64, ho * 64 + 64)])
        for he, ho in (_head_pair(s) for s in range(16))])
    wq = np.ascontiguousarray(
        np.asarray(L0_attn_wq, dtype=np.float32)[_PERMF][:, qcolmap])
    wk = np.ascontiguousarray(np.asarray(L0_attn_wk, dtype=np.float32)[_PERMF])
    wv = np.ascontiguousarray(np.asarray(L0_attn_wv, dtype=np.float32)[_PERMF])
    womap = np.concatenate([
        np.concatenate([np.arange(he * 64, he * 64 + 64),
                        np.arange(ho * 64, ho * 64 + 64)])
        for he, ho in (_head_pair(s) for s in range(16))])
    wo = np.ascontiguousarray(np.asarray(L0_attn_wo, dtype=np.float32)[womap])

    gp = np.asarray(L0_ffn_gate_W_packed, dtype=np.int32)[_PERMU]
    up = np.asarray(L0_ffn_up_W_packed, dtype=np.int32)[_PERMU]
    dn = np.asarray(L0_ffn_down_W_packed, dtype=np.int32)
    gpk = np.ascontiguousarray(gp.view(np.int16)[:, 0::2].T)
    upk = np.ascontiguousarray(up.view(np.int16)[:, 0::2].T)
    dpk = np.ascontiguousarray(dn.view(np.int16)[:, 0::2].T)

    def _retile_gu(sc):
        t = np.asarray(sc, np.float32)[_PERMU].T.reshape(8, 8, 64, 256)
        return np.ascontiguousarray(t.transpose(1, 2, 0, 3).reshape(8, 64 * 2048))

    def _retile_d(sc):
        t = np.asarray(sc, np.float32).T.reshape(64, 8, 2048)
        return np.ascontiguousarray(t.transpose(1, 0, 2).reshape(8, 64 * 2048))

    gsc = _retile_gu(L0_ffn_gate_scales)
    usc = _retile_gu(L0_ffn_up_scales)
    dsc = _retile_d(L0_ffn_down_scales)

    ident = np.ascontiguousarray(np.eye(128, dtype=BF16NP))
    sw = np.zeros((128, 128), dtype=BF16NP)
    for i in range(128):
        sw[(i // 64) * 64 + (i % 64 + 32) % 64, i] = 1.0
    sw = np.ascontiguousarray(sw)

    in_maps = []
    for core in range(NCORE):
        b, own = _own_rows(core)
        other = np.setdiff1d(np.arange(L), own)
        order = np.concatenate([other, own])
        cosk, sink = _rope_tables(order)
        cosq, sinq = _rope_tables(own)
        in_maps.append({
            "x_pref": np.ascontiguousarray(X[b][order]),
            "wq": wq, "wk": wk, "wv": wv, "wo": wo,
            "gpk": gpk, "upk": upk, "dpk": dpk,
            "gsc": gsc, "usc": usc, "dsc": dsc,
            "cosk": cosk, "sink": sink, "cosq": cosq, "sinq": sinq,
            "masks": _build_masks(core),
            "ident": ident, "swapm": sw,
        })
    return in_maps


def kernel(**inputs):
    in_maps = prepare_in_maps(**inputs)
    nc = build_program()
    res = run_bass_kernel_spmd(nc, in_maps, core_ids=list(range(NCORE)))

    OUT = np.empty((B, L, D), dtype=np.float32)
    for core in range(NCORE):
        b, own = _own_rows(core)
        OUT[b][own] = res.results[core]["out"]
    return OUT


# version 08
# speedup vs baseline: 1.3335x; 1.0002x over previous
import numpy as np
import ml_dtypes
import concourse.bass as bass
import concourse.mybir as mybir
from concourse.bass_utils import run_bass_kernel_spmd

F32 = mybir.dt.float32
BF16 = mybir.dt.bfloat16
I16 = mybir.dt.int16
I32 = mybir.dt.int32
ALU = mybir.AluOpType
AF = mybir.ActivationFunctionType
AX = mybir.AxisListType
BF16NP = ml_dtypes.bfloat16

D = 2048
L = 2048
B = 4
NCORE = 8
OWN = 1024
PRE = 2048
NFC = 16
KT_LISTS = [[0, 1, 2, 3, 8, 9, 10, 11], list(range(16))]
MASKED = [[0, 1, 2, 3, 8, 9, 10, 11], [4, 5, 6, 7, 12, 13, 14, 15]]
MLP = 16384
NBK = 32
UCH = 4
GRP = 2
EXPC = 14.0
ROPE_BASE = 10000.0

ENGINES = ("sync", "gpsimd", "vector", "scalar", "tensor")


class Builder:
    def __init__(self):
        self.ops = []
        self.cnt = dict.fromkeys(ENGINES, 0)
        self.stores = {}

    def _store(self, sp):
        return self.stores.setdefault(sp, {"w": [], "r": []})

    def op(self, eng, fn, r=(), w=(), dma=False):
        inc = 16 if dma else 1
        deps = {}

        def dep(e, c):
            if e == eng and eng in ("tensor", "sync"):
                return
            if deps.get(e, 0) < c:
                deps[e] = c

        for (sp, lo, hi) in r:
            st = self._store(sp)
            for (wlo, whi, we, wc) in st["w"]:
                if wlo < hi and lo < whi:
                    dep(we, wc)
            if sp == "ps" and eng in ("scalar", "vector"):
                other = "vector" if eng == "scalar" else "scalar"
                for (rlo, rhi, dd) in st["r"]:
                    if rlo < hi and lo < rhi and other in dd:
                        dep(other, dd[other])
        for (sp, lo, hi) in w:
            st = self._store(sp)
            for (wlo, whi, we, wc) in st["w"]:
                if wlo < hi and lo < whi:
                    dep(we, wc)
            for (rlo, rhi, dd) in st["r"]:
                if rlo < hi and lo < rhi:
                    for e, c in dd.items():
                        dep(e, c)

        self.cnt[eng] += inc
        c_after = self.cnt[eng]

        for (sp, lo, hi) in w:
            st = self._store(sp)
            neww = []
            for (wlo, whi, we, wc) in st["w"]:
                if wlo < hi and lo < whi:
                    if wlo < lo:
                        neww.append((wlo, lo, we, wc))
                    if hi < whi:
                        neww.append((hi, whi, we, wc))
                else:
                    neww.append((wlo, whi, we, wc))
            neww.append((lo, hi, eng, c_after))
            st["w"] = neww
            newr = []
            for (rlo, rhi, dd) in st["r"]:
                if rlo < hi and lo < rhi:
                    if rlo < lo:
                        newr.append((rlo, lo, dict(dd)))
                    if hi < rhi:
                        newr.append((hi, rhi, dict(dd)))
                else:
                    newr.append((rlo, rhi, dd))
            st["r"] = newr
        for (sp, lo, hi) in r:
            st = self._store(sp)
            for ent in st["r"]:
                if ent[0] == lo and ent[1] == hi:
                    ent[2][eng] = c_after
                    break
            else:
                st["r"].append((lo, hi, {eng: c_after}))

        self.ops.append((eng, fn, deps, inc))
        return c_after

    def check(self):
        qs = {e: [o for o in self.ops if o[0] == e] for e in ENGINES}
        idx = dict.fromkeys(ENGINES, 0)
        done = dict.fromkeys(ENGINES, 0)
        progress = True
        while progress:
            progress = False
            for e in ENGINES:
                while idx[e] < len(qs[e]):
                    _, _, deps, inc = qs[e][idx[e]]
                    if all(done[de] >= dc for de, dc in deps.items()):
                        done[e] += inc
                        idx[e] += 1
                        progress = True
                    else:
                        break
        stuck = {e: (idx[e], len(qs[e])) for e in ENGINES if idx[e] < len(qs[e])}
        if stuck:
            for e, (i, n) in stuck.items():
                print("STUCK", e, "at", i, "of", n, "deps", qs[e][i][2])
            raise RuntimeError(f"build-time deadlock: {stuck}")
        return done

    def emit(self, nc, sems, tail):
        hw = {e: dict.fromkeys(ENGINES, 0) for e in ENGINES}
        with nc.Block() as block:
            for eng in ENGINES:
                def emit_one(h, eng=eng):
                    for (e, fn, deps, inc) in self.ops:
                        if e != eng:
                            continue
                        for de in ENGINES:
                            dc = deps.get(de, 0)
                            if dc > hw[eng][de]:
                                h.wait_ge(sems[de], dc)
                                hw[eng][de] = dc
                        inst = fn(h)
                        inst.then_inc(sems[eng], inc)
                    if eng in tail:
                        h.wait_ge(sems[eng], self.cnt[eng])
                getattr(block, eng)(emit_one)


def build_program(limit=99, dbg=None, dbg_s=False, sub=None):
    nc = bass.Bass("TRN2", target_bir_lowering=False, debug=False)

    def din(name, shape, dtype):
        return nc.dram_tensor(name, shape, dtype, kind="ExternalInput").ap()

    x_pref = din("x_pref", [PRE, D], F32)
    wq = din("wq", [D, D], F32)
    wk = din("wk", [D, 512], F32)
    wv = din("wv", [D, 512], F32)
    wo = din("wo", [D, D], F32)
    gpk = din("gpk", [1024, MLP], I16)
    upk = din("upk", [1024, MLP], I16)
    dpk = din("dpk", [MLP // 2, D], I16)
    gsc = din("gsc", [8, 64 * 2048], F32)
    usc = din("usc", [8, 64 * 2048], F32)
    dsc = din("dsc", [8, 64 * 2048], F32)
    cosk = din("cosk", [128, PRE], F32)
    sink = din("sink", [128, PRE], F32)
    cosq = din("cosq", [128, OWN], F32)
    sinq = din("sinq", [128, OWN], F32)
    masks_in = din("masks", [16 * 128, 512], BF16)
    ident_in = din("ident", [128, 128], BF16)
    swap_in = din("swapm", [128, 128], BF16)

    out = nc.dram_tensor("out", [OWN, D], F32, kind="ExternalOutput").ap()
    s_dram = nc.dram_tensor("s_spill", [OWN, D], BF16).ap()

    GA_EL = 52416
    ga_cm = nc.sbuf_tensor("GA", [128, GA_EL], F32)
    ga = ga_cm.__enter__().ap()
    gab = ga.bitcast(BF16)
    gai = ga.bitcast(I16)
    LIMIT = GA_EL * 4

    def fsl(off, nel, r0=0, r1=128):
        assert off % 4 == 0 and off + 4 * nel <= LIMIT, (off, nel)
        return ga[r0:r1, off // 4: off // 4 + nel]

    def bsl(off, nel, r0=0, r1=128):
        assert off % 4 == 0 and off + 2 * nel <= LIMIT, (off, nel)
        return gab[r0:r1, off // 2: off // 2 + nel]

    def isl(off, nel, r0=0, r1=128):
        assert off % 4 == 0 and off + 2 * nel <= LIMIT, (off, nel)
        return gai[r0:r1, off // 2: off // 2 + nel]

    O_XNT_OWN = 0
    O_CONST = 32768
    O_P = 33280
    P_SZ = LIMIT - O_P
    assert O_P + P_SZ <= LIMIT

    XNT_OWN = bsl(O_XNT_OWN, NFC * 1024)
    c_expb = fsl(O_CONST, 1)
    c_exps = fsl(O_CONST + 16, 1)
    c_eps = fsl(O_CONST + 32, 1)
    c_one64 = fsl(O_CONST + 48, 64, 0, 1)
    c_smask = fsl(O_CONST + 304, 1).bitcast(I32)
    c_seven = fsl(O_CONST + 320, 1).bitcast(I32)
    c_half = fsl(O_CONST + 336, 1)
    c_neg2 = fsl(O_CONST + 352, 1)
    c_neg6 = fsl(O_CONST + 368, 1)
    c_one1 = fsl(O_CONST + 384, 1)

    _o = [O_P]

    def take(n):
        r = _o[0]
        _o[0] += n
        assert _o[0] <= O_P + P_SZ, (_o[0], O_P + P_SZ)
        return r

    O_XNT_PEER = take(32768)
    O_COSK = take(8192)
    O_SINK = take(8192)
    O_COSQ = take(4096)
    O_SINQ = take(4096)
    O_MASK = take(8192)
    O_KT = take(16384)
    O_VO = take(16640)
    O_QT = take(16384)
    O_AOT = take(16384)
    O_WBUF = take(8192)
    O_XLN = take(8192)
    O_XNB = take(4096)
    O_RTMP = take(6144)
    O_PB = take(2048)
    O_REC = take(4096)
    O_STAT = take(128)
    O_SEV = take(1024)
    O_IDSW = take(1024)
    O_WBUF2 = take(8192)
    P123_END = _o[0]
    assert P123_END <= O_P + P_SZ

    XNT_PEER = bsl(O_XNT_PEER, NFC * 1024)
    COSK = fsl(O_COSK, 2048)
    SINK = fsl(O_SINK, 2048)
    COSQ = fsl(O_COSQ, 1024)
    SINQ = fsl(O_SINQ, 1024)
    MASKS_SB = bsl(O_MASK, 8 * 512)
    KT = bsl(O_KT, 4 * 2048)
    VO = bsl(O_VO, 16 * 8 * 65)
    QT = bsl(O_QT, NFC * 512)
    AOT = bsl(O_AOT, NFC * 512)
    WBUFS = [bsl(O_WBUF, 8 * 512), bsl(O_WBUF2, 8 * 512)]
    O_WBUFS = [O_WBUF, O_WBUF2]
    WBUF = WBUFS[0]
    WVBUF = bsl(O_QT, NFC * 512)
    XLNS = [fsl(O_XLN, 2048), fsl(O_WBUF2, 2048)]
    O_XLNS = [O_XLN, O_WBUF2]
    XLN = XLNS[0]
    XNBS = [bsl(O_XNB, 2048), bsl(O_RTMP, 2048)]
    O_XNBS = [O_XNB, O_RTMP]
    XNB = XNBS[0]
    KRAW = fsl(O_RTMP, 512)
    RT0 = fsl(O_RTMP + 2048, 512)
    RT1 = fsl(O_RTMP + 4096, 512)
    PB = [bsl(O_PB + i * 1024, 512) for i in range(2)] + \
         [bsl(O_XLN + i * 1024, 512) for i in range(2)]
    O_PBS = [O_PB, O_PB + 1024, O_XLN, O_XLN + 1024]
    RECROWS = [fsl(O_REC, 512, 0, 1), fsl(O_RTMP, 512, 0, 1)]
    RECBCS = [fsl(O_REC + 2048, 512), fsl(O_RTMP + 2048, 512)]
    O_RECS = [O_REC, O_RTMP]
    ST_SUM = fsl(O_STAT, 1)
    ST_SQ = fsl(O_STAT + 16, 1)
    ST_MEAN = fsl(O_STAT + 32, 1)
    ST_VAR = fsl(O_STAT + 48, 1)
    ST_SD = fsl(O_STAT + 64, 1)
    ST_RSTD = fsl(O_STAT + 80, 1)
    SEV = bsl(O_SEV, 512)
    IDENT_SB = bsl(O_IDSW, 128)
    SWAP_SB = bsl(O_IDSW + 256, 128)

    _o[0] = O_P
    O_WG = take(16384)
    O_WU = take(16384)
    O_WD = take(32768)
    O_INT = take(16384)
    O_PK = take(8192)
    O_SC = take(8192)
    O_DQ = take(12288)
    O_FACC = take(65536)
    P4_END = _o[0]
    assert P4_END <= O_P + P_SZ

    WG = bsl(O_WG, NFC * 512)
    WU = bsl(O_WU, NFC * 512)
    WD = [bsl(O_WD + g * 16384, UCH * 2048) for g in range(GRP)]
    INTER = [bsl(O_INT + g * 8192, UCH * 1024) for g in range(GRP)]
    PK = [isl(O_PK + i * 4096, 2048) for i in range(2)]
    SC = [bsl(O_SC + i * 4096, 2048) for i in range(2)]
    DQ_M = isl(O_DQ, 2048)
    DQ_T1 = isl(O_DQ + 4096, 2048)
    DQ_T2 = isl(O_DQ + 8192, 2048)

    def facc(tc):
        return fsl(O_FACC + tc * 8192, 2048)

    nc.dbg_offsets = dict(
        XNT_OWN=O_XNT_OWN, XNT_PEER=O_XNT_PEER, KT=O_KT, VO=O_VO, QT=O_QT,
        AOT=O_AOT, FACC=O_FACC, COSK=O_COSK, SINK=O_SINK, MASK=O_MASK)

    SIL = fsl(O_WG, 2048)
    XOUT = fsl(O_WG + 8192, 2048)
    SBF5 = bsl(O_WU, 2048)
    XIN5 = fsl(O_WU + 4096, 2048)

    ps_cm = nc.psum_tensor("PS", [128, 4096], F32)
    psf = ps_cm.__enter__().ap()
    psb = psf.bitcast(BF16)

    def bank(i, n=1):
        return psf[:, 512 * i: 512 * (i + n)]

    def bankb128(i):
        return psb[:, 1024 * i: 1024 * i + 128]

    def R(off, nbytes):
        return ("sb", off, off + nbytes)

    def RP(i, n=1):
        return ("ps", 2048 * i, 2048 * (i + n))

    bld = Builder()
    op = bld.op

    def dma_sp(dst, src, r=(), w=()):
        return op("sync", lambda h, d=dst, s=src: h.dma_start(out=d, in_=s),
                  r=r, w=w, dma=True)

    def dma_gp(dst, src, r=(), w=()):
        return op("gpsimd", lambda h, d=dst, s=src: h.dma_start(out=d, in_=s),
                  r=r, w=w, dma=True)

    op("vector", lambda h: h.memset(c_expb, -EXPC), w=[R(O_CONST, 16)])
    op("vector", lambda h: h.memset(c_exps, 0.125), w=[R(O_CONST + 16, 16)])
    op("vector", lambda h: h.memset(c_eps, 1e-5), w=[R(O_CONST + 32, 16)])
    op("vector", lambda h: h.memset(c_one64, 1.0), w=[R(O_CONST + 48, 256)])
    op("vector", lambda h: h.memset(c_smask, -32768), w=[R(O_CONST + 304, 16)])
    op("vector", lambda h: h.memset(c_seven, 7), w=[R(O_CONST + 320, 16)])
    op("vector", lambda h: h.memset(c_half, 0.5), w=[R(O_CONST + 336, 16)])
    op("vector", lambda h: h.memset(c_neg2, -2.0), w=[R(O_CONST + 352, 16)])
    op("vector", lambda h: h.memset(c_neg6, -6.0), w=[R(O_CONST + 368, 16)])
    op("vector", lambda h: h.memset(c_one1, 1.0), w=[R(O_CONST + 384, 16)])
    op("vector", lambda h: h.memset(
        gab[0:128, O_VO // 2 + 64: O_VO // 2 + 16 * 8 * 65: 65], 1.0),
       w=[R(O_VO, 16 * 8 * 65 * 2)])

    dma_sp(IDENT_SB, ident_in, w=[R(O_IDSW, 256)])
    dma_sp(SWAP_SB, swap_in, w=[R(O_IDSW + 256, 256)])
    dma_gp(COSK, cosk, w=[R(O_COSK, 8192)])
    dma_gp(SINK, sink, w=[R(O_SINK, 8192)])
    dma_gp(COSQ, cosq, w=[R(O_COSQ, 4096)])
    dma_gp(SINQ, sinq, w=[R(O_SINQ, 4096)])

    def _truncate():
        for i, (off, nbytes) in enumerate(dbg or []):
            dt = nc.dram_tensor("dbg%d" % i, [128, nbytes // 4], F32,
                                kind="ExternalOutput").ap()
            dma_sp(dt, fsl(off, nbytes // 4), r=[R(off, nbytes)])
        if dbg_s:
            ds = nc.dram_tensor("dbgs", [OWN, D], BF16,
                                kind="ExternalOutput").ap()
            dma_gp(ds, s_dram, r=[("sdram", 0, 1)])
        dma_gp(out, x_pref[1024:2048, :])
        bld.check()
        _emit(nc, bld)
        return nc

    if limit <= 0:
        return _truncate()

    INV_D = 1.0 / D
    for ti in range(PRE // 128):
        bi = ti % 2
        xln, r_xln = XLNS[bi], R(O_XLNS[bi], 8192)
        xnb, r_xnb = XNBS[bi], R(O_XNBS[bi], 4096)
        dma_sp(xln, x_pref[ti * 128:(ti + 1) * 128, :], w=[r_xln])
        op("vector", lambda h, x_=xln: h.reduce_sum(ST_SUM, x_, axis=AX.X),
           r=[r_xln], w=[R(O_STAT, 16)])
        op("scalar", lambda h, x_=xln, n_=xnb: h.activation(
            n_, x_, AF.Square, accum_out=ST_SQ),
           r=[r_xln], w=[r_xnb, R(O_STAT + 16, 16)])
        op("vector", lambda h: h.tensor_scalar(ST_MEAN, ST_SUM, INV_D, None, ALU.mult),
           r=[R(O_STAT, 16)], w=[R(O_STAT + 32, 16)])
        op("vector", lambda h: h.tensor_scalar(ST_VAR, ST_SQ, INV_D, None, ALU.mult),
           r=[R(O_STAT + 16, 16)], w=[R(O_STAT + 48, 16)])
        op("vector", lambda h: h.tensor_tensor(ST_SD, ST_MEAN, ST_MEAN, ALU.mult),
           r=[R(O_STAT + 32, 16)], w=[R(O_STAT + 64, 16)])
        op("vector", lambda h: h.tensor_sub(ST_VAR, ST_VAR, ST_SD),
           r=[R(O_STAT + 48, 32)], w=[R(O_STAT + 48, 16)])
        op("vector", lambda h: h.tensor_tensor(ST_VAR, ST_VAR, c_eps, ALU.add),
           r=[R(O_STAT + 48, 16), R(O_CONST + 32, 16)], w=[R(O_STAT + 48, 16)])
        op("scalar", lambda h: h.activation(ST_SD, ST_VAR, AF.Sqrt),
           r=[R(O_STAT + 48, 16)], w=[R(O_STAT + 64, 16)])
        op("vector", lambda h: h.reciprocal(ST_RSTD, ST_SD),
           r=[R(O_STAT + 64, 16)], w=[R(O_STAT + 80, 16)])
        op("vector", lambda h, x_=xln, n_=xnb: h.tensor_scalar(
            n_, x_, ST_MEAN, ST_RSTD, ALU.subtract, ALU.mult),
           r=[r_xln, R(O_STAT + 32, 16), R(O_STAT + 80, 16)], w=[r_xnb])
        dst_off = O_XNT_PEER if ti < 8 else O_XNT_OWN
        dst = XNT_PEER if ti < 8 else XNT_OWN
        col = (ti % 8) * 128
        for fc in range(NFC):
            if fc < 8:
                src = xnb.rearrange("p (j two) -> p two j", two=2)[:, 0, 128 * fc: 128 * fc + 128]
            else:
                src = xnb.rearrange("p (j two) -> p two j", two=2)[:, 1, 128 * (fc - 8): 128 * (fc - 8) + 128]
            pdst = bankb128(fc % 2)
            op("tensor", lambda h, s=src, p=pdst: h.transpose(p, s, IDENT_SB),
               r=[r_xnb, R(O_IDSW, 256)], w=[RP(fc % 2)])
            op("vector",
               lambda h, p=pdst, dd=dst, c=col, f=fc: h.tensor_copy(
                   dd[:, 1024 * f + c: 1024 * f + c + 128], p),
               r=[RP(fc % 2)],
               w=[R(dst_off + (1024 * fc + col) * 2, 256)])

    if limit <= 1:
        return _truncate()

    def xnt_rhs(fc, t0, width):
        if t0 + width <= 1024:
            return XNT_PEER[:, 1024 * fc + t0: 1024 * fc + t0 + width]
        assert t0 >= 1024
        return XNT_OWN[:, 1024 * fc + (t0 - 1024): 1024 * fc + (t0 - 1024) + width]

    def xnt_r(fc, t0, width):
        if t0 + width <= 1024:
            return R(O_XNT_PEER + (1024 * fc + t0) * 2, width * 2)
        return R(O_XNT_OWN + (1024 * fc + (t0 - 1024)) * 2, width * 2)

    R_XNT_ALL = [R(O_XNT_PEER, 32768), R(O_XNT_OWN, 32768)]

    R_WBUF = R(O_WBUF, 8192)
    R_XNB = R(O_XNB, 4096)
    R_KRAW = R(O_RTMP, 2048)
    R_RT0 = R(O_RTMP + 2048, 2048)
    R_RT1 = R(O_RTMP + 4096, 2048)

    def rope_store(pbank_i, cos_t, cos_off, sin_t, sin_off, tsz, col0, width,
                   dst, dst_r):
        pk = bank(pbank_i)[:, 0:width]
        if sub != "krope2b":
            op("scalar", lambda h: h.activation(KRAW[:, 0:width], pk, AF.Copy),
               r=[RP(pbank_i)], w=[R_KRAW])
        if sub == "krope1":
            op("vector", lambda h: h.tensor_copy(dst, KRAW[:, 0:width]),
               r=[R_KRAW], w=[dst_r])
            return
        op("vector", lambda h: h.tensor_copy(XNB[:, 0:width], pk),
           r=[RP(pbank_i)], w=[R_XNB])
        swm = IDENT_SB if sub == "krope2c" else SWAP_SB
        obank = 3 if sub == "krope2d" else 7
        swb = bank(obank)[:, 0:width]
        op("tensor", lambda h: h.matmul(swb, swm, XNB[:, 0:width],
                                        start=True, stop=True),
           r=[R_XNB, R(O_IDSW, 512)], w=[RP(obank)])
        if sub in ("krope2", "krope2b", "krope2c", "krope2d"):
            op("vector", lambda h: h.tensor_copy(dst, swb),
               r=[RP(obank)], w=[dst_r])
            return
        op("vector", lambda h: h.tensor_tensor(
            RT0[:, 0:width], KRAW[:, 0:width],
            cos_t[:, col0:col0 + width], ALU.mult),
           r=[R_KRAW, R(cos_off, tsz)], w=[R_RT0])
        op("vector", lambda h: h.tensor_tensor(
            RT1[:, 0:width], swb, sin_t[:, col0:col0 + width], ALU.mult),
           r=[RP(7), R(sin_off, tsz)], w=[R_RT1])
        op("vector", lambda h: h.tensor_add(dst, RT0[:, 0:width], RT1[:, 0:width]),
           r=[R_RT0, R_RT1], w=[dst_r])

    ndc, nt4 = (1, 1) if (sub or "").endswith("a") else (4, 4)
    if (sub or "").endswith("a"):
        sub = sub[:-1]
    if sub != "vonly":
        for dc in range(ndc):
            wb = dc % 2
            dma_gp(WBUFS[wb][:, 0:16 * 128],
                   wk[:, dc * 128:(dc + 1) * 128].rearrange("(c p) d -> p c d", p=128),
                   w=[R(O_WBUFS[wb], 8192)])
            for t4 in range(nt4):
                pb = 2 + (dc * 4 + t4) % 2
                for fc in range(NFC):
                    op("tensor", lambda h, f=fc, tt=t4 * 512, b=pb, w_=wb: h.matmul(
                        bank(b)[:, 0:512], WBUFS[w_][:, f * 128:(f + 1) * 128],
                        xnt_rhs(f, tt, 512),
                        start=(f == 0), stop=(f == NFC - 1)),
                       r=[R(O_WBUFS[wb], 8192), xnt_r(fc, t4 * 512, 512)],
                       w=[RP(pb)])
                if sub == "kproj":
                    op("vector", lambda h, dc_=dc, t_=t4, b=pb: h.tensor_copy(
                        KT[:, dc_ * 2048 + t_ * 512: dc_ * 2048 + t_ * 512 + 512],
                        bank(b)[:, 0:512]),
                       r=[RP(pb)],
                       w=[R(O_KT + (dc * 2048 + t4 * 512) * 2, 1024)])
                else:
                    rope_store(pb, COSK, O_COSK, SINK, O_SINK, 8192, t4 * 512, 512,
                               KT[:, dc * 2048 + t4 * 512: dc * 2048 + t4 * 512 + 512],
                               R(O_KT + (dc * 2048 + t4 * 512) * 2, 1024))

    R_WVBUF = R(O_QT, 16384)
    if sub not in ("kproj", "krope"):
        dma_gp(WVBUF, wv.rearrange("(c p) d -> p c d", p=128), w=[R_WVBUF])
        for tcv in range(16):
            vb = 2 + tcv % 2
            for fc in range(NFC):
                op("tensor", lambda h, f=fc, t=tcv, b=vb: h.matmul(
                    bank(b)[:, 0:512], xnt_rhs(f, t * 128, 128),
                    WVBUF[:, f * 512:(f + 1) * 512],
                    start=(f == 0), stop=(f == NFC - 1)),
                   r=[R_WVBUF, xnt_r(fc, tcv * 128, 128)], w=[RP(vb)])
            for g in range(8):
                op("scalar", lambda h, t=tcv, gg=g, b=vb: h.activation(
                    VO[:, (t * 8 + gg) * 65: (t * 8 + gg) * 65 + 64],
                    bank(b)[:, gg * 64: gg * 64 + 64], AF.Copy),
                   r=[RP(vb)], w=[R(O_VO + (tcv * 8 + g) * 65 * 2, 128)])

    if limit <= 2:
        return _truncate()

    R_QT = R(O_QT, 16384)
    R_AOT = R(O_AOT, 16384)
    R_MASK = R(O_MASK, 8192)
    R_REC = R(O_REC, 2048)
    R_RECBC = R(O_REC + 2048, 2048)
    R_SEV = R(O_SEV, 1024)

    for slot in range(2):
        q0 = slot * 512
        nkt = len(KT_LISTS[slot])
        mask_of = {kt: mi for mi, kt in enumerate(MASKED[slot])}
        dma_sp(MASKS_SB,
               masks_in.rearrange("(m p) n -> p m n", p=128)
               [:, slot * 8:(slot + 1) * 8, :],
               w=[R_MASK])
        for s16 in range(NFC):
            wb = s16 % 2
            pb = 2 + s16 % 2
            dma_gp(WBUFS[wb][:, 0:16 * 128],
                   wq[:, s16 * 128:(s16 + 1) * 128].rearrange("(c p) d -> p c d", p=128),
                   w=[R(O_WBUFS[wb], 8192)])
            for fc in range(NFC):
                op("tensor", lambda h, f=fc, qq=q0, b=pb, w_=wb: h.matmul(
                    bank(b)[:, 0:512], WBUFS[w_][:, f * 128:(f + 1) * 128],
                    XNT_OWN[:, 1024 * f + qq: 1024 * f + qq + 512],
                    start=(f == 0), stop=(f == NFC - 1)),
                   r=[R(O_WBUFS[wb], 8192), xnt_r(fc, 1024 + q0, 512)],
                   w=[RP(pb)])
            rope_store(pb, COSQ, O_COSQ, SINQ, O_SINQ, 4096, q0, 512,
                       QT[:, s16 * 512:(s16 + 1) * 512],
                       R(O_QT + s16 * 512 * 2, 1024))

        for s16 in range(NFC):
            for odd in range(2):
                H = (s16 // 4) * 8 + s16 % 4 + 4 * odd
                g = H // 4
                rb = 64 * (g % 2)
                assert rb == 64 * odd
                avb = 4 + odd
                qsl = QT[rb: rb + 64, s16 * 512:(s16 + 1) * 512]
                r_qsl = R(O_QT + s16 * 1024, 1024)
                for kt_i, kt in enumerate(KT_LISTS[slot]):
                    sb = 2 + (kt_i % 2)
                    kof = (g // 2) * 2048 + kt * 128
                    ksl = KT[rb: rb + 64, kof: kof + 128]
                    op("tensor", lambda h, p=sb, k=ksl, q=qsl: h.matmul(
                        bank(p)[:, 0:512], k, q, start=True, stop=True),
                       r=[R(O_KT + kof * 2, 256), r_qsl], w=[RP(sb)])
                    pbi = (2 * kt_i + odd) % 4
                    pbuf = PB[pbi]
                    r_pb = R(O_PBS[pbi], 1024)
                    op("scalar", lambda h, d=pbuf, p=sb: h.activation(
                        d, bank(p)[:, 0:512], AF.Exp, bias=c_expb, scale=c_exps),
                       r=[RP(sb), R(O_CONST, 32)], w=[r_pb])
                    if kt in mask_of:
                        mi = mask_of[kt]
                        op("gpsimd", lambda h, d=pbuf, m=mi: h.tensor_tensor(
                            d, d, MASKS_SB[:, m * 512:(m + 1) * 512], ALU.mult),
                           r=[r_pb, R_MASK], w=[r_pb])
                    st = (kt_i == 0)
                    sp = (kt_i == nkt - 1)
                    vbase = (kt * 8 + g) * 65
                    if odd == 0:
                        op("tensor", lambda h, st_=st, sp_=sp, vb=vbase, d=pbuf,
                           b=avb: h.matmul(bank(b)[0:65, 0:512],
                                           VO[:, vb: vb + 65], d,
                                           start=st_, stop=sp_),
                           r=[R(O_VO + vbase * 2, 130), r_pb], w=[RP(avb)])
                    else:
                        op("tensor", lambda h, st_=st, sp_=sp, vb=vbase, d=pbuf,
                           b=avb: h.matmul(bank(b)[64:128, 0:512],
                                           VO[:, vb: vb + 64], d,
                                           start=st_, stop=sp_),
                           r=[R(O_VO + vbase * 2, 130), r_pb], w=[RP(avb)])
                        op("tensor", lambda h, st_=st, sp_=sp, vb=vbase, d=pbuf,
                           b=avb: h.matmul(bank(b)[0:1, 0:512],
                                           VO[:, vb + 64: vb + 65], d,
                                           start=st_, stop=sp_),
                           r=[R(O_VO + vbase * 2, 130), r_pb], w=[RP(avb)])
                bcb = 6 if odd == 0 else 1
                recrow, recbc = RECROWS[odd], RECBCS[odd]
                r_rec = R(O_RECS[odd], 2048)
                r_recbc = R(O_RECS[odd] + 2048, 2048)
                sum_row = bank(avb)[64:65, 0:512] if odd == 0 else bank(avb)[0:1, 0:512]
                av_rows = bank(avb)[0:64, 0:512] if odd == 0 else bank(avb)[64:128, 0:512]
                op("vector", lambda h, sr=sum_row, rr=recrow: h.reciprocal(rr, sr),
                   r=[RP(avb)], w=[r_rec])
                op("tensor", lambda h, r0=rb, b=bcb, rr=recrow: h.matmul(
                    bank(b)[r0:r0 + 64, 0:512], c_one64, rr,
                    start=True, stop=True),
                   r=[r_rec, R(O_CONST + 48, 256)], w=[RP(bcb)])
                op("scalar", lambda h, r0=rb, b=bcb, rc=recbc: h.activation(
                    rc[r0:r0 + 64, :], bank(b)[r0:r0 + 64, 0:512], AF.Copy),
                   r=[RP(bcb)], w=[r_recbc])
                op("vector", lambda h, av=av_rows, r0=rb, s=s16, rc=recbc:
                   h.tensor_tensor(
                       AOT[r0:r0 + 64, s * 512:(s + 1) * 512], av,
                       rc[r0:r0 + 64, :], ALU.mult),
                   r=[RP(avb), r_recbc], w=[R(O_AOT + s16 * 1024, 1024)])

        for fp in range(4):
            for half in range(2):
                wb = (fp * 2 + half) % 2
                dma_gp(WBUFS[wb][:, 0:8 * 512],
                       wo[1024 * half: 1024 * half + 1024, fp * 512:(fp + 1) * 512]
                       .rearrange("(c p) d -> p c d", p=128),
                       w=[R(O_WBUFS[wb], 8192)])
                for tch in range(4):
                    for dc8 in range(8):
                        aof = (half * 8 + dc8) * 512 + tch * 128
                        op("tensor", lambda h, d=dc8, hf=half, t=tch, w_=wb,
                           ao=aof: h.matmul(
                               bank(t)[:, 0:512],
                               AOT[:, ao: ao + 128],
                               WBUFS[w_][:, d * 512:(d + 1) * 512],
                               start=(hf == 0 and d == 0), stop=(hf == 1 and d == 7)),
                           r=[R(O_AOT + aof * 2, 256), R(O_WBUFS[wb], 8192)],
                           w=[RP(tch)])
            for tch in range(4):
                op("scalar", lambda h, t=tch: h.activation(
                    SEV, bank(t)[:, 0:512], AF.Copy),
                   r=[RP(tch)], w=[R_SEV])
                dma_gp(s_dram[q0 + tch * 128: q0 + tch * 128 + 128,
                              fp * 512:(fp + 1) * 512],
                       SEV,
                       r=[R_SEV], w=[("sdram", 0, 1)])

    if limit <= 3:
        return _truncate()

    R_PK = [R(O_PK + i * 4096, 4096) for i in range(2)]
    R_SC = [R(O_SC + i * 4096, 4096) for i in range(2)]
    R_DQM = R(O_DQ, 4096)
    R_DQ1 = R(O_DQ + 4096, 4096)
    R_DQ2 = R(O_DQ + 8192, 4096)
    R_WG = R(O_WG, 16384)
    R_WU = R(O_WU, 16384)

    def load_packed_gu(src_pk, src_sc, kb, slot, hi):
        c0 = 8192 * hi + 256 * kb
        dma_sp(PK[slot].rearrange("p (r j) -> p r j", j=256),
               src_pk.rearrange("(r p) u -> p r u", p=128)[:, :, c0:c0 + 256],
               w=[R_PK[slot]])
        dma_gp(SC[slot],
               src_sc.rearrange("e (c n) -> e c n", n=2048)[:, 32 * hi + kb]
               .unsqueeze(1).broadcast_to([8, 16, 2048]),
               w=[R_SC[slot]])

    def load_packed_d(kb, rt, slot):
        dma_sp(PK[slot],
               dpk.rearrange("(r p) f -> p r f", p=128)[:, 2 * kb + rt, :],
               w=[R_PK[slot]])
        dma_gp(SC[slot],
               dsc.rearrange("e (c n) -> e c n", n=2048)[:, 2 * kb + rt]
               .unsqueeze(1).broadcast_to([8, 16, 2048]),
               w=[R_SC[slot]])

    DQ_M = isl(O_DQ, 2048)
    DQ_T1 = isl(O_DQ + 4096, 2048)
    DQ_T2 = isl(O_DQ + 8192, 2048)
    R_DQM = R(O_DQ, 4096)
    R_DQ1 = R(O_DQ + 4096, 4096)
    R_DQ2 = R(O_DQ + 8192, 4096)

    fillers = []

    def fill(n=1):
        for _ in range(min(n, len(fillers))):
            fillers.pop(0)()

    def flush_fill():
        while fillers:
            fillers.pop(0)()

    def dequant(dst_lo, dst_hi, r_dst, slot, grouped=False):
        if grouped:
            t1b = DQ_T1.bitcast(BF16).rearrange("p (r j) -> p r j", j=256)
            scv = SC[slot].rearrange("p (r j) -> p r j", j=256)
        else:
            t1b = DQ_T1.bitcast(BF16)
            scv = SC[slot]
        for hi, dsti, w_dst in ((0, dst_lo, r_dst[0]), (1, dst_hi, r_dst[1])):
            if hi == 0:
                op("vector", lambda h: h.tensor_single_scalar(
                    DQ_M, PK[slot], 7, ALU.bitwise_and),
                   r=[R(O_PK + slot * 4096, 4096)], w=[R_DQM])
            else:
                op("vector", lambda h: h.tensor_scalar(
                    DQ_M, PK[slot], 4, 7,
                    ALU.logical_shift_right, ALU.bitwise_and),
                   r=[R(O_PK + slot * 4096, 4096)], w=[R_DQM])
            op("vector", lambda h: h.tensor_scalar(
                DQ_T1, DQ_M, 1, float(0x3F00), ALU.min, ALU.mult),
               r=[R_DQM], w=[R_DQ1])
            op("vector", lambda h: h.tensor_scalar(
                DQ_T2, DQ_M, 1, float(0x80), ALU.max, ALU.mult),
               r=[R_DQM], w=[R_DQ2])
            op("vector", lambda h: h.tensor_add(DQ_T1, DQ_T1, DQ_T2),
               r=[R_DQ1, R_DQ2], w=[R_DQ1])
            fill(1)
            op("vector", lambda h: h.tensor_scalar(
                DQ_T2, DQ_M, 2, float(-0x40), ALU.max, ALU.mult),
               r=[R_DQM], w=[R_DQ2])
            op("vector", lambda h: h.tensor_add(DQ_T1, DQ_T1, DQ_T2),
               r=[R_DQ1, R_DQ2], w=[R_DQ1])
            op("vector", lambda h, sh=12 - 4 * hi: h.tensor_scalar(
                DQ_T2, PK[slot], sh, c_smask,
                ALU.logical_shift_left, ALU.bitwise_and),
               r=[R(O_PK + slot * 4096, 4096), R(O_CONST + 304, 16)],
               w=[R_DQ2])
            op("vector", lambda h: h.tensor_tensor(
                DQ_T1, DQ_T1, DQ_T2, ALU.bitwise_or),
               r=[R_DQ1, R_DQ2], w=[R_DQ1])
            op("vector", lambda h, dd=dsti: h.tensor_tensor(dd, t1b, scv, ALU.mult),
               r=[R_DQ1, R(O_SC + slot * 4096, 4096)], w=w_dst)
            fill(1)

    def w4(W):
        return W.rearrange("p (fc half j) -> p fc half j", fc=16, half=2)

    seq = 0
    ffn_sub = sub if (sub or "").startswith("ffn") else None
    nbk_run = {None: NBK, "ffn1": 1, "ffn2": 1, "ffn3": 2}[ffn_sub]
    do_mm = ffn_sub != "ffn1"
    for kb in range(nbk_run):
        gi = kb % GRP
        for hi in range(2):
            load_packed_gu(gpk, gsc, kb, seq % 2, hi)
            wv_ = w4(WG)
            dequant(wv_[:, 0:8, hi, :], wv_[:, 8:16, hi, :],
                    ([R(O_WG + (fc * 512 + hi * 256) * 2, 512) for fc in range(8)],
                     [R(O_WG + (fc * 512 + hi * 256) * 2, 512) for fc in range(8, 16)]),
                    seq % 2, grouped=True)
            seq += 1
        flush_fill()
        for u in (range(UCH) if do_mm else []):
            for t2 in range(2):
                bk_i = (2 * u + t2) % 2
                for fc in range(NFC):
                    op("tensor", lambda h, p=bk_i, f=fc, uu=u, tt=t2: h.matmul(
                        bank(p)[:, 0:512],
                        WG[:, f * 512 + uu * 128: f * 512 + uu * 128 + 128],
                        XNT_OWN[:, 1024 * f + tt * 512: 1024 * f + tt * 512 + 512],
                        start=(f == 0), stop=(f == NFC - 1)),
                       r=[R_WG, R(O_XNT_OWN, 32768)], w=[RP(bk_i)])
                op("scalar", lambda h, p=bk_i, uu=u, tt=t2, g_=gi: h.activation(
                    INTER[g_][:, uu * 1024 + tt * 512: uu * 1024 + tt * 512 + 512],
                    bank(p)[:, 0:512], AF.Gelu_apprx_tanh),
                   r=[RP(bk_i)],
                   w=[R(O_INT + gi * 8192 + (u * 1024 + t2 * 512) * 2, 1024)])
        for hi in range(2):
            load_packed_gu(upk, usc, kb, seq % 2, hi)
            wv_ = w4(WU)
            dequant(wv_[:, 0:8, hi, :], wv_[:, 8:16, hi, :], ([R(O_WU + (fc * 512 + hi * 256) * 2, 512) for fc in range(8)],
                     [R(O_WU + (fc * 512 + hi * 256) * 2, 512) for fc in range(8, 16)]),
                    seq % 2, grouped=True)
            seq += 1
        for rt in range(2):
            load_packed_d(kb, rt, seq % 2)
            dequant(WD[gi][:, rt * 2048: rt * 2048 + 2048],
                    WD[gi][:, (2 + rt) * 2048: (2 + rt) * 2048 + 2048],
                    ([R(O_WD + gi * 16384 + rt * 4096, 4096)],
                     [R(O_WD + gi * 16384 + (2 + rt) * 4096, 4096)]), seq % 2)
            seq += 1
        flush_fill()
        for u in (range(UCH) if do_mm else []):
            for t2 in range(2):
                bk_i = 2 + (2 * u + t2) % 2
                for fc in range(NFC):
                    op("tensor", lambda h, p=bk_i, f=fc, uu=u, tt=t2: h.matmul(
                        bank(p)[:, 0:512],
                        WU[:, f * 512 + uu * 128: f * 512 + uu * 128 + 128],
                        XNT_OWN[:, 1024 * f + tt * 512: 1024 * f + tt * 512 + 512],
                        start=(f == 0), stop=(f == NFC - 1)),
                       r=[R_WU, R(O_XNT_OWN, 32768)], w=[RP(bk_i)])
                op("vector", lambda h, p=bk_i, uu=u, tt=t2, g_=gi: h.tensor_tensor(
                    INTER[g_][:, uu * 1024 + tt * 512: uu * 1024 + tt * 512 + 512],
                    INTER[g_][:, uu * 1024 + tt * 512: uu * 1024 + tt * 512 + 512],
                    bank(p)[:, 0:512], ALU.mult),
                   r=[RP(bk_i),
                      R(O_INT + gi * 8192 + (u * 1024 + t2 * 512) * 2, 1024)],
                   w=[R(O_INT + gi * 8192 + (u * 1024 + t2 * 512) * 2, 1024)])
        if gi == GRP - 1 and do_mm and ffn_sub != "ffn2":
            flush_fill()
            for tc in range(8):
                bset = 4 if tc % 2 == 0 else 0
                for g2 in range(GRP):
                    for u in range(UCH):
                        for fpp in range(4):
                            st = (g2 == 0 and u == 0)
                            spv = (g2 == GRP - 1 and u == UCH - 1)
                            op("tensor", lambda h, b=bset, g_=g2, uu=u, f=fpp,
                               t=tc, st_=st, sp_=spv: h.matmul(
                                   bank(b + f)[:, 0:512],
                                   INTER[g_][:, uu * 1024 + t * 128:
                                             uu * 1024 + t * 128 + 128],
                                   WD[g_][:, uu * 2048 + f * 512:
                                          uu * 2048 + f * 512 + 512],
                                   start=st_, stop=sp_),
                               r=[R(O_INT + g2 * 8192, 8192),
                                  R(O_WD + g2 * 16384, 16384)],
                               w=[RP(bset + fpp)])
                if kb == GRP - 1:
                    op("vector", lambda h, t=tc, b=bset: h.tensor_copy(
                        facc(t), bank(b, 4)),
                       r=[RP(bset, 4)], w=[R(O_FACC + tc * 8192, 8192)])
                else:
                    op("vector", lambda h, t=tc, b=bset: h.tensor_tensor(
                        facc(t), facc(t), bank(b, 4), ALU.add),
                       r=[RP(bset, 4), R(O_FACC + tc * 8192, 8192)],
                       w=[R(O_FACC + tc * 8192, 8192)])

    flush_fill()
    if limit <= 4:
        return _truncate()

    R_SIL = R(O_WG, 8192)
    R_XOUT = R(O_WG + 8192, 8192)
    R_SBF5 = R(O_WU, 4096)
    R_XIN5 = R(O_WU + 4096, 8192)
    for tc in range(8):
        dma_sp(SBF5, s_dram[tc * 128:(tc + 1) * 128, :],
               r=[("sdram", 0, 1)], w=[R_SBF5])
        dma_sp(XIN5, x_pref[1024 + tc * 128: 1024 + (tc + 1) * 128, :],
               w=[R_XIN5])
        op("scalar", lambda h, t=tc: h.activation(SIL, facc(t), AF.Silu),
           r=[R(O_FACC + tc * 8192, 8192)], w=[R_SIL])
        op("vector", lambda h: h.tensor_tensor(SIL, SIL, SBF5, ALU.mult),
           r=[R_SIL, R_SBF5], w=[R_SIL])
        op("vector", lambda h: h.tensor_tensor(XOUT, SIL, XIN5, ALU.add),
           r=[R_SIL, R_XIN5], w=[R_XOUT])
        dma_gp(out[tc * 128:(tc + 1) * 128, :], XOUT,
               r=[R_XOUT], w=[("outd", tc, tc + 1)])

    bld.check()
    _emit(nc, bld)
    return nc


def _emit(nc, bld):
    sem_cms = {}
    sems = {}
    for n in ENGINES:
        sem_cms[n] = nc.semaphore(n)
        sems[n] = sem_cms[n].__enter__()
    bld.emit(nc, sems, tail=("gpsimd", "sync"))
    for cm in sem_cms.values():
        cm.__exit__(None, None, None)


_PERMF = np.concatenate([np.arange(0, D, 2), np.arange(1, D, 2)])
_PERMU = np.concatenate([np.arange(0, MLP, 2), np.arange(1, MLP, 2)])


def _own_rows(core):
    b, p = core // 2, core % 2
    if p == 0:
        own = np.concatenate([np.arange(0, 512), np.arange(1536, 2048)])
    else:
        own = np.arange(512, 1536)
    return b, own


def _head_pair(s):
    he = (s // 4) * 8 + s % 4
    return he, he + 4


def _rope_tables(pos):
    inv = 1.0 / ROPE_BASE ** (np.arange(0, 64, 2, dtype=np.float64) / 64.0)
    d = np.arange(128)
    ang = pos[None, :].astype(np.float64) * inv[d % 32][:, None]
    cos = np.cos(ang).astype(np.float32)
    sin = np.sin(ang).astype(np.float32)
    sign = np.where((d % 64) < 32, -1.0, 1.0).astype(np.float32)
    return np.ascontiguousarray(cos), np.ascontiguousarray(sin * sign[:, None])


def _build_masks(core):
    p = core % 2
    mk = np.zeros((16, 128, 512), dtype=BF16NP)
    diag = (np.arange(512)[:, None] <= np.arange(512)[None, :])
    for slot in range(2):
        for mi, kt in enumerate(MASKED[slot]):
            m = slot * 8 + mi
            if mi < 4:
                if slot == 0:
                    mk[m] = 1.0 if p == 1 else 0.0
                else:
                    mk[m] = 1.0 if p == 0 else 0.0
            else:
                j = mi - 4
                mk[m] = diag[j * 128:(j + 1) * 128, :].astype(BF16NP)
    return np.ascontiguousarray(mk.reshape(16 * 128, 512))


def prepare_in_maps(L0_attn_x, L0_attn_wq, L0_attn_wk, L0_attn_wv, L0_attn_wo,
                    L0_ffn_gate_W_packed, L0_ffn_gate_scales,
                    L0_ffn_up_W_packed, L0_ffn_up_scales,
                    L0_ffn_down_W_packed, L0_ffn_down_scales):
    X = np.ascontiguousarray(np.asarray(L0_attn_x, dtype=np.float32))
    qcolmap = np.concatenate([
        np.concatenate([np.arange(he * 64, he * 64 + 64),
                        np.arange(ho * 64, ho * 64 + 64)])
        for he, ho in (_head_pair(s) for s in range(16))])
    wq = np.ascontiguousarray(
        np.asarray(L0_attn_wq, dtype=np.float32)[_PERMF][:, qcolmap])
    wk = np.ascontiguousarray(np.asarray(L0_attn_wk, dtype=np.float32)[_PERMF])
    wv = np.ascontiguousarray(np.asarray(L0_attn_wv, dtype=np.float32)[_PERMF])
    womap = np.concatenate([
        np.concatenate([np.arange(he * 64, he * 64 + 64),
                        np.arange(ho * 64, ho * 64 + 64)])
        for he, ho in (_head_pair(s) for s in range(16))])
    wo = np.ascontiguousarray(np.asarray(L0_attn_wo, dtype=np.float32)[womap])

    gp = np.asarray(L0_ffn_gate_W_packed, dtype=np.int32)[_PERMU]
    up = np.asarray(L0_ffn_up_W_packed, dtype=np.int32)[_PERMU]
    dn = np.asarray(L0_ffn_down_W_packed, dtype=np.int32)
    gpk = np.ascontiguousarray(gp.view(np.int16)[:, 0::2].T)
    upk = np.ascontiguousarray(up.view(np.int16)[:, 0::2].T)
    dpk = np.ascontiguousarray(dn.view(np.int16)[:, 0::2].T)

    def _retile_gu(sc):
        t = np.asarray(sc, np.float32)[_PERMU].T.reshape(8, 8, 64, 256)
        return np.ascontiguousarray(t.transpose(1, 2, 0, 3).reshape(8, 64 * 2048))

    def _retile_d(sc):
        t = np.asarray(sc, np.float32).T.reshape(64, 8, 2048)
        return np.ascontiguousarray(t.transpose(1, 0, 2).reshape(8, 64 * 2048))

    gsc = _retile_gu(L0_ffn_gate_scales)
    usc = _retile_gu(L0_ffn_up_scales)
    dsc = _retile_d(L0_ffn_down_scales)

    ident = np.ascontiguousarray(np.eye(128, dtype=BF16NP))
    sw = np.zeros((128, 128), dtype=BF16NP)
    for i in range(128):
        sw[(i // 64) * 64 + (i % 64 + 32) % 64, i] = 1.0
    sw = np.ascontiguousarray(sw)

    in_maps = []
    for core in range(NCORE):
        b, own = _own_rows(core)
        other = np.setdiff1d(np.arange(L), own)
        order = np.concatenate([other, own])
        cosk, sink = _rope_tables(order)
        cosq, sinq = _rope_tables(own)
        in_maps.append({
            "x_pref": np.ascontiguousarray(X[b][order]),
            "wq": wq, "wk": wk, "wv": wv, "wo": wo,
            "gpk": gpk, "upk": upk, "dpk": dpk,
            "gsc": gsc, "usc": usc, "dsc": dsc,
            "cosk": cosk, "sink": sink, "cosq": cosq, "sinq": sinq,
            "masks": _build_masks(core),
            "ident": ident, "swapm": sw,
        })
    return in_maps


def kernel(**inputs):
    in_maps = prepare_in_maps(**inputs)
    nc = build_program()
    res = run_bass_kernel_spmd(nc, in_maps, core_ids=list(range(NCORE)))

    OUT = np.empty((B, L, D), dtype=np.float32)
    for core in range(NCORE):
        b, own = _own_rows(core)
        OUT[b][own] = res.results[core]["out"]
    return OUT


# version 09
# speedup vs baseline: 1.3702x; 1.0276x over previous
import numpy as np
import ml_dtypes
import concourse.bass as bass
import concourse.mybir as mybir
from concourse.bass_utils import run_bass_kernel_spmd

F32 = mybir.dt.float32
BF16 = mybir.dt.bfloat16
I16 = mybir.dt.int16
I32 = mybir.dt.int32
ALU = mybir.AluOpType
AF = mybir.ActivationFunctionType
AX = mybir.AxisListType
BF16NP = ml_dtypes.bfloat16

D = 2048
L = 2048
B = 4
NCORE = 8
OWN = 1024
PRE = 2048
NFC = 16
KT_LISTS = [[0, 1, 2, 3, 8, 9, 10, 11], list(range(16))]
MASKED = [[0, 1, 2, 3, 8, 9, 10, 11], [4, 5, 6, 7, 12, 13, 14, 15]]
MLP = 16384
NBK = 32
UCH = 4
GRP = 2
EXPC = 14.0
ROPE_BASE = 10000.0

ENGINES = ("sync", "gpsimd", "vector", "scalar", "tensor")


class Builder:
    def __init__(self):
        self.ops = []
        self.cnt = dict.fromkeys(ENGINES, 0)
        self.stores = {}

    def _store(self, sp):
        return self.stores.setdefault(sp, {"w": [], "r": []})

    def op(self, eng, fn, r=(), w=(), dma=False):
        inc = 16 if dma else 1
        deps = {}

        def dep(e, c):
            if e == eng and eng in ("tensor", "sync"):
                return
            if deps.get(e, 0) < c:
                deps[e] = c

        for (sp, lo, hi) in r:
            st = self._store(sp)
            for (wlo, whi, we, wc) in st["w"]:
                if wlo < hi and lo < whi:
                    dep(we, wc)
            if sp == "ps" and eng in ("scalar", "vector"):
                other = "vector" if eng == "scalar" else "scalar"
                for (rlo, rhi, dd) in st["r"]:
                    if rlo < hi and lo < rhi and other in dd:
                        dep(other, dd[other])
        for (sp, lo, hi) in w:
            st = self._store(sp)
            for (wlo, whi, we, wc) in st["w"]:
                if wlo < hi and lo < whi:
                    dep(we, wc)
            for (rlo, rhi, dd) in st["r"]:
                if rlo < hi and lo < rhi:
                    for e, c in dd.items():
                        dep(e, c)

        self.cnt[eng] += inc
        c_after = self.cnt[eng]

        for (sp, lo, hi) in w:
            st = self._store(sp)
            neww = []
            for (wlo, whi, we, wc) in st["w"]:
                if wlo < hi and lo < whi:
                    if wlo < lo:
                        neww.append((wlo, lo, we, wc))
                    if hi < whi:
                        neww.append((hi, whi, we, wc))
                else:
                    neww.append((wlo, whi, we, wc))
            neww.append((lo, hi, eng, c_after))
            st["w"] = neww
            newr = []
            for (rlo, rhi, dd) in st["r"]:
                if rlo < hi and lo < rhi:
                    if rlo < lo:
                        newr.append((rlo, lo, dict(dd)))
                    if hi < rhi:
                        newr.append((hi, rhi, dict(dd)))
                else:
                    newr.append((rlo, rhi, dd))
            st["r"] = newr
        for (sp, lo, hi) in r:
            st = self._store(sp)
            for ent in st["r"]:
                if ent[0] == lo and ent[1] == hi:
                    ent[2][eng] = c_after
                    break
            else:
                st["r"].append((lo, hi, {eng: c_after}))

        self.ops.append((eng, fn, deps, inc))
        return c_after

    def check(self):
        qs = {e: [o for o in self.ops if o[0] == e] for e in ENGINES}
        idx = dict.fromkeys(ENGINES, 0)
        done = dict.fromkeys(ENGINES, 0)
        progress = True
        while progress:
            progress = False
            for e in ENGINES:
                while idx[e] < len(qs[e]):
                    _, _, deps, inc = qs[e][idx[e]]
                    if all(done[de] >= dc for de, dc in deps.items()):
                        done[e] += inc
                        idx[e] += 1
                        progress = True
                    else:
                        break
        stuck = {e: (idx[e], len(qs[e])) for e in ENGINES if idx[e] < len(qs[e])}
        if stuck:
            for e, (i, n) in stuck.items():
                print("STUCK", e, "at", i, "of", n, "deps", qs[e][i][2])
            raise RuntimeError(f"build-time deadlock: {stuck}")
        return done

    def emit(self, nc, sems, tail):
        hw = {e: dict.fromkeys(ENGINES, 0) for e in ENGINES}
        with nc.Block() as block:
            for eng in ENGINES:
                def emit_one(h, eng=eng):
                    for (e, fn, deps, inc) in self.ops:
                        if e != eng:
                            continue
                        for de in ENGINES:
                            dc = deps.get(de, 0)
                            if dc > hw[eng][de]:
                                h.wait_ge(sems[de], dc)
                                hw[eng][de] = dc
                        inst = fn(h)
                        inst.then_inc(sems[eng], inc)
                    if eng in tail:
                        h.wait_ge(sems[eng], self.cnt[eng])
                getattr(block, eng)(emit_one)


def build_program(limit=99, dbg=None, dbg_s=False, sub=None):
    nc = bass.Bass("TRN2", target_bir_lowering=False, debug=False)

    def din(name, shape, dtype):
        return nc.dram_tensor(name, shape, dtype, kind="ExternalInput").ap()

    x_pref = din("x_pref", [PRE, D], F32)
    wq = din("wq", [D, D], F32)
    wk = din("wk", [D, 512], F32)
    wv = din("wv", [D, 512], F32)
    wo = din("wo", [D, D], F32)
    gpk = din("gpk", [1024, MLP], I16)
    upk = din("upk", [1024, MLP], I16)
    dpk = din("dpk", [MLP // 2, D], I16)
    gsc = din("gsc", [8, 64 * 2048], F32)
    usc = din("usc", [8, 64 * 2048], F32)
    dsc = din("dsc", [8, 64 * 2048], F32)
    cosk = din("cosk", [128, PRE], F32)
    sink = din("sink", [128, PRE], F32)
    cosq = din("cosq", [128, OWN], F32)
    sinq = din("sinq", [128, OWN], F32)
    masks_in = din("masks", [16 * 128, 512], BF16)
    ident_in = din("ident", [128, 128], BF16)
    swap_in = din("swapm", [128, 128], BF16)

    out = nc.dram_tensor("out", [OWN, D], F32, kind="ExternalOutput").ap()
    s_dram = nc.dram_tensor("s_spill", [OWN, D], BF16).ap()

    GA_EL = 52416
    ga_cm = nc.sbuf_tensor("GA", [128, GA_EL], F32)
    ga = ga_cm.__enter__().ap()
    gab = ga.bitcast(BF16)
    gai = ga.bitcast(I16)
    LIMIT = GA_EL * 4

    def fsl(off, nel, r0=0, r1=128):
        assert off % 4 == 0 and off + 4 * nel <= LIMIT, (off, nel)
        return ga[r0:r1, off // 4: off // 4 + nel]

    def bsl(off, nel, r0=0, r1=128):
        assert off % 4 == 0 and off + 2 * nel <= LIMIT, (off, nel)
        return gab[r0:r1, off // 2: off // 2 + nel]

    def isl(off, nel, r0=0, r1=128):
        assert off % 4 == 0 and off + 2 * nel <= LIMIT, (off, nel)
        return gai[r0:r1, off // 2: off // 2 + nel]

    O_XNT_OWN = 0
    O_CONST = 32768
    O_P = 33280
    P_SZ = LIMIT - O_P
    assert O_P + P_SZ <= LIMIT

    XNT_OWN = bsl(O_XNT_OWN, NFC * 1024)
    c_expb = fsl(O_CONST, 1)
    c_exps = fsl(O_CONST + 16, 1)
    c_eps = fsl(O_CONST + 32, 1)
    c_one64 = fsl(O_CONST + 48, 64, 0, 1)
    c_smask = fsl(O_CONST + 304, 1).bitcast(I32)
    c_seven = fsl(O_CONST + 320, 1).bitcast(I32)
    c_half = fsl(O_CONST + 336, 1)
    c_neg2 = fsl(O_CONST + 352, 1)
    c_neg6 = fsl(O_CONST + 368, 1)
    c_one1 = fsl(O_CONST + 384, 1)

    _o = [O_P]

    def take(n):
        r = _o[0]
        _o[0] += n
        assert _o[0] <= O_P + P_SZ, (_o[0], O_P + P_SZ)
        return r

    O_XNT_PEER = take(32768)
    O_COSK = take(8192)
    O_SINK = take(8192)
    O_COSQ = take(4096)
    O_SINQ = take(4096)
    O_MASK = take(8192)
    O_KT = take(16384)
    O_VO = take(16640)
    O_QT = take(16384)
    O_AOT = take(16384)
    O_WBUF = take(8192)
    O_XLN = take(8192)
    O_XNB = take(4096)
    O_RTMP = take(6144)
    O_PB = take(2048)
    O_REC = take(4096)
    O_STAT = take(128)
    O_SEV = take(1024)
    O_IDSW = take(1024)
    O_WBUF2 = take(8192)
    P123_END = _o[0]
    assert P123_END <= O_P + P_SZ

    XNT_PEER = bsl(O_XNT_PEER, NFC * 1024)
    COSK = fsl(O_COSK, 2048)
    SINK = fsl(O_SINK, 2048)
    COSQ = fsl(O_COSQ, 1024)
    SINQ = fsl(O_SINQ, 1024)
    MASKS_SB = bsl(O_MASK, 8 * 512)
    KT = bsl(O_KT, 4 * 2048)
    VO = bsl(O_VO, 16 * 8 * 65)
    QT = bsl(O_QT, NFC * 512)
    AOT = bsl(O_AOT, NFC * 512)
    WBUFS = [bsl(O_WBUF, 8 * 512), bsl(O_WBUF2, 8 * 512)]
    O_WBUFS = [O_WBUF, O_WBUF2]
    WBUF = WBUFS[0]
    WVBUF = bsl(O_QT, NFC * 512)
    XLNS = [fsl(O_XLN, 2048), fsl(O_WBUF2, 2048)]
    O_XLNS = [O_XLN, O_WBUF2]
    XLN = XLNS[0]
    XNBS = [bsl(O_XNB, 2048), bsl(O_RTMP, 2048)]
    O_XNBS = [O_XNB, O_RTMP]
    XNB = XNBS[0]
    KRAW = fsl(O_RTMP, 512)
    RT0 = fsl(O_RTMP + 2048, 512)
    RT1 = fsl(O_RTMP + 4096, 512)
    PB = [bsl(O_PB + i * 1024, 512) for i in range(2)] + \
         [bsl(O_XLN + i * 1024, 512) for i in range(2)]
    O_PBS = [O_PB, O_PB + 1024, O_XLN, O_XLN + 1024]
    RECROWS = [fsl(O_REC, 512, 0, 1), fsl(O_RTMP, 512, 0, 1)]
    RECBCS = [fsl(O_REC + 2048, 512), fsl(O_RTMP + 2048, 512)]
    O_RECS = [O_REC, O_RTMP]
    ST_SUM = fsl(O_STAT, 1)
    ST_SQ = fsl(O_STAT + 16, 1)
    ST_MEAN = fsl(O_STAT + 32, 1)
    ST_VAR = fsl(O_STAT + 48, 1)
    ST_SD = fsl(O_STAT + 64, 1)
    ST_RSTD = fsl(O_STAT + 80, 1)
    SEV = bsl(O_SEV, 512)
    IDENT_SB = bsl(O_IDSW, 128)
    SWAP_SB = bsl(O_IDSW + 256, 128)

    _o[0] = O_P
    O_WG = take(16384)
    O_WU = take(16384)
    O_WD = take(32768)
    O_INT = take(16384)
    O_PK = take(8192)
    O_SC = take(8192)
    O_DQ = take(12288)
    O_FACC = take(65536)
    P4_END = _o[0]
    assert P4_END <= O_P + P_SZ

    WG = bsl(O_WG, NFC * 512)
    WU = bsl(O_WU, NFC * 512)
    WD = [bsl(O_WD + g * 16384, UCH * 2048) for g in range(GRP)]
    INTER = [bsl(O_INT + g * 8192, UCH * 1024) for g in range(GRP)]
    PK = [isl(O_PK + i * 4096, 2048) for i in range(2)]
    SC = [bsl(O_SC + i * 4096, 2048) for i in range(2)]
    DQ_M = isl(O_DQ, 2048)
    DQ_T1 = isl(O_DQ + 4096, 2048)
    DQ_T2 = isl(O_DQ + 8192, 2048)

    def facc(tc):
        return fsl(O_FACC + tc * 8192, 2048)

    nc.dbg_offsets = dict(
        XNT_OWN=O_XNT_OWN, XNT_PEER=O_XNT_PEER, KT=O_KT, VO=O_VO, QT=O_QT,
        AOT=O_AOT, FACC=O_FACC, COSK=O_COSK, SINK=O_SINK, MASK=O_MASK)

    SIL = fsl(O_WG, 2048)
    XOUT = fsl(O_WG + 8192, 2048)
    SBF5 = bsl(O_WU, 2048)
    XIN5 = fsl(O_WU + 4096, 2048)

    ps_cm = nc.psum_tensor("PS", [128, 4096], F32)
    psf = ps_cm.__enter__().ap()
    psb = psf.bitcast(BF16)

    def bank(i, n=1):
        return psf[:, 512 * i: 512 * (i + n)]

    def bankb128(i):
        return psb[:, 1024 * i: 1024 * i + 128]

    def R(off, nbytes):
        return ("sb", off, off + nbytes)

    def RP(i, n=1):
        return ("ps", 2048 * i, 2048 * (i + n))

    bld = Builder()
    op = bld.op

    def dma_sp(dst, src, r=(), w=()):
        return op("sync", lambda h, d=dst, s=src: h.dma_start(out=d, in_=s),
                  r=r, w=w, dma=True)

    def dma_gp(dst, src, r=(), w=()):
        return op("gpsimd", lambda h, d=dst, s=src: h.dma_start(out=d, in_=s),
                  r=r, w=w, dma=True)

    op("vector", lambda h: h.memset(c_expb, -EXPC), w=[R(O_CONST, 16)])
    op("vector", lambda h: h.memset(c_exps, 0.125), w=[R(O_CONST + 16, 16)])
    op("vector", lambda h: h.memset(c_eps, 1e-5), w=[R(O_CONST + 32, 16)])
    op("vector", lambda h: h.memset(c_one64, 1.0), w=[R(O_CONST + 48, 256)])
    op("vector", lambda h: h.memset(c_smask, -32768), w=[R(O_CONST + 304, 16)])
    op("vector", lambda h: h.memset(c_seven, 7), w=[R(O_CONST + 320, 16)])
    op("vector", lambda h: h.memset(c_half, 0.5), w=[R(O_CONST + 336, 16)])
    op("vector", lambda h: h.memset(c_neg2, -2.0), w=[R(O_CONST + 352, 16)])
    op("vector", lambda h: h.memset(c_neg6, -6.0), w=[R(O_CONST + 368, 16)])
    op("vector", lambda h: h.memset(c_one1, 1.0), w=[R(O_CONST + 384, 16)])
    op("vector", lambda h: h.memset(
        gab[0:128, O_VO // 2 + 64: O_VO // 2 + 16 * 8 * 65: 65], 1.0),
       w=[R(O_VO, 16 * 8 * 65 * 2)])

    dma_sp(IDENT_SB, ident_in, w=[R(O_IDSW, 256)])
    dma_sp(SWAP_SB, swap_in, w=[R(O_IDSW + 256, 256)])
    dma_gp(COSK, cosk, w=[R(O_COSK, 8192)])
    dma_gp(SINK, sink, w=[R(O_SINK, 8192)])
    dma_gp(COSQ, cosq, w=[R(O_COSQ, 4096)])
    dma_gp(SINQ, sinq, w=[R(O_SINQ, 4096)])

    def _truncate():
        for i, (off, nbytes) in enumerate(dbg or []):
            dt = nc.dram_tensor("dbg%d" % i, [128, nbytes // 4], F32,
                                kind="ExternalOutput").ap()
            dma_sp(dt, fsl(off, nbytes // 4), r=[R(off, nbytes)])
        if dbg_s:
            ds = nc.dram_tensor("dbgs", [OWN, D], BF16,
                                kind="ExternalOutput").ap()
            dma_gp(ds, s_dram, r=[("sdram", 0, 1)])
        dma_gp(out, x_pref[1024:2048, :])
        bld.check()
        _emit(nc, bld)
        return nc

    if limit <= 0:
        return _truncate()

    INV_D = 1.0 / D
    for ti in range(PRE // 128):
        bi = ti % 2
        xln, r_xln = XLNS[bi], R(O_XLNS[bi], 8192)
        xnb, r_xnb = XNBS[bi], R(O_XNBS[bi], 4096)
        dma_sp(xln, x_pref[ti * 128:(ti + 1) * 128, :], w=[r_xln])
        op("vector", lambda h, x_=xln: h.reduce_sum(ST_SUM, x_, axis=AX.X),
           r=[r_xln], w=[R(O_STAT, 16)])
        op("scalar", lambda h, x_=xln, n_=xnb: h.activation(
            n_, x_, AF.Square, accum_out=ST_SQ),
           r=[r_xln], w=[r_xnb, R(O_STAT + 16, 16)])
        op("vector", lambda h: h.tensor_scalar(ST_MEAN, ST_SUM, INV_D, None, ALU.mult),
           r=[R(O_STAT, 16)], w=[R(O_STAT + 32, 16)])
        op("vector", lambda h: h.tensor_scalar(ST_VAR, ST_SQ, INV_D, None, ALU.mult),
           r=[R(O_STAT + 16, 16)], w=[R(O_STAT + 48, 16)])
        op("vector", lambda h: h.tensor_tensor(ST_SD, ST_MEAN, ST_MEAN, ALU.mult),
           r=[R(O_STAT + 32, 16)], w=[R(O_STAT + 64, 16)])
        op("vector", lambda h: h.tensor_sub(ST_VAR, ST_VAR, ST_SD),
           r=[R(O_STAT + 48, 32)], w=[R(O_STAT + 48, 16)])
        op("vector", lambda h: h.tensor_tensor(ST_VAR, ST_VAR, c_eps, ALU.add),
           r=[R(O_STAT + 48, 16), R(O_CONST + 32, 16)], w=[R(O_STAT + 48, 16)])
        op("scalar", lambda h: h.activation(ST_SD, ST_VAR, AF.Sqrt),
           r=[R(O_STAT + 48, 16)], w=[R(O_STAT + 64, 16)])
        op("vector", lambda h: h.reciprocal(ST_RSTD, ST_SD),
           r=[R(O_STAT + 64, 16)], w=[R(O_STAT + 80, 16)])
        op("vector", lambda h, x_=xln, n_=xnb: h.tensor_scalar(
            n_, x_, ST_MEAN, ST_RSTD, ALU.subtract, ALU.mult),
           r=[r_xln, R(O_STAT + 32, 16), R(O_STAT + 80, 16)], w=[r_xnb])
        dst_off = O_XNT_PEER if ti < 8 else O_XNT_OWN
        dst = XNT_PEER if ti < 8 else XNT_OWN
        col = (ti % 8) * 128
        for fc in range(NFC):
            if fc < 8:
                src = xnb.rearrange("p (j two) -> p two j", two=2)[:, 0, 128 * fc: 128 * fc + 128]
            else:
                src = xnb.rearrange("p (j two) -> p two j", two=2)[:, 1, 128 * (fc - 8): 128 * (fc - 8) + 128]
            pdst = bankb128(fc % 2)
            op("tensor", lambda h, s=src, p=pdst: h.transpose(p, s, IDENT_SB),
               r=[r_xnb, R(O_IDSW, 256)], w=[RP(fc % 2)])
            op("vector",
               lambda h, p=pdst, dd=dst, c=col, f=fc: h.tensor_copy(
                   dd[:, 1024 * f + c: 1024 * f + c + 128], p),
               r=[RP(fc % 2)],
               w=[R(dst_off + (1024 * fc + col) * 2, 256)])

    if limit <= 1:
        return _truncate()

    def xnt_rhs(fc, t0, width):
        if t0 + width <= 1024:
            return XNT_PEER[:, 1024 * fc + t0: 1024 * fc + t0 + width]
        assert t0 >= 1024
        return XNT_OWN[:, 1024 * fc + (t0 - 1024): 1024 * fc + (t0 - 1024) + width]

    def xnt_r(fc, t0, width):
        if t0 + width <= 1024:
            return R(O_XNT_PEER + (1024 * fc + t0) * 2, width * 2)
        return R(O_XNT_OWN + (1024 * fc + (t0 - 1024)) * 2, width * 2)

    R_XNT_ALL = [R(O_XNT_PEER, 32768), R(O_XNT_OWN, 32768)]

    R_WBUF = R(O_WBUF, 8192)
    R_XNB = R(O_XNB, 4096)
    R_KRAW = R(O_RTMP, 2048)
    R_RT0 = R(O_RTMP + 2048, 2048)
    R_RT1 = R(O_RTMP + 4096, 2048)

    def rope_store(pbank_i, cos_t, cos_off, sin_t, sin_off, tsz, col0, width,
                   dst, dst_r):
        pk = bank(pbank_i)[:, 0:width]
        if sub != "krope2b":
            op("scalar", lambda h: h.activation(KRAW[:, 0:width], pk, AF.Copy),
               r=[RP(pbank_i)], w=[R_KRAW])
        if sub == "krope1":
            op("vector", lambda h: h.tensor_copy(dst, KRAW[:, 0:width]),
               r=[R_KRAW], w=[dst_r])
            return
        op("vector", lambda h: h.tensor_copy(XNB[:, 0:width], pk),
           r=[RP(pbank_i)], w=[R_XNB])
        swm = IDENT_SB if sub == "krope2c" else SWAP_SB
        obank = 3 if sub == "krope2d" else 7
        swb = bank(obank)[:, 0:width]
        op("tensor", lambda h: h.matmul(swb, swm, XNB[:, 0:width],
                                        start=True, stop=True),
           r=[R_XNB, R(O_IDSW, 512)], w=[RP(obank)])
        if sub in ("krope2", "krope2b", "krope2c", "krope2d"):
            op("vector", lambda h: h.tensor_copy(dst, swb),
               r=[RP(obank)], w=[dst_r])
            return
        op("vector", lambda h: h.tensor_tensor(
            RT0[:, 0:width], KRAW[:, 0:width],
            cos_t[:, col0:col0 + width], ALU.mult),
           r=[R_KRAW, R(cos_off, tsz)], w=[R_RT0])
        op("vector", lambda h: h.tensor_tensor(
            RT1[:, 0:width], swb, sin_t[:, col0:col0 + width], ALU.mult),
           r=[RP(7), R(sin_off, tsz)], w=[R_RT1])
        op("vector", lambda h: h.tensor_add(dst, RT0[:, 0:width], RT1[:, 0:width]),
           r=[R_RT0, R_RT1], w=[dst_r])

    ndc, nt4 = (1, 1) if (sub or "").endswith("a") else (4, 4)
    if (sub or "").endswith("a"):
        sub = sub[:-1]
    if sub != "vonly":
        for dc in range(ndc):
            wb = dc % 2
            dma_gp(WBUFS[wb][:, 0:16 * 128],
                   wk[:, dc * 128:(dc + 1) * 128].rearrange("(c p) d -> p c d", p=128),
                   w=[R(O_WBUFS[wb], 8192)])
            for t4 in range(nt4):
                pb = 2 + (dc * 4 + t4) % 2
                for fc in range(NFC):
                    op("tensor", lambda h, f=fc, tt=t4 * 512, b=pb, w_=wb: h.matmul(
                        bank(b)[:, 0:512], WBUFS[w_][:, f * 128:(f + 1) * 128],
                        xnt_rhs(f, tt, 512),
                        start=(f == 0), stop=(f == NFC - 1)),
                       r=[R(O_WBUFS[wb], 8192), xnt_r(fc, t4 * 512, 512)],
                       w=[RP(pb)])
                if sub == "kproj":
                    op("vector", lambda h, dc_=dc, t_=t4, b=pb: h.tensor_copy(
                        KT[:, dc_ * 2048 + t_ * 512: dc_ * 2048 + t_ * 512 + 512],
                        bank(b)[:, 0:512]),
                       r=[RP(pb)],
                       w=[R(O_KT + (dc * 2048 + t4 * 512) * 2, 1024)])
                else:
                    rope_store(pb, COSK, O_COSK, SINK, O_SINK, 8192, t4 * 512, 512,
                               KT[:, dc * 2048 + t4 * 512: dc * 2048 + t4 * 512 + 512],
                               R(O_KT + (dc * 2048 + t4 * 512) * 2, 1024))

    R_WVBUF = R(O_QT, 16384)
    if sub not in ("kproj", "krope"):
        dma_gp(WVBUF, wv.rearrange("(c p) d -> p c d", p=128), w=[R_WVBUF])
        for tcv in range(16):
            vb = 2 + tcv % 2
            for fc in range(NFC):
                op("tensor", lambda h, f=fc, t=tcv, b=vb: h.matmul(
                    bank(b)[:, 0:512], xnt_rhs(f, t * 128, 128),
                    WVBUF[:, f * 512:(f + 1) * 512],
                    start=(f == 0), stop=(f == NFC - 1)),
                   r=[R_WVBUF, xnt_r(fc, tcv * 128, 128)], w=[RP(vb)])
            for g in range(8):
                op("scalar", lambda h, t=tcv, gg=g, b=vb: h.activation(
                    VO[:, (t * 8 + gg) * 65: (t * 8 + gg) * 65 + 64],
                    bank(b)[:, gg * 64: gg * 64 + 64], AF.Copy),
                   r=[RP(vb)], w=[R(O_VO + (tcv * 8 + g) * 65 * 2, 128)])

    if limit <= 2:
        return _truncate()

    R_QT = R(O_QT, 16384)
    R_AOT = R(O_AOT, 16384)
    R_MASK = R(O_MASK, 8192)
    R_REC = R(O_REC, 2048)
    R_RECBC = R(O_REC + 2048, 2048)
    R_SEV = R(O_SEV, 1024)

    for slot in range(2):
        q0 = slot * 512
        nkt = len(KT_LISTS[slot])
        mask_of = {kt: mi for mi, kt in enumerate(MASKED[slot])}
        dma_sp(MASKS_SB,
               masks_in.rearrange("(m p) n -> p m n", p=128)
               [:, slot * 8:(slot + 1) * 8, :],
               w=[R_MASK])
        for s16 in range(NFC):
            wb = s16 % 2
            pb = 2 + s16 % 2
            dma_gp(WBUFS[wb][:, 0:16 * 128],
                   wq[:, s16 * 128:(s16 + 1) * 128].rearrange("(c p) d -> p c d", p=128),
                   w=[R(O_WBUFS[wb], 8192)])
            for fc in range(NFC):
                op("tensor", lambda h, f=fc, qq=q0, b=pb, w_=wb: h.matmul(
                    bank(b)[:, 0:512], WBUFS[w_][:, f * 128:(f + 1) * 128],
                    XNT_OWN[:, 1024 * f + qq: 1024 * f + qq + 512],
                    start=(f == 0), stop=(f == NFC - 1)),
                   r=[R(O_WBUFS[wb], 8192), xnt_r(fc, 1024 + q0, 512)],
                   w=[RP(pb)])
            rope_store(pb, COSQ, O_COSQ, SINQ, O_SINQ, 4096, q0, 512,
                       QT[:, s16 * 512:(s16 + 1) * 512],
                       R(O_QT + s16 * 512 * 2, 1024))

        norm_q = []
        for s16 in range(NFC):
            for odd in range(2):
                H = (s16 // 4) * 8 + s16 % 4 + 4 * odd
                g = H // 4
                rb = 64 * (g % 2)
                assert rb == 64 * odd
                avb = 4 + odd
                qsl = QT[rb: rb + 64, s16 * 512:(s16 + 1) * 512]
                r_qsl = R(O_QT + s16 * 1024, 1024)
                norm_q.append((odd, rb, avb, s16))
                for kt_i, kt in enumerate(KT_LISTS[slot]):
                    sb = 2 + (kt_i % 2)
                    kof = (g // 2) * 2048 + kt * 128
                    ksl = KT[rb: rb + 64, kof: kof + 128]
                    op("tensor", lambda h, p=sb, k=ksl, q=qsl: h.matmul(
                        bank(p)[:, 0:512], k, q, start=True, stop=True),
                       r=[R(O_KT + kof * 2, 256), r_qsl], w=[RP(sb)])
                    pbi = (2 * kt_i + odd) % 4
                    pbuf = PB[pbi]
                    r_pb = R(O_PBS[pbi], 1024)
                    op("scalar", lambda h, d=pbuf, p=sb: h.activation(
                        d, bank(p)[:, 0:512], AF.Exp, bias=c_expb, scale=c_exps),
                       r=[RP(sb), R(O_CONST, 32)], w=[r_pb])
                    if kt in mask_of:
                        mi = mask_of[kt]
                        op("gpsimd", lambda h, d=pbuf, m=mi: h.tensor_tensor(
                            d, d, MASKS_SB[:, m * 512:(m + 1) * 512], ALU.mult),
                           r=[r_pb, R_MASK], w=[r_pb])
                    st = (kt_i == 0)
                    sp = (kt_i == nkt - 1)
                    vbase = (kt * 8 + g) * 65
                    if odd == 0:
                        op("tensor", lambda h, st_=st, sp_=sp, vb=vbase, d=pbuf,
                           b=avb: h.matmul(bank(b)[0:65, 0:512],
                                           VO[:, vb: vb + 65], d,
                                           start=st_, stop=sp_),
                           r=[R(O_VO + vbase * 2, 130), r_pb], w=[RP(avb)])
                    else:
                        op("tensor", lambda h, st_=st, sp_=sp, vb=vbase, d=pbuf,
                           b=avb: h.matmul(bank(b)[64:128, 0:512],
                                           VO[:, vb: vb + 64], d,
                                           start=st_, stop=sp_),
                           r=[R(O_VO + vbase * 2, 130), r_pb], w=[RP(avb)])
                        op("tensor", lambda h, st_=st, sp_=sp, vb=vbase, d=pbuf,
                           b=avb: h.matmul(bank(b)[0:1, 0:512],
                                           VO[:, vb + 64: vb + 65], d,
                                           start=st_, stop=sp_),
                           r=[R(O_VO + vbase * 2, 130), r_pb], w=[RP(avb)])
            for odd, rb, avb, s16n in norm_q:
                bcb = 6 if odd == 0 else 1
                recrow, recbc = RECROWS[odd], RECBCS[odd]
                r_rec = R(O_RECS[odd], 2048)
                r_recbc = R(O_RECS[odd] + 2048, 2048)
                sum_row = bank(avb)[64:65, 0:512] if odd == 0 else bank(avb)[0:1, 0:512]
                av_rows = bank(avb)[0:64, 0:512] if odd == 0 else bank(avb)[64:128, 0:512]
                op("vector", lambda h, sr=sum_row, rr=recrow: h.reciprocal(rr, sr),
                   r=[RP(avb)], w=[r_rec])
                op("tensor", lambda h, r0=rb, b=bcb, rr=recrow: h.matmul(
                    bank(b)[r0:r0 + 64, 0:512], c_one64, rr,
                    start=True, stop=True),
                   r=[r_rec, R(O_CONST + 48, 256)], w=[RP(bcb)])
                op("scalar", lambda h, r0=rb, b=bcb, rc=recbc: h.activation(
                    rc[r0:r0 + 64, :], bank(b)[r0:r0 + 64, 0:512], AF.Copy),
                   r=[RP(bcb)], w=[r_recbc])
                op("vector", lambda h, av=av_rows, r0=rb, s=s16n, b=avb, rc=recbc:
                   h.tensor_tensor(
                       AOT[r0:r0 + 64, s * 512:(s + 1) * 512], av,
                       rc[r0:r0 + 64, :], ALU.mult),
                   r=[RP(avb), r_recbc], w=[R(O_AOT + s16n * 1024, 1024)])
            norm_q = []

        for fp in range(4):
            for half in range(2):
                wb = (fp * 2 + half) % 2
                dma_gp(WBUFS[wb][:, 0:8 * 512],
                       wo[1024 * half: 1024 * half + 1024, fp * 512:(fp + 1) * 512]
                       .rearrange("(c p) d -> p c d", p=128),
                       w=[R(O_WBUFS[wb], 8192)])
                for tch in range(4):
                    for dc8 in range(8):
                        aof = (half * 8 + dc8) * 512 + tch * 128
                        op("tensor", lambda h, d=dc8, hf=half, t=tch, w_=wb,
                           ao=aof: h.matmul(
                               bank(t)[:, 0:512],
                               AOT[:, ao: ao + 128],
                               WBUFS[w_][:, d * 512:(d + 1) * 512],
                               start=(hf == 0 and d == 0), stop=(hf == 1 and d == 7)),
                           r=[R(O_AOT + aof * 2, 256), R(O_WBUFS[wb], 8192)],
                           w=[RP(tch)])
            for tch in range(4):
                op("scalar", lambda h, t=tch: h.activation(
                    SEV, bank(t)[:, 0:512], AF.Copy),
                   r=[RP(tch)], w=[R_SEV])
                dma_gp(s_dram[q0 + tch * 128: q0 + tch * 128 + 128,
                              fp * 512:(fp + 1) * 512],
                       SEV,
                       r=[R_SEV], w=[("sdram", 0, 1)])

    if limit <= 3:
        return _truncate()

    R_PK = [R(O_PK + i * 4096, 4096) for i in range(2)]
    R_SC = [R(O_SC + i * 4096, 4096) for i in range(2)]
    R_DQM = R(O_DQ, 4096)
    R_DQ1 = R(O_DQ + 4096, 4096)
    R_DQ2 = R(O_DQ + 8192, 4096)
    R_WG = R(O_WG, 16384)
    R_WU = R(O_WU, 16384)

    def load_packed_gu(src_pk, src_sc, kb, slot, hi):
        c0 = 8192 * hi + 256 * kb
        dma_sp(PK[slot].rearrange("p (r j) -> p r j", j=256),
               src_pk.rearrange("(r p) u -> p r u", p=128)[:, :, c0:c0 + 256],
               w=[R_PK[slot]])
        dma_gp(SC[slot],
               src_sc.rearrange("e (c n) -> e c n", n=2048)[:, 32 * hi + kb]
               .unsqueeze(1).broadcast_to([8, 16, 2048]),
               w=[R_SC[slot]])

    def load_packed_d(kb, rt, slot):
        dma_sp(PK[slot],
               dpk.rearrange("(r p) f -> p r f", p=128)[:, 2 * kb + rt, :],
               w=[R_PK[slot]])
        dma_gp(SC[slot],
               dsc.rearrange("e (c n) -> e c n", n=2048)[:, 2 * kb + rt]
               .unsqueeze(1).broadcast_to([8, 16, 2048]),
               w=[R_SC[slot]])

    DQ_M = isl(O_DQ, 2048)
    DQ_T1 = isl(O_DQ + 4096, 2048)
    DQ_T2 = isl(O_DQ + 8192, 2048)
    R_DQM = R(O_DQ, 4096)
    R_DQ1 = R(O_DQ + 4096, 4096)
    R_DQ2 = R(O_DQ + 8192, 4096)

    fillers = []

    def fill(n=1):
        for _ in range(min(n, len(fillers))):
            fillers.pop(0)()

    def flush_fill():
        while fillers:
            fillers.pop(0)()

    def dequant(dst_lo, dst_hi, r_dst, slot, grouped=False):
        if grouped:
            t1b = DQ_T1.bitcast(BF16).rearrange("p (r j) -> p r j", j=256)
            scv = SC[slot].rearrange("p (r j) -> p r j", j=256)
        else:
            t1b = DQ_T1.bitcast(BF16)
            scv = SC[slot]
        for hi, dsti, w_dst in ((0, dst_lo, r_dst[0]), (1, dst_hi, r_dst[1])):
            if hi == 0:
                op("vector", lambda h: h.tensor_single_scalar(
                    DQ_M, PK[slot], 7, ALU.bitwise_and),
                   r=[R(O_PK + slot * 4096, 4096)], w=[R_DQM])
            else:
                op("vector", lambda h: h.tensor_scalar(
                    DQ_M, PK[slot], 4, 7,
                    ALU.logical_shift_right, ALU.bitwise_and),
                   r=[R(O_PK + slot * 4096, 4096)], w=[R_DQM])
            op("vector", lambda h: h.tensor_scalar(
                DQ_T1, DQ_M, 1, float(0x3F00), ALU.min, ALU.mult),
               r=[R_DQM], w=[R_DQ1])
            op("vector", lambda h: h.tensor_scalar(
                DQ_T2, DQ_M, 1, float(0x80), ALU.max, ALU.mult),
               r=[R_DQM], w=[R_DQ2])
            op("vector", lambda h: h.tensor_add(DQ_T1, DQ_T1, DQ_T2),
               r=[R_DQ1, R_DQ2], w=[R_DQ1])
            fill(1)
            op("vector", lambda h: h.tensor_scalar(
                DQ_T2, DQ_M, 2, float(-0x40), ALU.max, ALU.mult),
               r=[R_DQM], w=[R_DQ2])
            op("vector", lambda h: h.tensor_add(DQ_T1, DQ_T1, DQ_T2),
               r=[R_DQ1, R_DQ2], w=[R_DQ1])
            op("vector", lambda h, sh=12 - 4 * hi: h.tensor_scalar(
                DQ_T2, PK[slot], sh, c_smask,
                ALU.logical_shift_left, ALU.bitwise_and),
               r=[R(O_PK + slot * 4096, 4096), R(O_CONST + 304, 16)],
               w=[R_DQ2])
            op("vector", lambda h: h.tensor_tensor(
                DQ_T1, DQ_T1, DQ_T2, ALU.bitwise_or),
               r=[R_DQ1, R_DQ2], w=[R_DQ1])
            op("vector", lambda h, dd=dsti: h.tensor_tensor(dd, t1b, scv, ALU.mult),
               r=[R_DQ1, R(O_SC + slot * 4096, 4096)], w=w_dst)
            fill(1)

    def w4(W):
        return W.rearrange("p (fc half j) -> p fc half j", fc=16, half=2)

    seq = 0
    ffn_sub = sub if (sub or "").startswith("ffn") else None
    nbk_run = {None: NBK, "ffn1": 1, "ffn2": 1, "ffn3": 2}[ffn_sub]
    do_mm = ffn_sub != "ffn1"
    for kb in range(nbk_run):
        gi = kb % GRP
        for hi in range(2):
            load_packed_gu(gpk, gsc, kb, seq % 2, hi)
            wv_ = w4(WG)
            dequant(wv_[:, 0:8, hi, :], wv_[:, 8:16, hi, :],
                    ([R(O_WG + (fc * 512 + hi * 256) * 2, 512) for fc in range(8)],
                     [R(O_WG + (fc * 512 + hi * 256) * 2, 512) for fc in range(8, 16)]),
                    seq % 2, grouped=True)
            seq += 1
        flush_fill()
        for u in (range(UCH) if do_mm else []):
            for t2 in range(2):
                bk_i = (2 * u + t2) % 2
                for fc in range(NFC):
                    op("tensor", lambda h, p=bk_i, f=fc, uu=u, tt=t2: h.matmul(
                        bank(p)[:, 0:512],
                        WG[:, f * 512 + uu * 128: f * 512 + uu * 128 + 128],
                        XNT_OWN[:, 1024 * f + tt * 512: 1024 * f + tt * 512 + 512],
                        start=(f == 0), stop=(f == NFC - 1)),
                       r=[R_WG, R(O_XNT_OWN, 32768)], w=[RP(bk_i)])
                op("scalar", lambda h, p=bk_i, uu=u, tt=t2, g_=gi: h.activation(
                    INTER[g_][:, uu * 1024 + tt * 512: uu * 1024 + tt * 512 + 512],
                    bank(p)[:, 0:512], AF.Gelu_apprx_tanh),
                   r=[RP(bk_i)],
                   w=[R(O_INT + gi * 8192 + (u * 1024 + t2 * 512) * 2, 1024)])
        for hi in range(2):
            load_packed_gu(upk, usc, kb, seq % 2, hi)
            wv_ = w4(WU)
            dequant(wv_[:, 0:8, hi, :], wv_[:, 8:16, hi, :], ([R(O_WU + (fc * 512 + hi * 256) * 2, 512) for fc in range(8)],
                     [R(O_WU + (fc * 512 + hi * 256) * 2, 512) for fc in range(8, 16)]),
                    seq % 2, grouped=True)
            seq += 1
        for rt in range(2):
            load_packed_d(kb, rt, seq % 2)
            dequant(WD[gi][:, rt * 2048: rt * 2048 + 2048],
                    WD[gi][:, (2 + rt) * 2048: (2 + rt) * 2048 + 2048],
                    ([R(O_WD + gi * 16384 + rt * 4096, 4096)],
                     [R(O_WD + gi * 16384 + (2 + rt) * 4096, 4096)]), seq % 2)
            seq += 1
        flush_fill()
        for u in (range(UCH) if do_mm else []):
            for t2 in range(2):
                bk_i = 2 + (2 * u + t2) % 2
                for fc in range(NFC):
                    op("tensor", lambda h, p=bk_i, f=fc, uu=u, tt=t2: h.matmul(
                        bank(p)[:, 0:512],
                        WU[:, f * 512 + uu * 128: f * 512 + uu * 128 + 128],
                        XNT_OWN[:, 1024 * f + tt * 512: 1024 * f + tt * 512 + 512],
                        start=(f == 0), stop=(f == NFC - 1)),
                       r=[R_WU, R(O_XNT_OWN, 32768)], w=[RP(bk_i)])
                op("vector", lambda h, p=bk_i, uu=u, tt=t2, g_=gi: h.tensor_tensor(
                    INTER[g_][:, uu * 1024 + tt * 512: uu * 1024 + tt * 512 + 512],
                    INTER[g_][:, uu * 1024 + tt * 512: uu * 1024 + tt * 512 + 512],
                    bank(p)[:, 0:512], ALU.mult),
                   r=[RP(bk_i),
                      R(O_INT + gi * 8192 + (u * 1024 + t2 * 512) * 2, 1024)],
                   w=[R(O_INT + gi * 8192 + (u * 1024 + t2 * 512) * 2, 1024)])
        if gi == GRP - 1 and do_mm and ffn_sub != "ffn2":
            flush_fill()
            for tc in range(8):
                bset = 4 if tc % 2 == 0 else 0
                for g2 in range(GRP):
                    for u in range(UCH):
                        for fpp in range(4):
                            st = (g2 == 0 and u == 0)
                            spv = (g2 == GRP - 1 and u == UCH - 1)
                            op("tensor", lambda h, b=bset, g_=g2, uu=u, f=fpp,
                               t=tc, st_=st, sp_=spv: h.matmul(
                                   bank(b + f)[:, 0:512],
                                   INTER[g_][:, uu * 1024 + t * 128:
                                             uu * 1024 + t * 128 + 128],
                                   WD[g_][:, uu * 2048 + f * 512:
                                          uu * 2048 + f * 512 + 512],
                                   start=st_, stop=sp_),
                               r=[R(O_INT + g2 * 8192, 8192),
                                  R(O_WD + g2 * 16384, 16384)],
                               w=[RP(bset + fpp)])
                if kb == GRP - 1:
                    op("vector", lambda h, t=tc, b=bset: h.tensor_copy(
                        facc(t), bank(b, 4)),
                       r=[RP(bset, 4)], w=[R(O_FACC + tc * 8192, 8192)])
                else:
                    op("vector", lambda h, t=tc, b=bset: h.tensor_tensor(
                        facc(t), facc(t), bank(b, 4), ALU.add),
                       r=[RP(bset, 4), R(O_FACC + tc * 8192, 8192)],
                       w=[R(O_FACC + tc * 8192, 8192)])

    flush_fill()
    if limit <= 4:
        return _truncate()

    R_SIL = R(O_WG, 8192)
    R_XOUT = R(O_WG + 8192, 8192)
    R_SBF5 = R(O_WU, 4096)
    R_XIN5 = R(O_WU + 4096, 8192)
    for tc in range(8):
        dma_sp(SBF5, s_dram[tc * 128:(tc + 1) * 128, :],
               r=[("sdram", 0, 1)], w=[R_SBF5])
        dma_sp(XIN5, x_pref[1024 + tc * 128: 1024 + (tc + 1) * 128, :],
               w=[R_XIN5])
        op("scalar", lambda h, t=tc: h.activation(SIL, facc(t), AF.Silu),
           r=[R(O_FACC + tc * 8192, 8192)], w=[R_SIL])
        op("vector", lambda h: h.tensor_tensor(SIL, SIL, SBF5, ALU.mult),
           r=[R_SIL, R_SBF5], w=[R_SIL])
        op("vector", lambda h: h.tensor_tensor(XOUT, SIL, XIN5, ALU.add),
           r=[R_SIL, R_XIN5], w=[R_XOUT])
        dma_gp(out[tc * 128:(tc + 1) * 128, :], XOUT,
               r=[R_XOUT], w=[("outd", tc, tc + 1)])

    bld.check()
    _emit(nc, bld)
    return nc


def _emit(nc, bld):
    sem_cms = {}
    sems = {}
    for n in ENGINES:
        sem_cms[n] = nc.semaphore(n)
        sems[n] = sem_cms[n].__enter__()
    bld.emit(nc, sems, tail=("gpsimd", "sync"))
    for cm in sem_cms.values():
        cm.__exit__(None, None, None)


_PERMF = np.concatenate([np.arange(0, D, 2), np.arange(1, D, 2)])
_PERMU = np.concatenate([np.arange(0, MLP, 2), np.arange(1, MLP, 2)])


def _own_rows(core):
    b, p = core // 2, core % 2
    if p == 0:
        own = np.concatenate([np.arange(0, 512), np.arange(1536, 2048)])
    else:
        own = np.arange(512, 1536)
    return b, own


def _head_pair(s):
    he = (s // 4) * 8 + s % 4
    return he, he + 4


def _rope_tables(pos):
    inv = 1.0 / ROPE_BASE ** (np.arange(0, 64, 2, dtype=np.float64) / 64.0)
    d = np.arange(128)
    ang = pos[None, :].astype(np.float64) * inv[d % 32][:, None]
    cos = np.cos(ang).astype(np.float32)
    sin = np.sin(ang).astype(np.float32)
    sign = np.where((d % 64) < 32, -1.0, 1.0).astype(np.float32)
    return np.ascontiguousarray(cos), np.ascontiguousarray(sin * sign[:, None])


def _build_masks(core):
    p = core % 2
    mk = np.zeros((16, 128, 512), dtype=BF16NP)
    diag = (np.arange(512)[:, None] <= np.arange(512)[None, :])
    for slot in range(2):
        for mi, kt in enumerate(MASKED[slot]):
            m = slot * 8 + mi
            if mi < 4:
                if slot == 0:
                    mk[m] = 1.0 if p == 1 else 0.0
                else:
                    mk[m] = 1.0 if p == 0 else 0.0
            else:
                j = mi - 4
                mk[m] = diag[j * 128:(j + 1) * 128, :].astype(BF16NP)
    return np.ascontiguousarray(mk.reshape(16 * 128, 512))


def prepare_in_maps(L0_attn_x, L0_attn_wq, L0_attn_wk, L0_attn_wv, L0_attn_wo,
                    L0_ffn_gate_W_packed, L0_ffn_gate_scales,
                    L0_ffn_up_W_packed, L0_ffn_up_scales,
                    L0_ffn_down_W_packed, L0_ffn_down_scales):
    X = np.ascontiguousarray(np.asarray(L0_attn_x, dtype=np.float32))
    qcolmap = np.concatenate([
        np.concatenate([np.arange(he * 64, he * 64 + 64),
                        np.arange(ho * 64, ho * 64 + 64)])
        for he, ho in (_head_pair(s) for s in range(16))])
    wq = np.ascontiguousarray(
        np.asarray(L0_attn_wq, dtype=np.float32)[_PERMF][:, qcolmap])
    wk = np.ascontiguousarray(np.asarray(L0_attn_wk, dtype=np.float32)[_PERMF])
    wv = np.ascontiguousarray(np.asarray(L0_attn_wv, dtype=np.float32)[_PERMF])
    womap = np.concatenate([
        np.concatenate([np.arange(he * 64, he * 64 + 64),
                        np.arange(ho * 64, ho * 64 + 64)])
        for he, ho in (_head_pair(s) for s in range(16))])
    wo = np.ascontiguousarray(np.asarray(L0_attn_wo, dtype=np.float32)[womap])

    gp = np.asarray(L0_ffn_gate_W_packed, dtype=np.int32)[_PERMU]
    up = np.asarray(L0_ffn_up_W_packed, dtype=np.int32)[_PERMU]
    dn = np.asarray(L0_ffn_down_W_packed, dtype=np.int32)
    gpk = np.ascontiguousarray(gp.view(np.int16)[:, 0::2].T)
    upk = np.ascontiguousarray(up.view(np.int16)[:, 0::2].T)
    dpk = np.ascontiguousarray(dn.view(np.int16)[:, 0::2].T)

    def _retile_gu(sc):
        t = np.asarray(sc, np.float32)[_PERMU].T.reshape(8, 8, 64, 256)
        return np.ascontiguousarray(t.transpose(1, 2, 0, 3).reshape(8, 64 * 2048))

    def _retile_d(sc):
        t = np.asarray(sc, np.float32).T.reshape(64, 8, 2048)
        return np.ascontiguousarray(t.transpose(1, 0, 2).reshape(8, 64 * 2048))

    gsc = _retile_gu(L0_ffn_gate_scales)
    usc = _retile_gu(L0_ffn_up_scales)
    dsc = _retile_d(L0_ffn_down_scales)

    ident = np.ascontiguousarray(np.eye(128, dtype=BF16NP))
    sw = np.zeros((128, 128), dtype=BF16NP)
    for i in range(128):
        sw[(i // 64) * 64 + (i % 64 + 32) % 64, i] = 1.0
    sw = np.ascontiguousarray(sw)

    in_maps = []
    for core in range(NCORE):
        b, own = _own_rows(core)
        other = np.setdiff1d(np.arange(L), own)
        order = np.concatenate([other, own])
        cosk, sink = _rope_tables(order)
        cosq, sinq = _rope_tables(own)
        in_maps.append({
            "x_pref": np.ascontiguousarray(X[b][order]),
            "wq": wq, "wk": wk, "wv": wv, "wo": wo,
            "gpk": gpk, "upk": upk, "dpk": dpk,
            "gsc": gsc, "usc": usc, "dsc": dsc,
            "cosk": cosk, "sink": sink, "cosq": cosq, "sinq": sinq,
            "masks": _build_masks(core),
            "ident": ident, "swapm": sw,
        })
    return in_maps


def kernel(**inputs):
    in_maps = prepare_in_maps(**inputs)
    nc = build_program()
    res = run_bass_kernel_spmd(nc, in_maps, core_ids=list(range(NCORE)))

    OUT = np.empty((B, L, D), dtype=np.float32)
    for core in range(NCORE):
        b, own = _own_rows(core)
        OUT[b][own] = res.results[core]["out"]
    return OUT
